# version 19
# baseline (speedup 1.0000x reference)
import math
from contextlib import ExitStack

import numpy as np
import concourse.bass as bass
import concourse.mybir as mybir
from concourse.bass_utils import run_bass_kernel_spmd

F32 = mybir.dt.float32
BF16 = mybir.dt.bfloat16
AF = mybir.ActivationFunctionType
ALU = mybir.AluOpType
AX = mybir.AxisListType
U8 = mybir.dt.uint8
U32 = mybir.dt.uint32
I32 = mybir.dt.int32


class Buf:
    __slots__ = ("ap", "writes", "reads", "name", "dsem", "dqueue")

    def __init__(self, ap, name=""):
        self.ap = ap
        self.writes = {}
        self.reads = {}
        self.name = name
        self.dsem = None
        self.dqueue = None


class K:
    def __init__(self, nc):
        self.nc = nc
        self.eng = {
            "pe": nc.tensor, "dve": nc.vector, "act": nc.scalar,
            "pool": nc.gpsimd, "sp": nc.sync,
        }
        self.es = ExitStack()
        self.sem = {}
        self.cnt = {}
        for t in ("pe", "dve", "act", "pool"):
            self.sem[t] = self.es.enter_context(nc.semaphore(t))
            self.cnt[t] = 0
        self.memo = {}
        self.drained = {}
        t = self.es.enter_context(nc.sbuf_tensor("k_cpool", [128, 16], F32))
        self.cpool = Buf(t.ap(), "k_cpool")
        self.cvals = {}

    def _dsem_for(self, buf, queue):
        if buf.dsem is None:
            tag = "d_" + buf.name
            buf.dsem = tag
            buf.dqueue = queue
            self.sem[tag] = self.es.enter_context(self.nc.semaphore(tag))
            self.cnt[tag] = 0
        else:
            assert buf.dqueue == queue, \
                f"buffer {buf.name} DMA'd from both HWDGE and SWDGE queues"
        return buf.dsem

    def constcol(self, val):
        if val not in self.cvals:
            i = len(self.cvals)
            self.cvals[val] = i
            self.op("pool", "memset", [self.cpool], [],
                    self.cpool.ap[:, i:i + 1], float(val))
        i = self.cvals[val]
        return self.cpool, self.cpool.ap[:, i:i + 1]

    def sb(self, scope, name, shape, dtype):
        t = scope.enter_context(self.nc.sbuf_tensor(name, shape, dtype))
        return Buf(t.ap(), name)

    def psum(self, scope, name, shape, dtype=F32):
        t = scope.enter_context(self.nc.psum_tensor(name, shape, dtype))
        return Buf(t.ap(), name)

    def dram(self, name, shape, dtype, kind="Internal"):
        t = self.nc.dram_tensor(name, shape, dtype, kind=kind)
        return Buf(t.ap(), name)

    def _wait_deps(self, etag, outs, ins, skip_tag=None):
        deps = {}
        for b in ins:
            for t, v in b.writes.items():
                if deps.get(t, 0) < v:
                    deps[t] = v
        for b in outs:
            for src in (b.writes, b.reads):
                for t, v in src.items():
                    if t == skip_tag:
                        continue
                    if deps.get(t, 0) < v:
                        deps[t] = v
        eng = self.eng[etag]
        need_drain = False
        for t, v in deps.items():
            if t == etag:
                if etag == "pe":
                    continue
                if self.drained.get(etag, 0) < v:
                    need_drain = True
            else:
                if t not in ("pe", "dve", "act", "pool"):
                    v = self.cnt[t]
                if self.memo.get((etag, t), 0) < v:
                    eng.wait_ge(self.sem[t], v)
                    self.memo[(etag, t)] = v
        if need_drain:
            eng.drain()
            self.drained[etag] = self.cnt[etag]

    def _finish(self, etag, inst, outs, ins, tag, step):
        self.cnt[tag] += step
        inst.then_inc(self.sem[tag], step)
        tok_v = self.cnt[tag]
        for b in outs:
            if step == 16 and tag == b.dsem:
                b.writes[tag] = tok_v
            else:
                b.writes = {tag: tok_v}
            b.reads = {}
        for b in ins:
            b.reads[tag] = tok_v
        return (tag, tok_v)

    def barrier(self):
        for etag in ("pe", "dve", "act", "pool", "sp"):
            eng = self.eng[etag]
            for t in self.sem:
                v = self.cnt[t]
                if v and t != etag and self.memo.get((etag, t), 0) < v:
                    eng.wait_ge(self.sem[t], v)
                    self.memo[(etag, t)] = v
            if etag in ("dve", "act", "pool"):
                eng.drain()
                self.drained[etag] = self.cnt[etag]

    def finish_program(self):
        for t in list(self.sem):
            v = self.cnt[t]
            if v and t != "sp" and self.memo.get(("sp", t), 0) < v:
                self.eng["sp"].wait_ge(self.sem[t], v)
                self.memo[("sp", t)] = v

    def op(self, etag, name, outs, ins, *args, **kw):
        self._wait_deps(etag, outs, ins)
        inst = getattr(self.eng[etag], name)(*args, **kw)
        return self._finish(etag, inst, outs, ins, etag, 1)

    def idma(self, outs, ins, out_ap, out_off, in_ap, in_off, skip_waw=True, **kw):
        from concourse.bass import IndirectOffsetOnAxis
        tag = self._dsem_for(outs[0], "sw")
        self._wait_deps("pool", outs, ins, skip_tag=tag if skip_waw else None)
        inst = self.eng["pool"].indirect_dma_start(
            out_ap,
            IndirectOffsetOnAxis(out_off, 0) if out_off is not None else None,
            in_ap,
            IndirectOffsetOnAxis(in_off, 0) if in_off is not None else None,
            **kw)
        return self._finish("pool", inst, outs, ins, tag, 16)

    def dma(self, etag, outs, ins, out_ap, in_ap, **kw):
        queue = "hw" if etag in ("sp", "act") else "sw"
        tag = self._dsem_for(outs[0], queue)
        self._wait_deps(etag, outs, ins, skip_tag=tag)
        inst = self.eng[etag].dma_start(out=out_ap, in_=in_ap, **kw)
        return self._finish(etag, inst, outs, ins, tag, 16)

    def mm(self, psum_buf, items, ins, accum=False, stop=True):
        self._wait_deps("pe", [psum_buf], ins)
        n = len(items)
        inst = None
        for i, (ps_ap, lhsT, rhs) in enumerate(items):
            inst = self.nc.tensor.matmul(
                ps_ap, lhsT, rhs,
                start=(i == 0 and not accum), stop=(stop and i == n - 1))
        return self._finish("pe", inst, [psum_buf], ins, "pe", 1)

    def transpose(self, psum_buf, ps_ap, in_buf, in_ap, ident):
        kdim = in_ap.shape[0]
        self._wait_deps("pe", [psum_buf], [in_buf, ident])
        inst = self.nc.tensor.transpose(ps_ap, in_ap, ident.ap[0:kdim, 0:kdim])
        return self._finish("pe", inst, [psum_buf], [in_buf, ident], "pe", 1)


def emit_bcast_row(k, out_buf, row_buf, ps_buf, ones1, D):
    for c in range(0, D, 512):
        w = min(512, D - c)
        k.mm(ps_buf, [(ps_buf.ap[:, 0:w], ones1.ap[:], row_buf.ap[:, c:c + w])],
             [row_buf, ones1])
        k.op("dve", "tensor_copy", [out_buf], [ps_buf],
             out_buf.ap[:, c:c + w], ps_buf.ap[:, 0:w])


def emit_identity(k, ident, n=128):
    k.op("pool", "memset", [ident], [], ident.ap[:], 0.0)
    k.op("pool", "affine_select", [ident], [ident],
         out=ident.ap[:], in_=ident.ap[:], compare_op=ALU.not_equal,
         fill=1.0, base=0, pattern=[[-1, n]], channel_multiplier=1)


def emit_tril(k, buf, n=128, val=1.0):
    k.op("pool", "memset", [buf], [], buf.ap[:], val)
    k.op("pool", "affine_select", [buf], [buf],
         out=buf.ap[:], in_=buf.ap[:], compare_op=ALU.is_ge,
         fill=0.0, base=0, pattern=[[-1, n]], channel_multiplier=1)


def emit_ln(k, mv, stats, rstd, xin_buf, x_ap, D, eps=1e-5):
    nch = (D + 511) // 512
    for c in range(nch):
        k.op("dve", "bn_stats", [stats], [xin_buf],
             stats.ap[:, 6 * c:6 * c + 6], x_ap[:, 512 * c:512 * (c + 1)])
    k.op("dve", "bn_aggr", [mv], [stats], mv.ap[:], stats.ap[:, 0:6 * nch])
    cb, eps_ap = k.constcol(eps)
    k.op("act", "activation", [rstd], [mv, cb], rstd.ap[:], mv.ap[:, 1:2], AF.Ln, bias=eps_ap)
    k.op("act", "activation", [rstd], [rstd], rstd.ap[:], rstd.ap[:], AF.Exp, scale=-0.5)
    return mv.ap[:, 0:1], rstd.ap[:]


P0_ITEMS = [("wkqv_fm", "wkqv_bf"), ("wproj_fm", "wproj_bf"),
            ("Wq", "wq_bf"), ("Wk", "wk_bf"), ("Wv", "wv_bf"),
            ("Wg", "wg_bf"), ("Wo", "wo_bf")]


def build_phase0(k, g, cfg, items=None, moe=True):
    items = items if items is not None else P0_ITEMS
    sc = ExitStack()
    CH = 8192
    bi = [k.sb(sc, f"p0_in{i}", [128, CH], F32) for i in range(2)]
    bo = [k.sb(sc, f"p0_out{i}", [128, CH], BF16) for i in range(2)]
    n = 0

    def cast_stream(pairs):
        nonlocal n
        for (sap, srcb, db, r0, c0, w) in pairs:
            i = n % 2
            k.dma("sp", [bi[i]], [srcb], bi[i].ap[:, 0:w], sap)
            k.op("dve", "tensor_copy", [bo[i]], [bi[i]], bo[i].ap[:, 0:w],
                 bi[i].ap[:, 0:w])
            k.dma("act", [db], [bo[i]], db.ap[r0:r0 + 128, c0:c0 + w],
                 bo[i].ap[:, 0:w])
            n += 1

    for srcn, dstn in items:
        sb = g[srcn]
        P, X = sb.ap.shape
        db = k.dram(dstn, [P, X], BF16)
        g[dstn] = db
        pairs = []
        for r0 in range(0, P, 128):
            for c0 in range(0, X, CH):
                w = min(CH, X - c0)
                pairs.append((sb.ap[r0:r0 + 128, c0:c0 + w], sb, db, r0, c0, w))
        cast_stream(pairs)
    if moe:
        w1v = g["W1"].ap[:].rearrange("r (c f) -> r c f", f=4096)
        w2v = g["W2"].ap[:].rearrange("r (c d) -> r c d", d=1024)
        for half in range(2):
            d1 = k.dram(f"w1{'ab'[half]}_bf", [1024, 8 * 2048], BF16)
            g[f"w1{'ab'[half]}_bf"] = d1
            pairs = []
            for r0 in range(0, 1024, 128):
                for ic in range(8):
                    pairs.append((w1v[r0:r0 + 128, ic, half * 2048:(half + 1) * 2048],
                                  g["W1"], d1, r0, ic * 2048, 2048))
            cast_stream(pairs)
        for half in range(2):
            d2 = k.dram(f"w2{'ab'[half]}_bf", [1024, 32 * 512], BF16)
            g[f"w2{'ab'[half]}_bf"] = d2
            pairs = []
            for r0 in range(0, 1024, 128):
                for hc0 in range(0, 32, 4):
                    sap = w2v[r0:r0 + 128, hc0:hc0 + 4, half * 512:(half + 1) * 512]
                    pairs.append((sap, g["W2"], d2, r0, hc0 * 512, 2048))
            cast_stream(pairs)
    sc.close()
    k.barrier()


def build_phase1(k, g, cfg):
    nc = k.nc
    B, T, D = cfg["B"], cfg["T"], cfg["D"]
    H, Dh = 16, 64
    NIC = D // 128
    NTB = T // 512
    sc = ExitStack()

    cb20, neg20 = k.constcol(-20.0)
    ident = k.sb(sc, "p1_ident", [128, 128], BF16)
    emit_identity(k, ident)
    triu = k.sb(sc, "p1_triu", [128, 128], BF16)
    k.op("pool", "memset", [triu], [], triu.ap[:], 1.0)
    k.op("pool", "affine_select", [triu], [triu],
         out=triu.ap[:], in_=triu.ap[:], compare_op=ALU.is_ge,
         fill=0.0, base=0, pattern=[[1, 128]], channel_multiplier=-1)

    wkqv = k.sb(sc, "p1_wkqv", [128, NIC * 3 * D], BF16)
    k.dma("sp", [wkqv], [g["wkqv_bf"]], wkqv.ap[:], g["wkqv_bf"].ap[:])
    wproj = k.sb(sc, "p1_wproj", [128, NIC * D], BF16)
    k.dma("sp", [wproj], [g["wproj_bf"]], wproj.ap[:], g["wproj_bf"].ap[:])
    bproj = k.sb(sc, "p1_bproj", [128, D], F32)

    qf = k.sb(sc, "p1_q", [128, NIC * T], BF16)
    kf = k.sb(sc, "p1_k", [128, NIC * T], BF16)
    vt = k.sb(sc, "p1_v", [128, (T // 128) * D], BF16)
    xin = k.sb(sc, "p1_xin", [128, D], F32)
    xnt = k.sb(sc, "p1_xnt", [128, D], BF16)
    xnf = k.sb(sc, "p1_xnf", [128, NIC * 512], BF16)
    stats = k.sb(sc, "p1_stats", [128, 12], F32)
    mv = k.sb(sc, "p1_mv", [128, 2], F32)
    rstd = k.sb(sc, "p1_rstd", [128, 1], F32)
    ctxf = xnf
    lparts = k.sb(sc, "p1_lparts", [128, 4 * 8], F32)
    ptb = k.sb(sc, "p1_ptb", [128, 16 * 512], BF16)
    lr = k.sb(sc, "p1_lr", [1, 2 * 512], F32)
    linv64 = k.sb(sc, "p1_linv64", [128, 512], F32)
    ones128b = k.sb(sc, "p1_o128", [128, 1], BF16)
    k.op("dve", "memset", [ones128b], [], ones128b.ap[:], 1.0)
    ones1f = k.sb(sc, "p1_o1f", [1, 128], F32)
    k.op("dve", "memset", [ones1f], [], ones1f.ap[:], 1.0)
    res = k.sb(sc, "p1_res", [128, D], F32)

    psA = [k.psum(sc, f"p1_psA{i}", [128, 512], F32) for i in range(2)]
    psC = k.psum(sc, "p1_psC", [128, 1024], F32)
    _psT0 = k.psum(sc, "p1_psT0", [128, 512], BF16)
    psT = [_psT0, _psT0]
    psO = k.psum(sc, "p1_psO", [128, 512], F32)
    psL = k.psum(sc, "p1_psL", [1, 512], F32)
    psLB = k.psum(sc, "p1_psLB", [128, 512], F32)

    sub = ExitStack()
    ones1 = k.sb(sub, "p1_ones1", [1, 128], F32)
    k.dma("sp", [res], [g["b_proj"]], res.ap[0:1, :], g["b_proj"].ap[0:1, :])
    k.op("dve", "memset", [ones1], [], ones1.ap[:], 1.0)
    for c in range(0, D, 512):
        k.mm(psA[0], [(psA[0].ap[:, 0:512], ones1.ap[:], res.ap[0:1, c:c + 512])],
             [res, ones1])
        k.op("dve", "tensor_copy", [bproj], [psA[0]],
             bproj.ap[:, c:c + 512], psA[0].ap[:, 0:512])
    k.barrier()
    sub.close()

    for b in range(B):
        for tb in range(NTB):
            t0 = b * T + tb * 512
            for j in range(4):
                k.dma("sp", [xin], [g["x0"]], xin.ap[:],
                      g["x0"].ap[t0 + j * 128: t0 + (j + 1) * 128, :])
                mu_ap, rstd_ap = emit_ln(k, mv, stats, rstd, xin, xin.ap[:], D)
                k.op("dve", "tensor_scalar", [xnt], [xin, mv, rstd],
                     xnt.ap[:], xin.ap[:], mu_ap, rstd_ap,
                     ALU.subtract, ALU.mult)
                for ic in range(NIC):
                    pT = psT[ic % 2]
                    k.transpose(pT, pT.ap[0:128, 0:128], xnt,
                                xnt.ap[:, ic * 128:(ic + 1) * 128], ident)
                    k.op("dve", "tensor_copy", [xnf], [pT],
                         xnf.ap[:, ic * 512 + j * 128: ic * 512 + (j + 1) * 128],
                         pT.ap[0:128, 0:128])
            for oc in range(2 * NIC):
                dst = kf if oc < NIC else qf
                oc_l = oc % NIC
                pA = psA[oc % 2]
                items = [(pA.ap[:],
                          wkqv.ap[:, ic * 3 * D + oc * 128: ic * 3 * D + (oc + 1) * 128],
                          xnf.ap[:, ic * 512:(ic + 1) * 512]) for ic in range(NIC)]
                k.mm(pA, items, [wkqv, xnf])
                k.op("dve", "tensor_copy", [dst], [pA],
                     dst.ap[:, oc_l * T + tb * 512: oc_l * T + (tb + 1) * 512], pA.ap[:])
            for j in range(4):
                for h2 in range(2):
                    items = [(psC.ap[:, h2 * 512:(h2 + 1) * 512],
                              xnf.ap[:, ic * 512 + j * 128: ic * 512 + (j + 1) * 128],
                              wkqv.ap[:, ic * 3 * D + 2 * D + h2 * 512:
                                      ic * 3 * D + 2 * D + (h2 + 1) * 512])
                             for ic in range(NIC)]
                    k.mm(psC, items, [wkqv, xnf], accum=(False))
                tt = tb * 4 + j
                k.op("dve", "tensor_copy", [vt], [psC],
                     vt.ap[:, tt * D:(tt + 1) * D], psC.ap[:])

        for qb in range(NTB):
            nkb = qb + 1
            ntile = 4 * nkb
            for h in range(H):
                hc, hr = h // 2, (h % 2) * 64
                for tkg in range(ntile):
                    diag = (tkg // 4 == qb)
                    c0 = (tkg % 4) * 128 if diag else 0
                    pA = psA[tkg % 2]
                    k.mm(pA, [(pA.ap[:, c0:512],
                               kf.ap[hr:hr + 64, hc * T + tkg * 128:
                                     hc * T + tkg * 128 + 128],
                               qf.ap[hr:hr + 64, hc * T + qb * 512 + c0:
                                     hc * T + qb * 512 + 512])],
                         [qf, kf])
                    if c0 > 0:
                        k.op("dve", "memset", [ptb], [],
                             ptb.ap[:, tkg * 512: tkg * 512 + c0], 0.0)
                    k.op("act", "activation", [ptb], [pA, cb20],
                         ptb.ap[:, tkg * 512 + c0: (tkg + 1) * 512],
                         pA.ap[:, c0:512], AF.Exp, scale=0.125, bias=neg20)
                    if diag:
                        k.op("dve", "tensor_tensor", [ptb], [ptb, triu],
                             ptb.ap[:, tkg * 512 + c0: tkg * 512 + c0 + 128],
                             ptb.ap[:, tkg * 512 + c0: tkg * 512 + c0 + 128],
                             triu.ap[:], ALU.mult)
                for tkg in range(ntile):
                    k.mm(psL, [(psL.ap[:], ones128b.ap[:],
                                ptb.ap[:, tkg * 512:(tkg + 1) * 512])],
                         [ptb, ones128b], accum=(tkg > 0), stop=(tkg == ntile - 1))
                hp = h % 2
                k.op("dve", "tensor_copy", [lr], [psL],
                     lr.ap[:, 0:512], psL.ap[:])
                for tkg in range(ntile):
                    k.mm(psO, [(psO.ap[hr:hr + 64, :],
                                vt.ap[:, tkg * D + h * Dh:
                                      tkg * D + (h + 1) * Dh],
                                ptb.ap[:, tkg * 512:(tkg + 1) * 512])],
                         [vt, ptb], accum=(tkg > 0), stop=(tkg == ntile - 1))
                k.op("act", "activation", [lr], [lr], lr.ap[:, 512:1024],
                     lr.ap[:, 0:512], AF.Ln)
                k.op("act", "activation", [lr], [lr], lr.ap[:, 512:1024],
                     lr.ap[:, 512:1024], AF.Exp, scale=-1.0)
                k.mm(psLB, [(psLB.ap[hr:hr + 64, :], ones1f.ap[:, 0:64],
                             lr.ap[:, 512:1024])], [lr, ones1f])
                if hp == 1:
                    k.op("dve", "tensor_copy", [linv64], [psLB],
                         linv64.ap[:], psLB.ap[:])
                    k.op("dve", "tensor_tensor", [ctxf], [psO, linv64],
                         ctxf.ap[:, hc * 512:(hc + 1) * 512],
                         psO.ap[:], linv64.ap[:], ALU.mult)
            t0 = b * T + qb * 512
            for j in range(4):
                for half in range(2):
                    items = [(psC.ap[:, half * 512:(half + 1) * 512],
                              ctxf.ap[:, ic * 512 + j * 128: ic * 512 + (j + 1) * 128],
                              wproj.ap[:, ic * D + half * 512: ic * D + (half + 1) * 512])
                             for ic in range(NIC)]
                    k.mm(psC, items, [ctxf, wproj])
                k.dma("sp", [xin], [g["x0"]], xin.ap[:],
                      g["x0"].ap[t0 + j * 128: t0 + (j + 1) * 128, :])
                k.op("dve", "tensor_tensor", [res], [psC, bproj],
                     res.ap[:], psC.ap[:], bproj.ap[:], ALU.add)
                k.op("dve", "tensor_tensor", [res], [res, xin],
                     res.ap[:], res.ap[:], xin.ap[:], ALU.add)
                k.dma("sp", [g["x1"]], [res],
                      g["x1"].ap[t0 + j * 128: t0 + (j + 1) * 128, :], res.ap[:])
    sc.close()
    k.barrier()


def build_phase2(k, g, cfg):
    nc = k.nc
    B, T, D = cfg["B"], cfg["T"], cfg["D"]
    E, Fh = 8, 4096
    NIC = D // 128
    NOC = Fh // 128
    NTOK = B * T
    NTB = NTOK // 512
    NT1 = NTOK // 512 + E
    NSLOT = NT1 * 512
    OOBV = NTOK
    sc = ExitStack()

    ident = k.sb(sc, "p2_ident", [128, 128], BF16)
    emit_identity(k, ident)

    xn2tok = k.dram("p2_xn2tok", [NTOK, D], BF16)
    ysel = k.dram("p2_ysel", [NTOK, D], F32)
    yscr = k.dram("p2_yscr", [NTOK, D], F32)
    tokof_d = k.dram("p2_tokof", [NSLOT, 1], U32)

    wr = k.sb(sc, "p2_wr", [128, NIC * E], F32)
    k.dma("sp", [wr], [g["Wr"]], wr.ap[:].rearrange("p (c e) -> p c e", e=E),
          g["Wr"].ap[:].rearrange("(c p) e -> p c e", p=128))
    brow = k.sb(sc, "p2_br", [E, 1], F32)
    k.dma("sp", [brow], [g["br"]], brow.ap[:], g["br"].ap[0:1, :].rearrange("o e -> e o"))
    maskf = k.sb(sc, "p2_maskf", [E, NTOK], BF16)
    gates = k.sb(sc, "p2_gates", [128, NTOK // 128], F32)
    wofs = k.sb(sc, "p2_wofs", [128, NT1], U32)
    b1ofs = k.sb(sc, "p2_b1ofs", [128, NT1], U32)

    sa = ExitStack()
    ones1 = k.sb(sa, "p2_ones1", [1, 128], F32)
    k.op("dve", "memset", [ones1], [], ones1.ap[:], 1.0)
    b2sb = k.sb(sa, "p2_b2", [E, D], BF16)
    k.dma("pool", [b2sb], [g["b2"]], b2sb.ap[:], g["b2"].ap[:])
    lngrow = k.sb(sa, "p2_lngr", [1, D], F32)
    k.dma("sp", [lngrow], [g["ln_g"]], lngrow.ap[:], g["ln_g"].ap[0:1, :])
    lnbrow = k.sb(sa, "p2_lnbr", [1, D], F32)
    k.dma("sp", [lnbrow], [g["ln_b"]], lnbrow.ap[:], g["ln_b"].ap[0:1, :])
    lng = k.sb(sa, "p2_lng", [128, D], F32)
    lnb = k.sb(sa, "p2_lnb", [128, D], F32)
    xin = k.sb(sa, "p2_xin", [128, 4 * D], F32)
    xnp = k.sb(sa, "p2_xnp", [128, 4 * D], F32)
    xnpf = k.sb(sa, "p2_xnpf", [128, NIC * 512], F32)
    xn2t = k.sb(sa, "p2_xn2t", [128, D], BF16)
    tmp = k.sb(sa, "p2_tmp", [128, D], F32)
    stats = k.sb(sa, "p2_stats", [128, 12], F32)
    mv = k.sb(sa, "p2_mv", [128, 2], F32)
    rstd = k.sb(sa, "p2_rstd", [128, 1], F32)
    lgf = k.sb(sa, "p2_lgf", [E, 512], F32)
    lt = k.sb(sa, "p2_lt", [128, E], F32)
    ssum = k.sb(sa, "p2_ssum", [128, 1], F32)
    eexp = k.sb(sa, "p2_eexp", [128, E], F32)
    m1 = k.sb(sa, "p2_m1", [128, 1], F32)
    negm = k.sb(sa, "p2_negm", [128, 1], F32)
    eq = k.sb(sa, "p2_eq", [128, E], F32)
    cum = k.sb(sa, "p2_cum", [128, E], F32)
    le1 = k.sb(sa, "p2_le1", [128, E], F32)
    mtm = k.sb(sa, "p2_mtm", [128, E], BF16)
    ysb = k.sb(sa, "p2_ysb", [128, D], F32)
    psT = k.psum(sa, "p2_psT", [128, 512], F32)
    psTb = k.psum(sa, "p2_psTb", [128, 512], BF16)
    psR = k.psum(sa, "p2_psR", [E, 512], F32)
    psY = k.psum(sa, "p2_psY", [128, 1024], F32)
    identf = k.sb(sa, "p2_identf", [128, 128], F32)
    emit_identity(k, identf)

    emit_bcast_row(k, lng, lngrow, psT, ones1, D)
    emit_bcast_row(k, lnb, lnbrow, psT, ones1, D)

    for tb in range(NTB):
        t0 = tb * 512
        k.dma("sp", [xin], [g["x1"]],
              xin.ap[:].rearrange("p (f d) -> p f d", d=D),
              g["x1"].ap[t0:t0 + 512, :].rearrange("(f p) d -> p f d", p=128))
        for j in range(4):
            x_ap = xin.ap[:, j * D:(j + 1) * D]
            mu_ap, rstd_ap = emit_ln(k, mv, stats, rstd, xin, x_ap, D)
            k.op("dve", "tensor_scalar", [xnp], [xin, mv, rstd],
                 xnp.ap[:, j * D:(j + 1) * D], x_ap, mu_ap, rstd_ap,
                 ALU.subtract, ALU.mult)
            for ic in range(NIC):
                k.transpose(psT, psT.ap[0:128, 0:128], xnp,
                            xnp.ap[:, j * D + ic * 128: j * D + (ic + 1) * 128], identf)
                k.op("dve", "tensor_copy", [xnpf], [psT],
                     xnpf.ap[:, ic * 512 + j * 128: ic * 512 + (j + 1) * 128],
                     psT.ap[0:128, 0:128])
            xp_ap = xnp.ap[:, j * D:(j + 1) * D]
            mu2, rstd2 = emit_ln(k, mv, stats, rstd, xnp, xp_ap, D)
            k.op("dve", "tensor_scalar", [tmp], [xnp, mv, rstd],
                 tmp.ap[:], xp_ap, mu2, rstd2, ALU.subtract, ALU.mult)
            k.op("dve", "tensor_tensor", [tmp], [tmp, lng],
                 tmp.ap[:], tmp.ap[:], lng.ap[:], ALU.mult)
            k.op("dve", "tensor_tensor", [xn2t], [tmp, lnb],
                 xn2t.ap[:], tmp.ap[:], lnb.ap[:], ALU.add)
            k.dma("sp", [xn2tok], [xn2t],
                  xn2tok.ap[t0 + j * 128: t0 + (j + 1) * 128, :], xn2t.ap[:])
        k.mm(psR, [(psR.ap[:], wr.ap[:, ic * E:(ic + 1) * E],
                    xnpf.ap[:, ic * 512:(ic + 1) * 512]) for ic in range(NIC)],
             [wr, xnpf])
        k.op("act", "activation", [lgf], [psR, brow],
             lgf.ap[:], psR.ap[:], AF.Identity, bias=brow.ap[:])
        for j in range(4):
            tt = t0 // 128 + j
            k.transpose(psT, psT.ap[0:128, 0:E], lgf,
                        lgf.ap[:, j * 128:(j + 1) * 128], identf)
            k.op("dve", "tensor_copy", [lt], [psT], lt.ap[:], psT.ap[0:128, 0:E])
            k.op("dve", "tensor_reduce", [m1], [lt], m1.ap[:], lt.ap[:], AX.X, ALU.max)
            k.op("dve", "tensor_scalar", [negm], [m1], negm.ap[:], m1.ap[:],
                 -1.0, None, ALU.mult)
            k.op("act", "activation", [eexp, ssum], [lt, negm],
                 eexp.ap[:], lt.ap[:], AF.Exp, bias=negm.ap[:],
                 accum_out=ssum.ap[:])
            k.op("dve", "reciprocal", [gates], [ssum],
                 gates.ap[:, tt:tt + 1], ssum.ap[:])
            k.op("dve", "tensor_scalar", [eq], [lt, m1], eq.ap[:], lt.ap[:],
                 m1.ap[:], None, ALU.is_equal)
            k.op("dve", "tensor_tensor_scan", [cum], [eq],
                 cum.ap[:], eq.ap[:], eq.ap[:], 0.0, ALU.add, ALU.bypass)
            k.op("dve", "tensor_scalar", [le1], [cum], le1.ap[:], cum.ap[:],
                 1.0, None, ALU.is_le)
            k.op("dve", "tensor_tensor", [mtm], [eq, le1], mtm.ap[:], eq.ap[:],
                 le1.ap[:], ALU.mult)
            k.transpose(psTb, psTb.ap[0:E, 0:128], mtm, mtm.ap[:], ident)
            k.op("dve", "tensor_copy", [maskf], [psTb],
                 maskf.ap[:, t0 + j * 128: t0 + (j + 1) * 128], psTb.ap[0:E, 0:128])
            for half in range(2):
                k.mm(psY, [(psY.ap[:, half * 512:(half + 1) * 512],
                            maskf.ap[:, t0 + j * 128: t0 + (j + 1) * 128],
                            b2sb.ap[:, half * 512:(half + 1) * 512])],
                     [maskf, b2sb])
            k.op("dve", "tensor_copy", [ysb], [psY], ysb.ap[:], psY.ap[:])
            k.dma("sp", [ysel], [ysb],
                  ysel.ap[t0 + j * 128: t0 + (j + 1) * 128, :], ysb.ap[:])

    mf32 = k.sb(sa, "p2_mf32", [E, NTOK], F32)
    ranks = k.sb(sa, "p2_ranks", [E, NTOK], F32)
    c8 = k.sb(sa, "p2_c8", [E, 1], F32)
    ntl = k.sb(sa, "p2_ntl", [E, 1], F32)
    t8 = k.sb(sa, "p2_t8", [E, 1], F32)
    row8 = k.sb(sa, "p2_row8", [1, 3 * E], F32)
    row24 = k.sb(sa, "p2_row24", [1, 2 * NT1], F32)
    iop = k.sb(sa, "p2_iop", [128, 1], I32)
    iopf = k.sb(sa, "p2_iopf", [128, 1], F32)
    ekbf = k.sb(sa, "p2_ekbf", [128, NT1], F32)
    NTI = NTOK // 128
    NN = max(NT1, NTI)
    i64r = k.sb(sa, "p2_i64r", [1, NN], I32)
    i64f = k.sb(sa, "p2_i64f", [1, NN], F32)
    tokid = k.sb(sa, "p2_tokid", [128, NN], F32)
    tokidu = k.sb(sa, "p2_tokidu", [128, NTI], U32)
    slotm = k.sb(sa, "p2_slotm", [128, NTI], F32)
    slotu = k.sb(sa, "p2_slotu", [128, NTI], U32)
    srow = k.sb(sa, "p2_srow", [1, 512], F32)
    ones8f = k.sb(sa, "p2_o8f", [8, 1], F32)
    oobt = k.sb(sa, "p2_oobt", [128, NSLOT // 128], U32)

    k.op("dve", "memset", [ones8f], [], ones8f.ap[:], 1.0)
    k.op("pool", "iota", [iop], [], iop.ap[:], pattern=[[0, 1]], base=0,
         channel_multiplier=1)
    k.op("dve", "tensor_copy", [iopf], [iop], iopf.ap[:], iop.ap[:])
    k.op("pool", "iota", [i64r], [], i64r.ap[:], pattern=[[1, NN]], base=0,
         channel_multiplier=0)
    k.op("dve", "tensor_copy", [i64f], [i64r], i64f.ap[:], i64r.ap[:])
    k.op("pool", "iota", [oobt], [], oobt.ap[:], pattern=[[1, NSLOT // 128]],
         base=OOBV, channel_multiplier=NSLOT // 128,
         allow_small_or_imprecise_dtypes=True)
    k.dma("pool", [tokof_d], [oobt],
          tokof_d.ap[:].rearrange("(p i) o -> p (i o)", p=128), oobt.ap[:])

    k.op("dve", "tensor_copy", [mf32], [maskf], mf32.ap[:], maskf.ap[:])
    k.op("dve", "tensor_tensor_scan", [ranks], [mf32],
         ranks.ap[:], mf32.ap[:], mf32.ap[:], 0.0, ALU.add, ALU.bypass)
    k.op("dve", "tensor_copy", [c8], [ranks], c8.ap[:], ranks.ap[:, NTOK - 1:NTOK])
    k.op("dve", "tensor_single_scalar", [ntl], [c8], ntl.ap[:], c8.ap[:],
         0.0, ALU.is_gt)
    for kk in range(1, NTOK // 512):
        k.op("dve", "tensor_single_scalar", [t8], [c8], t8.ap[:], c8.ap[:],
             float(512 * kk), ALU.is_gt)
        k.op("dve", "tensor_tensor", [ntl], [ntl, t8], ntl.ap[:], ntl.ap[:],
             t8.ap[:], ALU.add)
    k.transpose(psT, psT.ap[0:1, 0:E], ntl, ntl.ap[:], identf)
    k.op("dve", "tensor_copy", [row8], [psT], row8.ap[:, 0:E], psT.ap[0:1, 0:E])
    k.op("dve", "tensor_tensor_scan", [row8], [row8],
         row8.ap[:, E:2 * E], row8.ap[:, 0:E], row8.ap[:, 0:E],
         0.0, ALU.add, ALU.bypass)
    k.op("dve", "tensor_tensor", [row8], [row8],
         row8.ap[:, E:2 * E], row8.ap[:, E:2 * E], row8.ap[:, 0:E], ALU.subtract)
    iota24 = i64f.ap[:, 0:NT1]
    k.op("dve", "tensor_scalar", [row24], [i64f, row8],
         row24.ap[:, 0:NT1], iota24, row8.ap[:, E:E + 1], None, ALU.is_ge)
    for e in range(1, E):
        k.op("dve", "tensor_scalar", [row24], [i64f, row8],
             row24.ap[:, NT1:2 * NT1], iota24, row8.ap[:, E + e:E + e + 1],
             None, ALU.is_ge)
        k.op("dve", "tensor_tensor", [row24], [row24],
             row24.ap[:, 0:NT1], row24.ap[:, 0:NT1], row24.ap[:, NT1:2 * NT1],
             ALU.add)
    k.op("dve", "tensor_scalar", [row24], [row24], row24.ap[:, 0:NT1],
         row24.ap[:, 0:NT1], -1.0, None, ALU.add)
    k.mm(psT, [(psT.ap[:, 0:NT1], ones1.ap[:], row24.ap[:, 0:NT1])],
         [row24, ones1])
    k.op("dve", "tensor_copy", [ekbf], [psT], ekbf.ap[:], psT.ap[:, 0:NT1])
    k.op("dve", "scalar_tensor_tensor", [tokid], [ekbf, iopf], tokid.ap[:, 0:NT1],
         ekbf.ap[:], 128.0, iopf.ap[:].broadcast_to((128, NT1)), ALU.mult, ALU.add)
    k.op("dve", "tensor_copy", [wofs], [tokid], wofs.ap[:], tokid.ap[:, 0:NT1])
    k.op("dve", "scalar_tensor_tensor", [tokid], [ekbf, iopf], tokid.ap[:, 0:NT1],
         ekbf.ap[:], 32.0, iopf.ap[:].broadcast_to((128, NT1)), ALU.mult, ALU.add)
    k.op("dve", "tensor_copy", [b1ofs], [tokid], b1ofs.ap[:], tokid.ap[:, 0:NT1])
    k.transpose(psT, psT.ap[0:E, 0:1], row8, row8.ap[:, E:2 * E].rearrange(
        "o e -> o e") if False else row8.ap[0:1, E:2 * E], identf)
    k.op("dve", "tensor_copy", [t8], [psT], t8.ap[:], psT.ap[0:E, 0:1])
    k.op("dve", "tensor_scalar", [t8], [t8], t8.ap[:], t8.ap[:], 512.0, -1.0,
         ALU.mult, ALU.add)
    k.op("dve", "tensor_scalar", [ranks], [ranks, t8], ranks.ap[:], ranks.ap[:],
         t8.ap[:], None, ALU.add)
    k.op("dve", "tensor_tensor", [ranks], [ranks, mf32], ranks.ap[:],
         ranks.ap[:], mf32.ap[:], ALU.mult)
    for ch in range(NTOK // 512):
        k.mm(psR, [(psR.ap[0:1, :], ones8f.ap[:],
                    ranks.ap[:, ch * 512:(ch + 1) * 512])], [ranks, ones8f])
        k.op("dve", "tensor_copy", [srow], [psR], srow.ap[:], psR.ap[0:1, :])
        for u in range(4):
            k.transpose(psT, psT.ap[0:128, 0:1], srow,
                        srow.ap[:, u * 128:(u + 1) * 128], identf)
            k.op("dve", "tensor_copy", [slotm], [psT],
                 slotm.ap[:, ch * 4 + u: ch * 4 + u + 1], psT.ap[0:128, 0:1])
    k.op("pool", "iota", [tokid], [], tokid.ap[:, 0:NTI], pattern=[[128, NTI]],
         base=0, channel_multiplier=1, allow_small_or_imprecise_dtypes=True)
    k.op("dve", "tensor_copy", [tokidu], [tokid], tokidu.ap[:], tokid.ap[:, 0:NTI])
    k.op("dve", "tensor_copy", [slotu], [slotm], slotu.ap[:], slotm.ap[:])
    for i in range(NTI):
        k.idma([tokof_d], [tokidu, slotu], tokof_d.ap[:],
               slotu.ap[:, i:i + 1], tokidu.ap[:, i:i + 1], None,
               skip_waw=(i > 0))
    sa.close()
    k.barrier()

    sb_ = ExitStack()
    w1h = [k.sb(sb_, f"p2_w1h{i}", [128, NIC * 2048], BF16) for i in range(2)]
    w2h = [k.sb(sb_, f"p2_w2h{i}", [128, NOC * 512], BF16) for i in range(2)]
    b1row = k.sb(sb_, "p2_b1r", [32, 128], F32)
    b1sb = k.sb(sb_, "p2_b1", [128, NOC], F32)
    identf2 = k.sb(sb_, "p2_identf2", [128, 128], F32)
    emit_identity(k, identf2)
    tkof = k.sb(sb_, "p2_tkof", [128, 4], U32)
    tkj = [k.sb(sb_, f"p2_tkj{j}", [128, 1], U32) for j in range(4)]
    xgt = k.sb(sb_, "p2_xgt", [128, 4 * D], BF16)
    xgf = k.sb(sb_, "p2_xgf", [128, NIC * 512], BF16)
    hm = k.sb(sb_, "p2_hm", [128, NOC * 512], BF16)
    yev = [k.sb(sb_, f"p2_yev{i}", [128, D], F32) for i in range(2)]
    psH = [k.psum(sb_, f"p2_psH{i}", [128, 512], F32) for i in range(2)]
    psYh = [k.psum(sb_, f"p2_psYh{i}", [128, 512], F32) for i in range(2)]
    psM = k.psum(sb_, "p2_psM", [128, 512], F32)
    psTb2 = k.psum(sb_, "p2_psTb2", [128, 512], BF16)

    breg = nc.gpsimd.to_reg(NTOK - 1)
    b1v = g["b1"].ap[:].rearrange("e (c p) -> (e c) p", p=128)

    def gather_w(kt):
        k.idma([w1h[0]], [wofs], w1h[0].ap[:], None,
               g["w1a_bf"].ap[:], wofs.ap[:, kt:kt + 1])
        k.idma([w1h[1]], [wofs], w1h[1].ap[:], None,
               g["w1b_bf"].ap[:], wofs.ap[:, kt:kt + 1])
        k.idma([w2h[0]], [wofs], w2h[0].ap[:], None,
               g["w2a_bf"].ap[:], wofs.ap[:, kt:kt + 1])
        k.idma([w2h[1]], [wofs], w2h[1].ap[:], None,
               g["w2b_bf"].ap[:], wofs.ap[:, kt:kt + 1])
        k.idma([b1row], [b1ofs], b1row.ap[:], None, b1v,
               b1ofs.ap[0:32, kt:kt + 1])

    dbg_xg = k.dram("p2_dbg_xg", [NT1, 128, 4 * D], BF16) if cfg.get("P2_DBG") else None
    if cfg.get("P2_DBG"):
        dbg_w = k.dram("p2_dbg_w", [128, NT1], U32)
        k.dma("sp", [dbg_w], [wofs], dbg_w.ap[:], wofs.ap[:])
        dbg_b = k.dram("p2_dbg_b", [128, NT1], U32)
        k.dma("sp", [dbg_b], [b1ofs], dbg_b.ap[:], b1ofs.ap[:])
        dbg_hm = k.dram("p2_dbg_hm", [NT1, 128, NOC * 512], BF16)
        dbg_w1 = k.dram("p2_dbg_w1", [NT1, 128, 4096], BF16)
        dbg_y0 = k.dram("p2_dbg_y0", [NT1 * 4, D], F32)
        dbg_tk = k.dram("p2_dbg_tk", [NT1, 128, 4], U32)
    for kt in range(cfg.get("P2_TILES", NT1)):
        k.dma("sp", [tkof], [tokof_d], tkof.ap[:],
              tokof_d.ap[kt * 512:(kt + 1) * 512, :]
              .rearrange("(p j) o -> p (j o)", j=4))
        gather_w(kt)
        k.transpose(psM, psM.ap[0:128, 0:NOC], b1row, b1row.ap[0:32, :], identf2)
        k.op("dve", "tensor_copy", [b1sb], [psM], b1sb.ap[:], psM.ap[0:128, 0:NOC])
        for j in range(4):
            k.op("dve", "tensor_copy", [tkj[j]], [tkof], tkj[j].ap[:],
                 tkof.ap[:, j:j + 1])
            k.idma([xgt], [xn2tok, tkj[j]], xgt.ap[:, j * D:(j + 1) * D], None,
                   xn2tok.ap[:], tkj[j].ap[:],
                   bounds_check=breg, oob_is_err=False)
            for ic in range(NIC):
                k.transpose(psTb2, psTb2.ap[0:128, 0:128], xgt,
                            xgt.ap[:, j * D + ic * 128: j * D + (ic + 1) * 128], ident)
                k.op("dve", "tensor_copy", [xgf], [psTb2],
                     xgf.ap[:, ic * 512 + j * 128: ic * 512 + (j + 1) * 128],
                     psTb2.ap[0:128, 0:128])
        if dbg_xg is not None:
            k.dma("sp", [dbg_xg], [xgt], dbg_xg.ap[kt], xgt.ap[:])
            if kt > 0:
                k.dma("sp", [dbg_hm], [hm], dbg_hm.ap[kt - 1], hm.ap[:])
                k.dma("sp", [dbg_w1], [w1h[0]], dbg_w1.ap[kt - 1], w1h[0].ap[:, 0:4096])
        for oc in range(NOC):
            hh = oc // 16
            ocl = oc % 16
            pH = psH[oc % 2]
            k.mm(pH, [(pH.ap[:],
                       w1h[hh].ap[:, ic * 2048 + ocl * 128: ic * 2048 + (ocl + 1) * 128],
                       xgf.ap[:, ic * 512:(ic + 1) * 512]) for ic in range(NIC)],
                 [w1h[hh], xgf])
            k.op("act", "activation", [hm], [pH, b1sb],
                 hm.ap[:, oc * 512:(oc + 1) * 512], pH.ap[:], AF.Relu,
                 bias=b1sb.ap[:, oc:oc + 1])
        if dbg_xg is not None:
            k.dma("sp", [dbg_tk], [tkof], dbg_tk.ap[kt], tkof.ap[:])
        for j in range(4):
            for half in range(2):
                pY = psYh[half]
                k.mm(pY, [(pY.ap[:],
                           hm.ap[:, hc * 512 + j * 128: hc * 512 + (j + 1) * 128],
                           w2h[half].ap[:, hc * 512:(hc + 1) * 512])
                          for hc in range(NOC)],
                     [hm, w2h[half]])
                ev = "dve" if half == 0 else "act"
                if ev == "dve":
                    k.op("dve", "tensor_copy", [yev[j % 2]], [pY],
                         yev[j % 2].ap[:, half * 512:(half + 1) * 512], pY.ap[:])
                else:
                    k.op("act", "copy", [yev[j % 2]], [pY],
                         yev[j % 2].ap[:, half * 512:(half + 1) * 512], pY.ap[:])
            k.idma([yscr], [yev[j % 2], tkj[j]], yscr.ap[:],
                   tkj[j].ap[:], yev[j % 2].ap[:], None,
                   bounds_check=breg, oob_is_err=False, skip_waw=False)

    sb_.close()
    k.barrier()

    sc2 = ExitStack()
    xin2 = [k.sb(sc2, f"p2_xin2{i}", [128, D], F32) for i in range(2)]
    ysum = [k.sb(sc2, f"p2_ysum{i}", [128, D], F32) for i in range(2)]
    yld = [k.sb(sc2, f"p2_yld{i}", [128, D], F32) for i in range(2)]
    out2 = [k.sb(sc2, f"p2_out2{i}", [128, D], F32) for i in range(2)]
    for ti in range(NTOK // 128):
        i = ti % 2
        t0 = ti * 128
        k.dma("sp", [xin2[i]], [g["x1"]], xin2[i].ap[:], g["x1"].ap[t0:t0 + 128, :])
        k.dma("sp", [ysum[i]], [ysel], ysum[i].ap[:], ysel.ap[t0:t0 + 128, :])
        k.dma("sp", [yld[i]], [yscr], yld[i].ap[:], yscr.ap[t0:t0 + 128, :])
        k.op("dve", "tensor_tensor", [ysum[i]], [ysum[i], yld[i]],
             ysum[i].ap[:], ysum[i].ap[:], yld[i].ap[:], ALU.add)
        k.op("dve", "tensor_scalar", [ysum[i]], [ysum[i], gates],
             ysum[i].ap[:], ysum[i].ap[:], gates.ap[:, ti:ti + 1], None, ALU.mult)
        k.op("dve", "tensor_tensor", [out2[i]], [ysum[i], xin2[i]],
             out2[i].ap[:], ysum[i].ap[:], xin2[i].ap[:], ALU.add)
        k.dma("sp", [g["x2"]], [out2[i]],
              g["x2"].ap[t0:t0 + 128, :], out2[i].ap[:])
    sc2.close()
    sc.close()
    k.barrier()


def build_phase3(k, g, cfg):
    nc = k.nc
    B, T, D = cfg["B"], cfg["T"], cfg["D"]
    HG, DK, DV = 4, 1024, 2048
    HK, HV = DK // HG, DV // HG
    C = 64
    NIC = D // 128
    NTB = T // 512
    NTOK = B * T
    NCH = T // C
    GN = 16.0
    sc = ExitStack()

    ident = k.sb(sc, "p3_ident", [128, 128], BF16)
    emit_identity(k, ident)
    identf = k.sb(sc, "p3_identf", [128, 128], F32)
    emit_identity(k, identf)
    triu64 = k.sb(sc, "p3_triu", [64, 64], mybir.dt.int8)
    k.op("pool", "memset", [triu64], [], triu64.ap[:], 1)
    k.op("pool", "affine_select", [triu64], [triu64],
         out=triu64.ap[:], in_=triu64.ap[:], compare_op=ALU.is_ge,
         fill=0, base=0, pattern=[[1, 64]], channel_multiplier=-1)
    z64 = k.sb(sc, "p3_z64", [64, 64], F32)
    k.op("pool", "memset", [z64], [], z64.ap[:], 0.0)
    ones1 = k.sb(sc, "p3_ones1", [1, 128], F32)
    k.op("dve", "memset", [ones1], [], ones1.ap[:], 1.0)

    xnf_scr = k.dram("p3_xnf", [B * NTB, 128, NIC * 512], BF16)
    qt_scr = k.dram("p3_qt", [B * NTB, 128, NIC * 512], BF16)
    ki_scr = k.dram("p3_ki", [B * NTB, 128, NIC * 512], BF16)
    khe_scr = k.dram("p3_khe", [NTOK, DK], BF16)
    v_scr = k.dram("p3_v", [NTOK, DV], BF16)
    o_scr = k.dram("p3_o", [NTOK, DV], BF16)
    Dsb = k.sb(sc, "p3_D", [128, NIC * (NTOK // C // B) * B], F32)

    gkw1 = k.sb(sc, "p3_gkw1", [128, NIC * 16], BF16)
    k.dma("pool", [gkw1], [g["gk_w1"]],
          gkw1.ap[:].rearrange("p (c e) -> p c e", e=16),
          g["gk_w1"].ap[:].rearrange("(c p) e -> p c e", p=128))
    gkw2 = k.sb(sc, "p3_gkw2", [16, DK], BF16)
    k.dma("pool", [gkw2], [g["gk_w2"]], gkw2.ap[:], g["gk_w2"].ap[:])
    gkbrow = k.sb(sc, "p3_gkbr", [1, DK], F32)
    k.dma("sp", [gkbrow], [g["gk_b"]], gkbrow.ap[:], g["gk_b"].ap[0:1, :])
    rmsrow = k.sb(sc, "p3_rmsr", [1, HV], F32)
    k.dma("sp", [rmsrow], [g["rms_w"]], rmsrow.ap[:], g["rms_w"].ap[0:1, :])
    rmsw = k.sb(sc, "p3_rmsw", [128, HV], F32)

    sa = ExitStack()
    wq = k.sb(sa, "p3_wq", [128, NIC * DK], BF16)
    wk_ = k.sb(sa, "p3_wk", [128, NIC * DK], BF16)
    wv = k.sb(sa, "p3_wv", [128, NIC * DV], BF16)
    k.dma("sp", [wq], [g["wq_bf"]], wq.ap[:], g["wq_bf"].ap[:])
    k.dma("sp", [wk_], [g["wk_bf"]], wk_.ap[:], g["wk_bf"].ap[:])
    k.dma("sp", [wv], [g["wv_bf"]], wv.ap[:], g["wv_bf"].ap[:])
    xin = k.sb(sa, "p3_xin", [128, 4 * D], F32)
    xnt = k.sb(sa, "p3_xnt", [128, 4 * D], BF16)
    xnf = k.sb(sa, "p3_xnf_sb", [128, NIC * 512], BF16)
    stats = k.sb(sa, "p3_stats", [128, 12], F32)
    mv = k.sb(sa, "p3_mv", [128, 2], F32)
    rstd = k.sb(sa, "p3_rstd", [128, 1], F32)
    qtf = k.sb(sa, "p3_qtf", [128, NIC * 512], BF16)
    kif = k.sb(sa, "p3_kif", [128, NIC * 512], BF16)
    kef = k.sb(sa, "p3_kef", [128, 512], BF16)
    ket = k.sb(sa, "p3_ket", [64, 8 * 1024], BF16)
    vtm = k.sb(sa, "p3_vtm", [128, 4 * DV], BF16)
    midb = k.sb(sa, "p3_mid", [16, 512], BF16)
    glf = k.sb(sa, "p3_glf", [128, 512], F32)
    gf = k.sb(sa, "p3_gf", [128, NIC * 512], F32)
    cumg = k.sb(sa, "p3_cumg", [128, NIC * 512], F32)
    lam = k.sb(sa, "p3_lam", [128, 512], F32)
    enc = k.sb(sa, "p3_enc", [128, 512], F32)
    t1 = k.sb(sa, "p3_t1", [128, 512], F32)
    t2 = k.sb(sa, "p3_t2", [128, 512], F32)
    dcol = k.sb(sa, "p3_dcol", [128, 8], F32)
    gkb = k.sb(sa, "p3_gkb", [128, NIC], F32)
    gkbr2 = k.sb(sa, "p3_gkbr2", [NIC, 128], F32)
    psT = k.psum(sa, "p3_psT", [128, 512], F32)
    psTb = k.psum(sa, "p3_psTb", [128, 512], BF16)
    psA = k.psum(sa, "p3_psA", [128, 512], F32)
    psV = k.psum(sa, "p3_psV", [128, 1024], F32)
    psM = k.psum(sa, "p3_psM", [16, 512], F32)

    k.dma("sp", [gkbr2], [g["gk_b"]],
          gkbr2.ap[:], g["gk_b"].ap[0:1, :].rearrange("o (c p) -> (o c) p", p=128))
    k.transpose(psT, psT.ap[0:128, 0:NIC], gkbr2, gkbr2.ap[:], identf)
    k.op("dve", "tensor_copy", [gkb], [psT], gkb.ap[:], psT.ap[0:128, 0:NIC])
    emit_bcast_row(k, rmsw, rmsrow, psT, ones1, HV)
    cbe, eps_col = k.constcol(1e-5)
    cb1, one_col = k.constcol(1.0)

    for b in range(B):
        for tb in range(NTB):
            gtb = b * NTB + tb
            t0 = b * T + tb * 512
            k.dma("sp", [xin], [g["x2"]],
                  xin.ap[:].rearrange("p (f d) -> p f d", d=D),
                  g["x2"].ap[t0:t0 + 512, :].rearrange("(f p) d -> p f d", p=128))
            for j in range(4):
                x_ap = xin.ap[:, j * D:(j + 1) * D]
                mu_ap, rstd_ap = emit_ln(k, mv, stats, rstd, xin, x_ap, D)
                k.op("dve", "tensor_scalar", [xnt], [xin, mv, rstd],
                     xnt.ap[:, j * D:(j + 1) * D], x_ap, mu_ap, rstd_ap,
                     ALU.subtract, ALU.mult)
                for ic in range(NIC):
                    k.transpose(psTb, psTb.ap[0:128, 0:128], xnt,
                                xnt.ap[:, j * D + ic * 128: j * D + (ic + 1) * 128], ident)
                    k.op("dve", "tensor_copy", [xnf], [psTb],
                         xnf.ap[:, ic * 512 + j * 128: ic * 512 + (j + 1) * 128],
                         psTb.ap[0:128, 0:128])
            k.dma("sp", [xnf_scr], [xnf], xnf_scr.ap[gtb], xnf.ap[:])
            for j in range(4):
                for q4 in range(4):
                    half = q4 % 2
                    k.mm(psV, [(psV.ap[:, half * 512:(half + 1) * 512],
                                xnf.ap[:, ic * 512 + j * 128: ic * 512 + (j + 1) * 128],
                                wv.ap[:, ic * DV + q4 * 512: ic * DV + (q4 + 1) * 512])
                               for ic in range(NIC)], [wv, xnf])
                    if half == 1:
                        k.op("dve", "tensor_copy", [vtm], [psV],
                             vtm.ap[:, j * DV + (q4 - 1) * 512: j * DV + (q4 + 1) * 512],
                             psV.ap[:])
            k.dma("sp", [v_scr], [vtm],
                  v_scr.ap[t0:t0 + 512, :].rearrange("(f p) d -> p f d", p=128),
                  vtm.ap[:].rearrange("p (f d) -> p f d", d=DV))
            k.mm(psM, [(psM.ap[:], gkw1.ap[:, ic * 16:(ic + 1) * 16],
                        xnf.ap[:, ic * 512:(ic + 1) * 512]) for ic in range(NIC)],
                 [gkw1, xnf])
            k.op("dve", "tensor_copy", [midb], [psM], midb.ap[:], psM.ap[:])
            for pc in range(NIC):
                k.mm(psA, [(psA.ap[:], gkw2.ap[:, pc * 128:(pc + 1) * 128],
                            midb.ap[:])], [gkw2, midb])
                k.op("act", "activation", [glf], [psA, gkb],
                     glf.ap[:], psA.ap[:], AF.Identity, bias=gkb.ap[:, pc:pc + 1])
                k.op("act", "activation", [t1], [glf], t1.ap[:], glf.ap[:], AF.Abs)
                k.op("act", "activation", [t1], [t1], t1.ap[:], t1.ap[:], AF.Exp,
                     scale=-1.0)
                k.op("act", "activation", [t1], [t1, cb1], t1.ap[:], t1.ap[:], AF.Ln,
                     bias=one_col)
                k.op("dve", "tensor_single_scalar", [t2], [glf], t2.ap[:], glf.ap[:],
                     0.0, ALU.min)
                k.op("dve", "tensor_tensor", [t2], [t2, t1], t2.ap[:], t2.ap[:],
                     t1.ap[:], ALU.subtract)
                k.op("dve", "tensor_scalar", [gf], [t2], gf.ap[:, pc * 512:(pc + 1) * 512],
                     t2.ap[:], 1.0 / GN, None, ALU.mult)
                for c in range(8):
                    k.op("dve", "tensor_tensor_scan", [cumg], [gf],
                         cumg.ap[:, pc * 512 + c * C: pc * 512 + (c + 1) * C],
                         gf.ap[:, pc * 512 + c * C: pc * 512 + (c + 1) * C],
                         gf.ap[:, pc * 512 + c * C: pc * 512 + (c + 1) * C],
                         0.0, ALU.add, ALU.bypass)
                cg = cumg.ap[:, pc * 512:(pc + 1) * 512]
                k.op("dve", "tensor_copy", [dcol], [cumg], dcol.ap[:],
                     cumg.ap[:, pc * 512 + C - 1: (pc + 1) * 512: C])
                k.op("act", "activation", [Dsb], [dcol],
                     Dsb.ap[:, pc * (B * NCH) + b * NCH + tb * 8:
                            pc * (B * NCH) + b * NCH + tb * 8 + 8],
                     dcol.ap[:], AF.Exp)
                k.op("act", "activation", [lam], [cumg], lam.ap[:], cg, AF.Exp)
                k.mm(psA, [(psA.ap[:], wq.ap[:, ic * DK + pc * 128: ic * DK + (pc + 1) * 128],
                            xnf.ap[:, ic * 512:(ic + 1) * 512]) for ic in range(NIC)],
                     [wq, xnf])
                k.op("dve", "scalar_tensor_tensor", [qtf], [psA, lam],
                     qtf.ap[:, pc * 512:(pc + 1) * 512], psA.ap[:], 0.0625,
                     lam.ap[:], ALU.mult, ALU.mult)
                k.mm(psA, [(psA.ap[:], wk_.ap[:, ic * DK + pc * 128: ic * DK + (pc + 1) * 128],
                            xnf.ap[:, ic * 512:(ic + 1) * 512]) for ic in range(NIC)],
                     [wk_, xnf])
                k.op("act", "activation", [enc], [cumg], enc.ap[:], cg, AF.Exp,
                     scale=-1.0)
                k.op("dve", "tensor_tensor", [kif], [psA, enc],
                     kif.ap[:, pc * 512:(pc + 1) * 512], psA.ap[:], enc.ap[:], ALU.mult)
                for c in range(8):
                    k.op("dve", "tensor_scalar", [t1], [cumg],
                         t1.ap[:, c * C:(c + 1) * C], cg[:, c * C:(c + 1) * C],
                         cg[:, c * C + C - 1: c * C + C], None, ALU.subtract)
                k.op("act", "activation", [t1], [t1], t1.ap[:], t1.ap[:], AF.Exp,
                     scale=-1.0)
                k.op("dve", "tensor_tensor", [kef], [psA, t1],
                     kef.ap[:], psA.ap[:], t1.ap[:], ALU.mult)
                for c in range(8):
                    k.transpose(psTb, psTb.ap[0:64, 0:128], kef,
                                kef.ap[:, c * C:(c + 1) * C], ident)
                    k.op("dve", "tensor_copy", [ket], [psTb],
                         ket.ap[0:64, c * DK + pc * 128: c * DK + (pc + 1) * 128],
                         psTb.ap[0:64, 0:128])
            k.dma("sp", [qt_scr], [qtf], qt_scr.ap[gtb], qtf.ap[:])
            k.dma("sp", [ki_scr], [kif], ki_scr.ap[gtb], kif.ap[:])
            k.dma("sp", [khe_scr], [ket],
                  khe_scr.ap[t0:t0 + 512, :].rearrange("(c p) d -> p c d", p=64),
                  ket.ap[:].rearrange("p (c d) -> p c d", d=DK))
    sa.close()
    k.barrier()

    sb_ = ExitStack()
    SBT = 1024
    NSB = T // SBT
    NCB = SBT // C
    NS2 = 2
    qtb = [k.sb(sb_, f"p3b_qt{s}", [128, 2 * SBT], BF16) for s in range(NS2)]
    kib = [k.sb(sb_, f"p3b_ki{s}", [128, 2 * SBT], BF16) for s in range(NS2)]
    keb = [k.sb(sb_, f"p3b_ke{s}", [64, NCB * HK], BF16) for s in range(NS2)]
    vb = [k.sb(sb_, f"p3b_v{s}", [64, NCB * HV], BF16) for s in range(NS2)]
    onb = [k.sb(sb_, f"p3b_on{s}", [64, NCB * HV], BF16) for s in range(NS2)]
    Sm = [k.sb(sb_, f"p3b_Sm{s}", [128, 2 * HV], F32) for s in range(NS2)]
    Sbf = [k.sb(sb_, f"p3b_Sbf{s}", [128, 2 * HV], BF16) for s in range(NS2)]
    stb = [k.sb(sb_, f"p3b_st{s}", [64, 64], BF16) for s in range(NS2)]
    sqs = [k.sb(sb_, f"p3b_sqs{s}", [64, HV], BF16) for s in range(NS2)]
    ssq = [k.sb(sb_, f"p3b_ssq{s}", [64, 1], F32) for s in range(NS2)]
    rso = [k.sb(sb_, f"p3b_rso{s}", [64, 1], F32) for s in range(NS2)]
    psST = [k.psum(sb_, f"p3b_psST{s}", [64, 64], F32) for s in range(NS2)]
    psO = [k.psum(sb_, f"p3b_psO{s}", [64, 512], F32) for s in range(NS2)]
    psU = [k.psum(sb_, f"p3b_psU{s}", [128, 1024], F32) for s in range(NS2)]

    def sb3_load(s, h, b, sb2):
        base_t = b * T + sb2 * SBT
        for i, tbl in enumerate(range(sb2 * 2, sb2 * 2 + 2)):
            gtb = b * NTB + tbl
            k.dma("sp", [qtb[s]], [qt_scr],
                  qtb[s].ap[:].rearrange("p (q t) -> p q t", q=2)[:, :, i * 512:(i + 1) * 512],
                  qt_scr.ap[gtb].rearrange("p (q t) -> p q t", q=NIC)[:, 2 * h:2 * h + 2, :])
            k.dma("sp", [kib[s]], [ki_scr],
                  kib[s].ap[:].rearrange("p (q t) -> p q t", q=2)[:, :, i * 512:(i + 1) * 512],
                  ki_scr.ap[gtb].rearrange("p (q t) -> p q t", q=NIC)[:, 2 * h:2 * h + 2, :])
        k.dma("sp", [keb[s]], [khe_scr],
              keb[s].ap[:].rearrange("p (c d) -> p c d", d=HK),
              khe_scr.ap[base_t:base_t + SBT, h * HK:(h + 1) * HK]
              .rearrange("(c p) d -> p c d", p=64))
        k.dma("sp", [vb[s]], [v_scr],
              vb[s].ap[:].rearrange("p (c d) -> p c d", d=HV),
              v_scr.ap[base_t:base_t + SBT, h * HV:(h + 1) * HV]
              .rearrange("(c p) d -> p c d", p=64))

    def sb3_chunk(s, h, b, sb2, c):
        tof = c * C
        gch = b * NCH + sb2 * NCB + c
        q0 = qtb[s].ap[:, tof:tof + C]
        q1 = qtb[s].ap[:, SBT + tof:SBT + tof + C]
        k0 = kib[s].ap[:, tof:tof + C]
        k1 = kib[s].ap[:, SBT + tof:SBT + tof + C]
        k.mm(psST[s], [(psST[s].ap[:], k0, q0), (psST[s].ap[:], k1, q1)],
             [qtb[s], kib[s]])
        k.op("dve", "select", [stb[s]], [psST[s], triu64, z64],
             stb[s].ap[:], triu64.ap[:], psST[s].ap[:], z64.ap[:],
             add_drain=True)
        k.mm(psO[s], [(psO[s].ap[:], q0, Sbf[s].ap[0:128, 0:HV]),
                      (psO[s].ap[:], q1, Sbf[s].ap[0:128, HV:2 * HV]),
                      (psO[s].ap[:], stb[s].ap[:],
                       vb[s].ap[0:64, c * HV:(c + 1) * HV])],
             [qtb[s], Sbf[s], stb[s], vb[s]])
        k.op("act", "activation", [sqs[s], ssq[s]], [psO[s]],
             sqs[s].ap[:], psO[s].ap[:], AF.Square, accum_out=ssq[s].ap[:])
        k.op("act", "activation", [rso[s]], [ssq[s], k.cpool],
             rso[s].ap[:], ssq[s].ap[:], AF.Ln, bias=eps_col[0:64, :],
             scale=1.0 / HV)
        k.op("act", "activation", [rso[s]], [rso[s]], rso[s].ap[:], rso[s].ap[:],
             AF.Exp, scale=-0.5)
        k.op("dve", "scalar_tensor_tensor", [onb[s]], [psO[s], rso[s], rmsw],
             onb[s].ap[:, c * HV:(c + 1) * HV], psO[s].ap[:], rso[s].ap[:],
             rmsw.ap[0:64, :], ALU.mult, ALU.mult)
        k.mm(psU[s], [(psU[s].ap[:, 0:HV], keb[s].ap[:, c * HK: c * HK + 128],
                       vb[s].ap[0:64, c * HV:(c + 1) * HV])], [keb[s], vb[s]])
        k.mm(psU[s], [(psU[s].ap[:, HV:2 * HV],
                       keb[s].ap[:, c * HK + 128: c * HK + 256],
                       vb[s].ap[0:64, c * HV:(c + 1) * HV])], [keb[s], vb[s]])
        for half in range(2):
            k.op("dve", "scalar_tensor_tensor", [Sm[s]], [Sm[s], Dsb, psU[s]],
                 Sm[s].ap[:, half * HV:(half + 1) * HV],
                 Sm[s].ap[:, half * HV:(half + 1) * HV],
                 Dsb.ap[:, (2 * h + half) * (B * NCH) + gch:
                        (2 * h + half) * (B * NCH) + gch + 1],
                 psU[s].ap[:, half * HV:(half + 1) * HV], ALU.mult, ALU.add)
        k.op("dve", "tensor_copy", [Sbf[s]], [Sm[s]], Sbf[s].ap[:], Sm[s].ap[:])

    def sb3_store(s, h, b, sb2):
        base_t = b * T + sb2 * SBT
        k.dma("sp", [o_scr], [onb[s]],
              o_scr.ap[base_t:base_t + SBT, h * HV:(h + 1) * HV]
              .rearrange("(c p) d -> p c d", p=64),
              onb[s].ap[:].rearrange("p (c d) -> p c d", d=HV))

    for b in range(B):
        for hp in range(HG // 2):
            h0, h1 = 2 * hp, 2 * hp + 1
            for s, h in ((0, h0), (1, h1)):
                k.op("dve", "memset", [Sm[s]], [], Sm[s].ap[:], 0.0)
                k.op("dve", "memset", [Sbf[s]], [], Sbf[s].ap[:], 0.0)
            for sb2 in range(NSB):
                sb3_load(0, h0, b, sb2)
                sb3_load(1, h1, b, sb2)
                for c in range(NCB):
                    sb3_chunk(0, h0, b, sb2, c)
                    sb3_chunk(1, h1, b, sb2, c)
                sb3_store(0, h0, b, sb2)
                sb3_store(1, h1, b, sb2)
    sb_.close()
    k.barrier()

    sc3 = ExitStack()
    wgs = k.sb(sc3, "p3c_wg", [128, NIC * DV], BF16)
    wos = k.sb(sc3, "p3c_wo", [128, (DV // 128) * D], BF16)
    k.dma("sp", [wgs], [g["wg_bf"]], wgs.ap[:], g["wg_bf"].ap[:])
    k.dma("sp", [wos], [g["wo_bf"]], wos.ap[:], g["wo_bf"].ap[:])
    xnfb = k.sb(sc3, "p3c_xnf", [128, NIC * 512], BF16)
    ob = k.sb(sc3, "p3c_ob", [128, 4 * DV], BF16)
    ogt = k.sb(sc3, "p3c_ogt", [128, DV], BF16)
    ogf = k.sb(sc3, "p3c_ogf", [128, (DV // 128) * 512], BF16)
    xres = k.sb(sc3, "p3c_xres", [128, 4 * D], F32)
    out3 = k.sb(sc3, "p3c_out", [128, D], F32)
    psG = k.psum(sc3, "p3c_psG", [128, 1024], F32)
    psY = k.psum(sc3, "p3c_psY", [128, 1024], F32)
    psTb2 = k.psum(sc3, "p3c_psTb", [128, 512], BF16)

    for b in range(B):
        for tb in range(NTB):
            gtb = b * NTB + tb
            t0 = b * T + tb * 512
            k.dma("sp", [xnfb], [xnf_scr], xnfb.ap[:], xnf_scr.ap[gtb])
            k.dma("sp", [ob], [o_scr],
                  ob.ap[:].rearrange("p (f d) -> p f d", d=DV),
                  o_scr.ap[t0:t0 + 512, :].rearrange("(f p) d -> p f d", p=128))
            k.dma("sp", [xres], [g["x2"]],
                  xres.ap[:].rearrange("p (f d) -> p f d", d=D),
                  g["x2"].ap[t0:t0 + 512, :].rearrange("(f p) d -> p f d", p=128))
            for j in range(4):
                for q4 in range(4):
                    half = q4 % 2
                    k.mm(psG, [(psG.ap[:, half * 512:(half + 1) * 512],
                                xnfb.ap[:, ic * 512 + j * 128: ic * 512 + (j + 1) * 128],
                                wgs.ap[:, ic * DV + q4 * 512: ic * DV + (q4 + 1) * 512])
                               for ic in range(NIC)], [wgs, xnfb])
                    if half == 1:
                        for hh in range(2):
                            k.op("act", "activation", [ogt], [psG],
                                 ogt.ap[:, (q4 - 1 + hh) * 512:(q4 + hh) * 512],
                                 psG.ap[:, hh * 512:(hh + 1) * 512], AF.Silu)
                k.op("dve", "tensor_tensor", [ogt], [ogt, ob],
                     ogt.ap[:], ogt.ap[:], ob.ap[:, j * DV:(j + 1) * DV], ALU.mult)
                for ic in range(DV // 128):
                    k.transpose(psTb2, psTb2.ap[0:128, 0:128], ogt,
                                ogt.ap[:, ic * 128:(ic + 1) * 128], ident)
                    k.op("dve", "tensor_copy", [ogf], [psTb2],
                         ogf.ap[:, ic * 512 + j * 128: ic * 512 + (j + 1) * 128],
                         psTb2.ap[0:128, 0:128])
            for j in range(4):
                for half in range(2):
                    k.mm(psY, [(psY.ap[:, half * 512:(half + 1) * 512],
                                ogf.ap[:, ic * 512 + j * 128: ic * 512 + (j + 1) * 128],
                                wos.ap[:, ic * D + half * 512: ic * D + (half + 1) * 512])
                               for ic in range(DV // 128)], [ogf, wos])
                k.op("dve", "tensor_tensor", [out3], [psY, xres],
                     out3.ap[:], psY.ap[:], xres.ap[:, j * D:(j + 1) * D], ALU.add)
                k.dma("sp", [g["x3"]], [out3],
                      g["x3"].ap[t0 + j * 128: t0 + (j + 1) * 128, :], out3.ap[:])
    sc3.close()
    sc.close()
    k.barrier()


def emit_quant(k, e, x_buf, x_ap, w, bufs):
    u, cc, G, sbc = bufs["u"], bufs["cc"], bufs["G"], bufs["sb"]
    sg, cb, gb = bufs["sg"], bufs["cb"], bufs["gb"]
    nb = w // 16
    c3 = cc.ap[:, 0:w].rearrange("p (n c) -> p n c", c=16)
    s3 = sbc.ap[:, 0:w // 2].rearrange("p (n c) -> p n c", c=8)
    k.op("act", "activation", [cc], [x_buf], cc.ap[:, 0:w], x_ap, AF.Abs)
    k.op("dve", "tensor_tensor", [sbc], [cc], s3[:, :, 0:8], c3[:, :, 0:8],
         c3[:, :, 8:16], ALU.max)
    k.op("dve", "tensor_tensor", [sbc], [sbc], s3[:, :, 0:4], s3[:, :, 0:4],
         s3[:, :, 4:8], ALU.max)
    k.op("dve", "tensor_tensor", [sbc], [sbc], s3[:, :, 0:2], s3[:, :, 0:2],
         s3[:, :, 2:4], ALU.max)
    k.op("dve", "tensor_tensor", [sbc], [sbc], s3[:, :, 0:1], s3[:, :, 0:1],
         s3[:, :, 1:2], ALU.max)
    k.op("dve", "tensor_scalar", [sbc], [sbc],
         sbc.ap[:, w // 2: w // 2 + nb], sbc.ap[:, 0:w // 2][:, 0::8],
         1.0 / 6.0, 1e-12, ALU.mult, ALU.max)
    k.op("dve", "reciprocal", [sbc], [sbc],
         sbc.ap[:, w // 2 + nb: w // 2 + 2 * nb],
         sbc.ap[:, w // 2: w // 2 + nb])
    k.op("dve", "tensor_copy", [gb], [sbc], gb.ap[:, 0:nb],
         sbc.ap[:, w // 2: w // 2 + nb])
    s_b = gb.ap[:, 0:nb].rearrange("p (n o) -> p n o", o=1).broadcast_to((128, nb, 16))
    rs_b = sbc.ap[:, w // 2 + nb: w // 2 + 2 * nb].rearrange(
        "p (n o) -> p n o", o=1).broadcast_to((128, nb, 16))
    k.op("dve", "tensor_tensor", [u], [cc, sbc],
         u.ap[:, 0:w].rearrange("p (n c) -> p n c", c=16), c3, rs_b, ALU.mult)
    uap = u.ap[:, 0:w]
    k.op("dve", "tensor_single_scalar", [G], [u], G.ap[:, 0:w], uap, 0.25, ALU.is_gt)
    k.op("dve", "tensor_single_scalar", [cb], [u], cb.ap[:, 0:w], uap, 0.75, ALU.is_gt)
    k.op("dve", "tensor_single_scalar", [sg], [u], sg.ap[:, 0:w], uap, 1.25, ALU.is_gt)
    k.op("dve", "tensor_tensor", [G], [G, cb], G.ap[:, 0:w], G.ap[:, 0:w],
         cb.ap[:, 0:w], ALU.add)
    k.op("dve", "tensor_single_scalar", [cb], [u], cb.ap[:, 0:w], uap, 1.75, ALU.is_gt)
    k.op("dve", "tensor_tensor", [sg], [sg, cb], sg.ap[:, 0:w], sg.ap[:, 0:w],
         cb.ap[:, 0:w], ALU.add)
    k.op("dve", "tensor_tensor", [G], [G, sg], G.ap[:, 0:w], G.ap[:, 0:w],
         sg.ap[:, 0:w], ALU.add)
    k.op("dve", "tensor_scalar", [G], [G], G.ap[:, 0:w], G.ap[:, 0:w], 0.5,
         None, ALU.mult)
    cbA, m25 = k.constcol(-2.5)
    cbB, m35 = k.constcol(-3.5)
    cbC, m50 = k.constcol(-5.0)
    k.op("act", "activation", [sg], [u, cbA], sg.ap[:, 0:w], uap, AF.Sign,
         bias=m25)
    k.op("act", "activation", [cb], [u, cbB], cb.ap[:, 0:w], uap, AF.Sign,
         bias=m35)
    k.op("dve", "tensor_tensor", [sg], [sg, cb], sg.ap[:, 0:w], sg.ap[:, 0:w],
         cb.ap[:, 0:w], ALU.add)
    k.op("act", "activation", [cb], [u, cbC], cb.ap[:, 0:w], uap, AF.Sign,
         bias=m50)
    k.op("dve", "scalar_tensor_tensor", [sg], [sg, cb], sg.ap[:, 0:w],
         sg.ap[:, 0:w], 0.5, cb.ap[:, 0:w], ALU.mult, ALU.add)
    k.op("dve", "tensor_tensor", [G], [G, sg], G.ap[:, 0:w], G.ap[:, 0:w],
         sg.ap[:, 0:w], ALU.add)
    k.op("dve", "tensor_scalar", [G], [G], G.ap[:, 0:w], G.ap[:, 0:w], 2.0,
         None, ALU.add)
    k.op("act", "activation", [cb], [x_buf], cb.ap[:, 0:w], x_ap, AF.Sign)
    k.op("dve", "tensor_tensor", [G], [G, cb], G.ap[:, 0:w], G.ap[:, 0:w],
         cb.ap[:, 0:w], ALU.mult)
    k.op("dve", "tensor_tensor", [bufs["out_buf"]], [G, gb],
         bufs["out_ap"].rearrange("p (n c) -> p n c", c=16),
         G.ap[:, 0:w].rearrange("p (n c) -> p n c", c=16), s_b, ALU.mult)


def build_phase4(k, g, cfg):
    nc = k.nc
    B, T, D = cfg["B"], cfg["T"], cfg["D"]
    I = 2816
    NIC = D // 128
    NHC = I // 128
    NTB = B * T // 512
    HW = I // 2
    sc = ExitStack()

    ident = k.sb(sc, "p4_ident", [128, 128], BF16)
    emit_identity(k, ident)

    wgate = k.sb(sc, "p4_wg", [128, NIC * I], BF16)
    wup = k.sb(sc, "p4_wu", [128, NIC * I], BF16)
    wdown = k.sb(sc, "p4_wd", [128, NHC * D], BF16)

    I4 = 2816
    u = k.sb(sc, "p4_u", [128, I4], F32)
    ccb = k.sb(sc, "p4_cc", [128, I4], F32)
    G = k.sb(sc, "p4_G", [128, I4], BF16)
    sbc = k.sb(sc, "p4_sb", [128, I4 // 2 + 2 * (I4 // 16)], F32)
    sgb = k.sb(sc, "p4_sg", [128, I4], BF16)
    cbb = k.sb(sc, "p4_cb", [128, I4], BF16)
    gbb = k.sb(sc, "p4_gb", [128, I4 // 16], BF16)
    qb = {"u": u, "cc": ccb, "G": G, "sb": sbc, "sg": sgb, "cb": cbb,
          "gb": gbb}

    sa = ExitStack()
    wrow = k.sb(sa, "p4a_wrow", [128, I], F32)
    wq_t = k.sb(sa, "p4a_wqt", [128, I], BF16)
    psTb = k.psum(sa, "p4a_psT", [128, 512], BF16)
    for wsrc, wdst in ((g["w_gate"], wgate), (g["w_up"], wup)):
        for r in range(NHC):
            k.dma("sp", [wrow], [wsrc], wrow.ap[:, 0:D],
                  wsrc.ap[r * 128:(r + 1) * 128, :])
            e = "dve" if r % 2 == 0 else "pool"
            emit_quant(k, e, wrow, wrow.ap[:, 0:D], D,
                       dict(qb, out_buf=wq_t, out_ap=wq_t.ap[:, 0:D]))
            for ic in range(NIC):
                k.transpose(psTb, psTb.ap[0:128, 0:128], wq_t,
                            wq_t.ap[:, ic * 128:(ic + 1) * 128], ident)
                k.op("dve", "tensor_copy", [wdst], [psTb],
                     wdst.ap[:, ic * I + r * 128: ic * I + (r + 1) * 128],
                     psTb.ap[0:128, 0:128])
    for r in range(NIC):
        k.dma("sp", [wrow], [g["w_down"]], wrow.ap[:],
              g["w_down"].ap[r * 128:(r + 1) * 128, :])
        for half in range(2):
            e = "dve" if half == 0 else "pool"
            emit_quant(k, e, wrow, wrow.ap[:, half * HW:(half + 1) * HW], HW,
                       dict(qb, out_buf=wq_t,
                            out_ap=wq_t.ap[:, half * HW:(half + 1) * HW]))
        for hc in range(NHC):
            k.transpose(psTb, psTb.ap[0:128, 0:128], wq_t,
                        wq_t.ap[:, hc * 128:(hc + 1) * 128], ident)
            k.op("dve", "tensor_copy", [wdown], [psTb],
                 wdown.ap[:, hc * D + r * 128: hc * D + (r + 1) * 128],
                 psTb.ap[0:128, 0:128])
    sa.close()
    k.barrier()

    sb_ = ExitStack()
    xin = k.sb(sb_, "p4_xin", [128, D], F32)
    xqf = k.sb(sb_, "p4_xqf", [128, NIC * 128], BF16)
    hh = k.sb(sb_, "p4_hh", [128, I], F32)
    hqf = k.sb(sb_, "p4_hqf", [128, 22 * 128], BF16)
    stats = k.sb(sb_, "p4_stats", [128, 12], F32)
    mv = k.sb(sb_, "p4_mv", [128, 2], F32)
    rstd = k.sb(sb_, "p4_rstd", [128, 1], F32)
    out4 = k.sb(sb_, "p4_out", [128, D], F32)
    psG = k.psum(sb_, "p4_psG", [128, 1408], F32)
    psY = k.psum(sb_, "p4_psY", [128, 1024], F32)
    psT2 = k.psum(sb_, "p4_psT2", [128, 512], BF16)

    for tb in range(NTB):
        for j in range(4):
            ti = tb * 4 + j
            t0 = ti * 128
            k.dma("sp", [xin], [g["x3"]], xin.ap[:], g["x3"].ap[t0:t0 + 128, :])
            mu_ap, rstd_ap = emit_ln(k, mv, stats, rstd, xin, xin.ap[:], D)
            k.op("dve", "tensor_scalar", [hh], [xin, mv, rstd],
                 hh.ap[:, 0:D], xin.ap[:], mu_ap, rstd_ap, ALU.subtract, ALU.mult)
            emit_quant(k, "dve", hh, hh.ap[:, 0:D], D,
                       dict(qb, out_buf=sgb, out_ap=sgb.ap[:, 0:D]))
            for ic in range(NIC):
                k.transpose(psT2, psT2.ap[0:128, 0:128], sgb,
                            sgb.ap[:, ic * 128:(ic + 1) * 128], ident)
                k.op("dve", "tensor_copy", [xqf], [psT2],
                     xqf.ap[:, ic * 128:(ic + 1) * 128],
                     psT2.ap[0:128, 0:128])
            for half in range(2):
                cols = [(0, 512), (512, 1024), (1024, 1408)]
                for wsb, is_up in ((wgate, False), (wup, True)):
                    for (c0, c1) in cols:
                        k.mm(psG, [(psG.ap[:, c0:c1],
                                    xqf.ap[:, ic * 128:(ic + 1) * 128],
                                    wsb.ap[:, ic * I + half * HW + c0:
                                           ic * I + half * HW + c1])
                                   for ic in range(NIC)], [wsb, xqf])
                    if not is_up:
                        for (c0, c1) in cols:
                            k.op("act", "activation", [hh], [psG],
                                 hh.ap[:, half * HW + c0: half * HW + c1],
                                 psG.ap[:, c0:c1], AF.Silu)
                    else:
                        k.op("dve", "tensor_tensor", [hh], [psG, hh],
                             hh.ap[:, half * HW:(half + 1) * HW], psG.ap[:],
                             hh.ap[:, half * HW:(half + 1) * HW], ALU.mult)
            emit_quant(k, "dve", hh, hh.ap[:], I,
                       dict(qb, out_buf=sgb, out_ap=sgb.ap[:, 0:I]))
            for hc in range(22):
                pT = psT2
                k.transpose(pT, pT.ap[0:128, 0:128], sgb,
                            sgb.ap[:, hc * 128:(hc + 1) * 128], ident)
                k.op("dve", "tensor_copy", [hqf], [pT],
                     hqf.ap[:, hc * 128:(hc + 1) * 128],
                     pT.ap[0:128, 0:128])
            for colh in range(2):
                k.mm(psY, [(psY.ap[:, colh * 512:(colh + 1) * 512],
                            hqf.ap[:, hc2 * 128:(hc2 + 1) * 128],
                            wdown.ap[:, hc2 * D + colh * 512:
                                     hc2 * D + (colh + 1) * 512])
                           for hc2 in range(22)],
                     [hqf, wdown])
            k.dma("sp", [xin], [g["x3"]], xin.ap[:], g["x3"].ap[t0:t0 + 128, :])
            k.op("dve", "tensor_tensor", [out4], [psY, xin],
                 out4.ap[:], psY.ap[:], xin.ap[:], ALU.add)
            k.dma("sp", [g["y"]], [out4], g["y"].ap[t0:t0 + 128, :], out4.ap[:])
    sb_.close()
    sc.close()
    k.barrier()


def layout_l0_attn(w_kqv, w_proj):
    w = np.asarray(w_kqv).reshape(16, 64, 3, 1024)
    wk = w[:, :, 0, :].reshape(1024, 1024)
    wq = w[:, :, 1, :].reshape(1024, 1024)
    wv = w[:, :, 2, :].reshape(1024, 1024)
    wkqv_fm = np.concatenate([wk.T, wq.T, wv.T], axis=1)
    wproj_fm = np.ascontiguousarray(np.asarray(w_proj).T)
    return np.ascontiguousarray(wkqv_fm), wproj_fm


def sbready(w_fm):
    K, N = w_fm.shape
    return np.ascontiguousarray(
        np.asarray(w_fm).reshape(K // 128, 128, N).transpose(1, 0, 2)
        .reshape(128, (K // 128) * N))


CFG = {"B": 4, "T": 2048, "D": 1024}
FUSED = True
NCORES = 8

_P1_W = [("wkqv_fm", [128, 8 * 3072]), ("wproj_fm", [128, 8 * 1024]),
         ("b_proj", [1, 1024])]
_P2_W = [("Wr", [1024, 8]), ("br", [1, 8]), ("ln_g", [1, 1024]), ("ln_b", [1, 1024]),
         ("W1", [8 * 128, 8 * 4096]), ("b1", [8, 4096]), ("W2", [8 * 128, 32 * 1024]),
         ("b2", [8, 1024])]
_P3_W = [("Wq", [128, 8 * 1024]), ("Wk", [128, 8 * 1024]), ("Wv", [128, 8 * 2048]),
         ("Wg", [128, 8 * 2048]), ("gk_w1", [1024, 16]), ("gk_w2", [16, 1024]),
         ("gk_b", [1, 1024]), ("Wo", [128, 16 * 1024]), ("rms_w", [1, 512])]
_P4_W = [("w_gate", [2816, 1024]), ("w_up", [2816, 1024]), ("w_down", [1024, 2816])]


def _weights_from_inputs(inputs):
    wkqv_fm, wproj_fm = layout_l0_attn(inputs["L0_attn_w_kqv"], inputs["L0_attn_w_proj"])
    r = lambda a: np.ascontiguousarray(np.asarray(a, np.float32).reshape(1, -1))
    c = lambda a: np.ascontiguousarray(np.asarray(a, np.float32))
    W1 = np.asarray(inputs["L0_ffn_W1"], np.float32)
    W2 = np.asarray(inputs["L0_ffn_W2"], np.float32)
    w1r = np.ascontiguousarray(
        W1.reshape(8, 8, 128, 4096).transpose(0, 2, 1, 3).reshape(8 * 128, 8 * 4096))
    w2r = np.ascontiguousarray(
        W2.reshape(8, 32, 128, 1024).transpose(0, 2, 1, 3).reshape(8 * 128, 32 * 1024))
    return {
        "wkqv_fm": sbready(wkqv_fm.astype(np.float32)),
        "wproj_fm": sbready(wproj_fm.astype(np.float32)),
        "b_proj": r(inputs["L0_attn_b_proj"]),
        "Wr": c(inputs["L0_ffn_Wr"]), "br": r(inputs["L0_ffn_br"]),
        "ln_g": r(inputs["L0_ffn_ln_g"]), "ln_b": r(inputs["L0_ffn_ln_b"]),
        "W1": w1r, "b1": c(inputs["L0_ffn_b1"]),
        "W2": w2r, "b2": c(inputs["L0_ffn_b2"]),
        "Wq": sbready(c(inputs["L1_attn_Wq"])), "Wk": sbready(c(inputs["L1_attn_Wk"])),
        "Wv": sbready(c(inputs["L1_attn_Wv"])), "Wg": sbready(c(inputs["L1_attn_Wg"])),
        "gk_w1": c(inputs["L1_attn_gk_w1"]), "gk_w2": c(inputs["L1_attn_gk_w2"]),
        "gk_b": r(inputs["L1_attn_gk_b"]), "Wo": sbready(c(inputs["L1_attn_Wo"])),
        "rms_w": r(inputs["L1_attn_rms_w"]),
        "w_gate": c(inputs["L1_ffn_w_gate"]), "w_up": c(inputs["L1_ffn_w_up"]),
        "w_down": c(inputs["L1_ffn_w_down"]),
    }


def _run_phase(build_fn, wspecs, in_name, out_name, xs, wmap):
    NT = CFG["B"] * CFG["T"]
    nc = bass.Bass("TRN2", target_bir_lowering=False)
    k = K(nc)
    g = {in_name: k.dram(in_name, [NT, CFG["D"]], F32, kind="ExternalInput"),
         out_name: k.dram(out_name, [NT, CFG["D"]], F32, kind="ExternalOutput")}
    for n, s in wspecs:
        g[n] = k.dram(n, s, F32, kind="ExternalInput")
    build_fn(k, g, CFG)
    k.finish_program()
    k.es.close()
    nc.finalize()
    in_maps = []
    for c in range(NCORES):
        m = {in_name: xs[c]}
        for n, _ in wspecs:
            m[n] = wmap[n]
        in_maps.append(m)
    res = run_bass_kernel_spmd(nc, in_maps, core_ids=list(range(NCORES)))
    return [r[out_name] for r in res.results]


def _run_fused(xs, wmap):
    NT = CFG["B"] * CFG["T"]
    D = CFG["D"]
    nc = bass.Bass("TRN2", target_bir_lowering=False)
    k = K(nc)
    g = {"x0": k.dram("x0", [NT, D], F32, kind="ExternalInput"),
         "y": k.dram("y", [NT, D], F32, kind="ExternalOutput"),
         "x1": k.dram("x1", [NT, D], F32, kind="Internal"),
         "x2": k.dram("x2", [NT, D], F32, kind="Internal"),
         "x3": k.dram("x3", [NT, D], F32, kind="Internal")}
    for n, s in _P1_W + _P2_W + _P3_W + _P4_W:
        g[n] = k.dram(n, s, F32, kind="ExternalInput")
    build_phase0(k, g, CFG)
    build_phase1(k, g, CFG)
    build_phase2(k, g, CFG)
    build_phase3(k, g, CFG)
    build_phase4(k, g, CFG)
    k.finish_program()
    k.es.close()
    nc.finalize()
    in_maps = []
    for c in range(NCORES):
        m = {"x0": xs[c]}
        for n, _ in _P1_W + _P2_W + _P3_W + _P4_W:
            m[n] = wmap[n]
        in_maps.append(m)
    res = run_bass_kernel_spmd(nc, in_maps, core_ids=list(range(NCORES)))
    return [r["y"] for r in res.results]


def kernel(**inputs):
    x = np.asarray(inputs["L0_attn_x"], np.float32)
    D = CFG["D"]
    xs = [np.ascontiguousarray(x[4 * c:4 * c + 4].reshape(-1, D)) for c in range(NCORES)]
    wmap = _weights_from_inputs(inputs)
    ys = _run_fused(xs, wmap)
    out = np.concatenate([y.reshape(4, CFG["T"], D) for y in ys], axis=0)
    return out.astype(np.float32)


# version 20
# speedup vs baseline: 1.1861x; 1.1861x over previous
import math
from contextlib import ExitStack

import numpy as np
import concourse.bass as bass
import concourse.mybir as mybir
from concourse.bass_utils import run_bass_kernel_spmd

F32 = mybir.dt.float32
BF16 = mybir.dt.bfloat16
AF = mybir.ActivationFunctionType
ALU = mybir.AluOpType
AX = mybir.AxisListType
U8 = mybir.dt.uint8
U32 = mybir.dt.uint32
I32 = mybir.dt.int32


class Buf:
    __slots__ = ("ap", "writes", "reads", "name", "dsem", "dqueue")

    def __init__(self, ap, name=""):
        self.ap = ap
        self.writes = {}
        self.reads = {}
        self.name = name
        self.dsem = None
        self.dqueue = None


class K:
    def __init__(self, nc):
        self.nc = nc
        self.eng = {
            "pe": nc.tensor, "dve": nc.vector, "act": nc.scalar,
            "pool": nc.gpsimd, "sp": nc.sync,
        }
        self.es = ExitStack()
        self.sem = {}
        self.cnt = {}
        for t in ("pe", "dve", "act", "pool"):
            self.sem[t] = self.es.enter_context(nc.semaphore(t))
            self.cnt[t] = 0
        self.memo = {}
        self.drained = {}
        t = self.es.enter_context(nc.sbuf_tensor("k_cpool", [128, 16], F32))
        self.cpool = Buf(t.ap(), "k_cpool")
        self.cvals = {}

    def _dsem_for(self, buf, queue):
        if buf.dsem is None:
            tag = "d_" + buf.name
            buf.dsem = tag
            buf.dqueue = queue
            self.sem[tag] = self.es.enter_context(self.nc.semaphore(tag))
            self.cnt[tag] = 0
        else:
            assert buf.dqueue == queue, \
                f"buffer {buf.name} DMA'd from both HWDGE and SWDGE queues"
        return buf.dsem

    def constcol(self, val):
        if val not in self.cvals:
            i = len(self.cvals)
            self.cvals[val] = i
            self.op("pool", "memset", [self.cpool], [],
                    self.cpool.ap[:, i:i + 1], float(val))
        i = self.cvals[val]
        return self.cpool, self.cpool.ap[:, i:i + 1]

    def sb(self, scope, name, shape, dtype):
        t = scope.enter_context(self.nc.sbuf_tensor(name, shape, dtype))
        return Buf(t.ap(), name)

    def psum(self, scope, name, shape, dtype=F32):
        t = scope.enter_context(self.nc.psum_tensor(name, shape, dtype))
        return Buf(t.ap(), name)

    def dram(self, name, shape, dtype, kind="Internal"):
        t = self.nc.dram_tensor(name, shape, dtype, kind=kind)
        return Buf(t.ap(), name)

    def _wait_deps(self, etag, outs, ins, skip_tag=None):
        deps = {}
        for b in ins:
            for t, v in b.writes.items():
                if deps.get(t, 0) < v:
                    deps[t] = v
        for b in outs:
            for src in (b.writes, b.reads):
                for t, v in src.items():
                    if t == skip_tag:
                        continue
                    if deps.get(t, 0) < v:
                        deps[t] = v
        eng = self.eng[etag]
        need_drain = False
        for t, v in deps.items():
            if t == etag:
                if etag == "pe":
                    continue
                if self.drained.get(etag, 0) < v:
                    need_drain = True
            else:
                if t not in ("pe", "dve", "act", "pool"):
                    v = self.cnt[t]
                if self.memo.get((etag, t), 0) < v:
                    eng.wait_ge(self.sem[t], v)
                    self.memo[(etag, t)] = v
        if need_drain:
            eng.drain()
            self.drained[etag] = self.cnt[etag]

    def _finish(self, etag, inst, outs, ins, tag, step):
        self.cnt[tag] += step
        inst.then_inc(self.sem[tag], step)
        tok_v = self.cnt[tag]
        for b in outs:
            if step == 16 and tag == b.dsem:
                b.writes[tag] = tok_v
            else:
                b.writes = {tag: tok_v}
            b.reads = {}
        for b in ins:
            b.reads[tag] = tok_v
        return (tag, tok_v)

    def barrier(self):
        for etag in ("pe", "dve", "act", "pool", "sp"):
            eng = self.eng[etag]
            for t in self.sem:
                v = self.cnt[t]
                if v and t != etag and self.memo.get((etag, t), 0) < v:
                    eng.wait_ge(self.sem[t], v)
                    self.memo[(etag, t)] = v
            if etag in ("dve", "act", "pool"):
                eng.drain()
                self.drained[etag] = self.cnt[etag]

    def finish_program(self):
        for t in list(self.sem):
            v = self.cnt[t]
            if v and t != "sp" and self.memo.get(("sp", t), 0) < v:
                self.eng["sp"].wait_ge(self.sem[t], v)
                self.memo[("sp", t)] = v

    def op(self, etag, name, outs, ins, *args, **kw):
        self._wait_deps(etag, outs, ins)
        inst = getattr(self.eng[etag], name)(*args, **kw)
        return self._finish(etag, inst, outs, ins, etag, 1)

    def idma(self, outs, ins, out_ap, out_off, in_ap, in_off, skip_waw=True, **kw):
        from concourse.bass import IndirectOffsetOnAxis
        tag = self._dsem_for(outs[0], "sw")
        self._wait_deps("pool", outs, ins, skip_tag=tag if skip_waw else None)
        inst = self.eng["pool"].indirect_dma_start(
            out_ap,
            IndirectOffsetOnAxis(out_off, 0) if out_off is not None else None,
            in_ap,
            IndirectOffsetOnAxis(in_off, 0) if in_off is not None else None,
            **kw)
        return self._finish("pool", inst, outs, ins, tag, 16)

    def dma(self, etag, outs, ins, out_ap, in_ap, **kw):
        queue = "hw" if etag in ("sp", "act") else "sw"
        tag = self._dsem_for(outs[0], queue)
        self._wait_deps(etag, outs, ins, skip_tag=tag)
        inst = self.eng[etag].dma_start(out=out_ap, in_=in_ap, **kw)
        return self._finish(etag, inst, outs, ins, tag, 16)

    def mm(self, psum_buf, items, ins, accum=False, stop=True):
        self._wait_deps("pe", [psum_buf], ins)
        n = len(items)
        inst = None
        for i, (ps_ap, lhsT, rhs) in enumerate(items):
            inst = self.nc.tensor.matmul(
                ps_ap, lhsT, rhs,
                start=(i == 0 and not accum), stop=(stop and i == n - 1))
        return self._finish("pe", inst, [psum_buf], ins, "pe", 1)

    def transpose(self, psum_buf, ps_ap, in_buf, in_ap, ident):
        kdim = in_ap.shape[0]
        self._wait_deps("pe", [psum_buf], [in_buf, ident])
        inst = self.nc.tensor.transpose(ps_ap, in_ap, ident.ap[0:kdim, 0:kdim])
        return self._finish("pe", inst, [psum_buf], [in_buf, ident], "pe", 1)


def emit_bcast_row(k, out_buf, row_buf, ps_buf, ones1, D):
    for c in range(0, D, 512):
        w = min(512, D - c)
        k.mm(ps_buf, [(ps_buf.ap[:, 0:w], ones1.ap[:], row_buf.ap[:, c:c + w])],
             [row_buf, ones1])
        k.op("dve", "tensor_copy", [out_buf], [ps_buf],
             out_buf.ap[:, c:c + w], ps_buf.ap[:, 0:w])


def emit_identity(k, ident, n=128):
    k.op("pool", "memset", [ident], [], ident.ap[:], 0.0)
    k.op("pool", "affine_select", [ident], [ident],
         out=ident.ap[:], in_=ident.ap[:], compare_op=ALU.not_equal,
         fill=1.0, base=0, pattern=[[-1, n]], channel_multiplier=1)


def emit_tril(k, buf, n=128, val=1.0):
    k.op("pool", "memset", [buf], [], buf.ap[:], val)
    k.op("pool", "affine_select", [buf], [buf],
         out=buf.ap[:], in_=buf.ap[:], compare_op=ALU.is_ge,
         fill=0.0, base=0, pattern=[[-1, n]], channel_multiplier=1)


def emit_ln(k, mv, stats, rstd, xin_buf, x_ap, D, eps=1e-5):
    nch = (D + 511) // 512
    for c in range(nch):
        k.op("dve", "bn_stats", [stats], [xin_buf],
             stats.ap[:, 6 * c:6 * c + 6], x_ap[:, 512 * c:512 * (c + 1)])
    k.op("dve", "bn_aggr", [mv], [stats], mv.ap[:], stats.ap[:, 0:6 * nch])
    cb, eps_ap = k.constcol(eps)
    k.op("act", "activation", [rstd], [mv, cb], rstd.ap[:], mv.ap[:, 1:2], AF.Ln, bias=eps_ap)
    k.op("act", "activation", [rstd], [rstd], rstd.ap[:], rstd.ap[:], AF.Exp, scale=-0.5)
    return mv.ap[:, 0:1], rstd.ap[:]


P0_ITEMS = [("wkqv_fm", "wkqv_bf"), ("wproj_fm", "wproj_bf"),
            ("Wq", "wq_bf"), ("Wk", "wk_bf"), ("Wv", "wv_bf"),
            ("Wg", "wg_bf"), ("Wo", "wo_bf")]


def build_phase0(k, g, cfg, items=None, moe=True):
    items = items if items is not None else P0_ITEMS
    sc = ExitStack()
    CH = 8192
    bi = [k.sb(sc, f"p0_in{i}", [128, CH], F32) for i in range(2)]
    bo = [k.sb(sc, f"p0_out{i}", [128, CH], BF16) for i in range(2)]
    n = 0

    def cast_stream(pairs):
        nonlocal n
        for (sap, srcb, db, r0, c0, w) in pairs:
            i = n % 2
            k.dma("sp", [bi[i]], [srcb], bi[i].ap[:, 0:w], sap)
            k.op("dve", "tensor_copy", [bo[i]], [bi[i]], bo[i].ap[:, 0:w],
                 bi[i].ap[:, 0:w])
            k.dma("act", [db], [bo[i]], db.ap[r0:r0 + 128, c0:c0 + w],
                 bo[i].ap[:, 0:w])
            n += 1

    for srcn, dstn in items:
        sb = g[srcn]
        P, X = sb.ap.shape
        db = k.dram(dstn, [P, X], BF16)
        g[dstn] = db
        pairs = []
        for r0 in range(0, P, 128):
            for c0 in range(0, X, CH):
                w = min(CH, X - c0)
                pairs.append((sb.ap[r0:r0 + 128, c0:c0 + w], sb, db, r0, c0, w))
        cast_stream(pairs)
    if moe:
        w1v = g["W1"].ap[:].rearrange("r (c f) -> r c f", f=4096)
        w2v = g["W2"].ap[:].rearrange("r (c d) -> r c d", d=1024)
        for half in range(2):
            d1 = k.dram(f"w1{'ab'[half]}_bf", [1024, 8 * 2048], BF16)
            g[f"w1{'ab'[half]}_bf"] = d1
            pairs = []
            for r0 in range(0, 1024, 128):
                for ic in range(8):
                    pairs.append((w1v[r0:r0 + 128, ic, half * 2048:(half + 1) * 2048],
                                  g["W1"], d1, r0, ic * 2048, 2048))
            cast_stream(pairs)
        for half in range(2):
            d2 = k.dram(f"w2{'ab'[half]}_bf", [1024, 32 * 512], BF16)
            g[f"w2{'ab'[half]}_bf"] = d2
            pairs = []
            for r0 in range(0, 1024, 128):
                for hc0 in range(0, 32, 4):
                    sap = w2v[r0:r0 + 128, hc0:hc0 + 4, half * 512:(half + 1) * 512]
                    pairs.append((sap, g["W2"], d2, r0, hc0 * 512, 2048))
            cast_stream(pairs)
    sc.close()
    k.barrier()


def build_phase1(k, g, cfg):
    nc = k.nc
    B, T, D = cfg["B"], cfg["T"], cfg["D"]
    H, Dh = 16, 64
    NIC = D // 128
    NTB = T // 512
    sc = ExitStack()

    cb20, neg20 = k.constcol(-20.0)
    ident = k.sb(sc, "p1_ident", [128, 128], BF16)
    emit_identity(k, ident)
    triu = k.sb(sc, "p1_triu", [128, 128], BF16)
    k.op("pool", "memset", [triu], [], triu.ap[:], 1.0)
    k.op("pool", "affine_select", [triu], [triu],
         out=triu.ap[:], in_=triu.ap[:], compare_op=ALU.is_ge,
         fill=0.0, base=0, pattern=[[1, 128]], channel_multiplier=-1)

    wkqv = k.sb(sc, "p1_wkqv", [128, NIC * 3 * D], BF16)
    k.dma("sp", [wkqv], [g["wkqv_bf"]], wkqv.ap[:], g["wkqv_bf"].ap[:])
    wproj = k.sb(sc, "p1_wproj", [128, NIC * D], BF16)
    k.dma("sp", [wproj], [g["wproj_bf"]], wproj.ap[:], g["wproj_bf"].ap[:])
    bproj = k.sb(sc, "p1_bproj", [128, D], F32)

    qf = k.sb(sc, "p1_q", [128, NIC * T], BF16)
    kf = k.sb(sc, "p1_k", [128, NIC * T], BF16)
    vt = k.sb(sc, "p1_v", [128, (T // 128) * D], BF16)
    xin = k.sb(sc, "p1_xin", [128, D], F32)
    xnt = k.sb(sc, "p1_xnt", [128, D], BF16)
    xnf = k.sb(sc, "p1_xnf", [128, NIC * 512], BF16)
    stats = k.sb(sc, "p1_stats", [128, 12], F32)
    mv = k.sb(sc, "p1_mv", [128, 2], F32)
    rstd = k.sb(sc, "p1_rstd", [128, 1], F32)
    ctxf = xnf
    lparts = k.sb(sc, "p1_lparts", [128, 4 * 8], F32)
    ptb = k.sb(sc, "p1_ptb", [128, 16 * 512], BF16)
    lr = k.sb(sc, "p1_lr", [1, 2 * 512], F32)
    linv64 = k.sb(sc, "p1_linv64", [128, 512], F32)
    ones128b = k.sb(sc, "p1_o128", [128, 1], BF16)
    k.op("dve", "memset", [ones128b], [], ones128b.ap[:], 1.0)
    ones1f = k.sb(sc, "p1_o1f", [1, 128], F32)
    k.op("dve", "memset", [ones1f], [], ones1f.ap[:], 1.0)
    res = k.sb(sc, "p1_res", [128, D], F32)

    psA = [k.psum(sc, f"p1_psA{i}", [128, 512], F32) for i in range(2)]
    psC = k.psum(sc, "p1_psC", [128, 1024], F32)
    _psT0 = k.psum(sc, "p1_psT0", [128, 512], BF16)
    psT = [_psT0, _psT0]
    psO = k.psum(sc, "p1_psO", [128, 512], F32)
    psL = k.psum(sc, "p1_psL", [1, 512], F32)
    psLB = k.psum(sc, "p1_psLB", [128, 512], F32)

    sub = ExitStack()
    ones1 = k.sb(sub, "p1_ones1", [1, 128], F32)
    k.dma("sp", [res], [g["b_proj"]], res.ap[0:1, :], g["b_proj"].ap[0:1, :])
    k.op("dve", "memset", [ones1], [], ones1.ap[:], 1.0)
    for c in range(0, D, 512):
        k.mm(psA[0], [(psA[0].ap[:, 0:512], ones1.ap[:], res.ap[0:1, c:c + 512])],
             [res, ones1])
        k.op("dve", "tensor_copy", [bproj], [psA[0]],
             bproj.ap[:, c:c + 512], psA[0].ap[:, 0:512])
    k.barrier()
    sub.close()

    for b in range(B):
        for tb in range(NTB):
            t0 = b * T + tb * 512
            for j in range(4):
                k.dma("sp", [xin], [g["x0"]], xin.ap[:],
                      g["x0"].ap[t0 + j * 128: t0 + (j + 1) * 128, :])
                mu_ap, rstd_ap = emit_ln(k, mv, stats, rstd, xin, xin.ap[:], D)
                k.op("dve", "tensor_scalar", [xnt], [xin, mv, rstd],
                     xnt.ap[:], xin.ap[:], mu_ap, rstd_ap,
                     ALU.subtract, ALU.mult)
                for ic in range(NIC):
                    pT = psT[ic % 2]
                    k.transpose(pT, pT.ap[0:128, 0:128], xnt,
                                xnt.ap[:, ic * 128:(ic + 1) * 128], ident)
                    k.op("dve", "tensor_copy", [xnf], [pT],
                         xnf.ap[:, ic * 512 + j * 128: ic * 512 + (j + 1) * 128],
                         pT.ap[0:128, 0:128])
            for oc in range(2 * NIC):
                dst = kf if oc < NIC else qf
                oc_l = oc % NIC
                pA = psA[oc % 2]
                items = [(pA.ap[:],
                          wkqv.ap[:, ic * 3 * D + oc * 128: ic * 3 * D + (oc + 1) * 128],
                          xnf.ap[:, ic * 512:(ic + 1) * 512]) for ic in range(NIC)]
                k.mm(pA, items, [wkqv, xnf])
                k.op("dve", "tensor_copy", [dst], [pA],
                     dst.ap[:, oc_l * T + tb * 512: oc_l * T + (tb + 1) * 512], pA.ap[:])
            for j in range(4):
                for h2 in range(2):
                    items = [(psC.ap[:, h2 * 512:(h2 + 1) * 512],
                              xnf.ap[:, ic * 512 + j * 128: ic * 512 + (j + 1) * 128],
                              wkqv.ap[:, ic * 3 * D + 2 * D + h2 * 512:
                                      ic * 3 * D + 2 * D + (h2 + 1) * 512])
                             for ic in range(NIC)]
                    k.mm(psC, items, [wkqv, xnf], accum=(False))
                tt = tb * 4 + j
                k.op("dve", "tensor_copy", [vt], [psC],
                     vt.ap[:, tt * D:(tt + 1) * D], psC.ap[:])

        for qb in range(NTB):
            nkb = qb + 1
            ntile = 4 * nkb
            for h in range(H):
                hc, hr = h // 2, (h % 2) * 64
                for tkg in range(ntile):
                    diag = (tkg // 4 == qb)
                    c0 = (tkg % 4) * 128 if diag else 0
                    pA = psA[tkg % 2]
                    k.mm(pA, [(pA.ap[:, c0:512],
                               kf.ap[hr:hr + 64, hc * T + tkg * 128:
                                     hc * T + tkg * 128 + 128],
                               qf.ap[hr:hr + 64, hc * T + qb * 512 + c0:
                                     hc * T + qb * 512 + 512])],
                         [qf, kf])
                    if c0 > 0:
                        k.op("dve", "memset", [ptb], [],
                             ptb.ap[:, tkg * 512: tkg * 512 + c0], 0.0)
                    k.op("act", "activation", [ptb], [pA, cb20],
                         ptb.ap[:, tkg * 512 + c0: (tkg + 1) * 512],
                         pA.ap[:, c0:512], AF.Exp, scale=0.125, bias=neg20)
                    if diag:
                        k.op("dve", "tensor_tensor", [ptb], [ptb, triu],
                             ptb.ap[:, tkg * 512 + c0: tkg * 512 + c0 + 128],
                             ptb.ap[:, tkg * 512 + c0: tkg * 512 + c0 + 128],
                             triu.ap[:], ALU.mult)
                for tkg in range(ntile):
                    k.mm(psL, [(psL.ap[:], ones128b.ap[:],
                                ptb.ap[:, tkg * 512:(tkg + 1) * 512])],
                         [ptb, ones128b], accum=(tkg > 0), stop=(tkg == ntile - 1))
                hp = h % 2
                k.op("dve", "tensor_copy", [lr], [psL],
                     lr.ap[:, 0:512], psL.ap[:])
                for tkg in range(ntile):
                    k.mm(psO, [(psO.ap[hr:hr + 64, :],
                                vt.ap[:, tkg * D + h * Dh:
                                      tkg * D + (h + 1) * Dh],
                                ptb.ap[:, tkg * 512:(tkg + 1) * 512])],
                         [vt, ptb], accum=(tkg > 0), stop=(tkg == ntile - 1))
                k.op("act", "activation", [lr], [lr], lr.ap[:, 512:1024],
                     lr.ap[:, 0:512], AF.Ln)
                k.op("act", "activation", [lr], [lr], lr.ap[:, 512:1024],
                     lr.ap[:, 512:1024], AF.Exp, scale=-1.0)
                k.mm(psLB, [(psLB.ap[hr:hr + 64, :], ones1f.ap[:, 0:64],
                             lr.ap[:, 512:1024])], [lr, ones1f])
                if hp == 1:
                    k.op("dve", "tensor_copy", [linv64], [psLB],
                         linv64.ap[:], psLB.ap[:])
                    k.op("dve", "tensor_tensor", [ctxf], [psO, linv64],
                         ctxf.ap[:, hc * 512:(hc + 1) * 512],
                         psO.ap[:], linv64.ap[:], ALU.mult)
            t0 = b * T + qb * 512
            for j in range(4):
                for half in range(2):
                    items = [(psC.ap[:, half * 512:(half + 1) * 512],
                              ctxf.ap[:, ic * 512 + j * 128: ic * 512 + (j + 1) * 128],
                              wproj.ap[:, ic * D + half * 512: ic * D + (half + 1) * 512])
                             for ic in range(NIC)]
                    k.mm(psC, items, [ctxf, wproj])
                k.dma("sp", [xin], [g["x0"]], xin.ap[:],
                      g["x0"].ap[t0 + j * 128: t0 + (j + 1) * 128, :])
                k.op("dve", "tensor_tensor", [res], [psC, bproj],
                     res.ap[:], psC.ap[:], bproj.ap[:], ALU.add)
                k.op("dve", "tensor_tensor", [res], [res, xin],
                     res.ap[:], res.ap[:], xin.ap[:], ALU.add)
                k.dma("sp", [g["x1"]], [res],
                      g["x1"].ap[t0 + j * 128: t0 + (j + 1) * 128, :], res.ap[:])
    sc.close()
    k.barrier()


def build_phase2(k, g, cfg):
    nc = k.nc
    B, T, D = cfg["B"], cfg["T"], cfg["D"]
    E, Fh = 8, 4096
    NIC = D // 128
    NOC = Fh // 128
    NTOK = B * T
    NTB = NTOK // 512
    NT1 = NTOK // 512 + E
    NSLOT = NT1 * 512
    OOBV = NTOK
    sc = ExitStack()

    ident = k.sb(sc, "p2_ident", [128, 128], BF16)
    emit_identity(k, ident)

    xn2tok = k.dram("p2_xn2tok", [NTOK, D], BF16)
    ysel = k.dram("p2_ysel", [NTOK, D], F32)
    yscr = k.dram("p2_yscr", [NTOK, D], F32)
    tokof_d = k.dram("p2_tokof", [NSLOT, 1], U32)

    wr = k.sb(sc, "p2_wr", [128, NIC * E], F32)
    k.dma("sp", [wr], [g["Wr"]], wr.ap[:].rearrange("p (c e) -> p c e", e=E),
          g["Wr"].ap[:].rearrange("(c p) e -> p c e", p=128))
    brow = k.sb(sc, "p2_br", [E, 1], F32)
    k.dma("sp", [brow], [g["br"]], brow.ap[:], g["br"].ap[0:1, :].rearrange("o e -> e o"))
    maskf = k.sb(sc, "p2_maskf", [E, NTOK], BF16)
    gates = k.sb(sc, "p2_gates", [128, NTOK // 128], F32)
    wofs = k.sb(sc, "p2_wofs", [128, NT1], U32)
    b1ofs = k.sb(sc, "p2_b1ofs", [128, NT1], U32)

    sa = ExitStack()
    ones1 = k.sb(sa, "p2_ones1", [1, 128], F32)
    k.op("dve", "memset", [ones1], [], ones1.ap[:], 1.0)
    b2sb = k.sb(sa, "p2_b2", [E, D], BF16)
    k.dma("pool", [b2sb], [g["b2"]], b2sb.ap[:], g["b2"].ap[:])
    lngrow = k.sb(sa, "p2_lngr", [1, D], F32)
    k.dma("sp", [lngrow], [g["ln_g"]], lngrow.ap[:], g["ln_g"].ap[0:1, :])
    lnbrow = k.sb(sa, "p2_lnbr", [1, D], F32)
    k.dma("sp", [lnbrow], [g["ln_b"]], lnbrow.ap[:], g["ln_b"].ap[0:1, :])
    lng = k.sb(sa, "p2_lng", [128, D], F32)
    lnb = k.sb(sa, "p2_lnb", [128, D], F32)
    xin = k.sb(sa, "p2_xin", [128, 4 * D], F32)
    xnp = k.sb(sa, "p2_xnp", [128, 4 * D], F32)
    xnpf = k.sb(sa, "p2_xnpf", [128, NIC * 512], F32)
    xn2t = k.sb(sa, "p2_xn2t", [128, D], BF16)
    tmp = k.sb(sa, "p2_tmp", [128, D], F32)
    stats = k.sb(sa, "p2_stats", [128, 12], F32)
    mv = k.sb(sa, "p2_mv", [128, 2], F32)
    rstd = k.sb(sa, "p2_rstd", [128, 1], F32)
    lgf = k.sb(sa, "p2_lgf", [E, 512], F32)
    lt = k.sb(sa, "p2_lt", [128, E], F32)
    ssum = k.sb(sa, "p2_ssum", [128, 1], F32)
    eexp = k.sb(sa, "p2_eexp", [128, E], F32)
    m1 = k.sb(sa, "p2_m1", [128, 1], F32)
    negm = k.sb(sa, "p2_negm", [128, 1], F32)
    eq = k.sb(sa, "p2_eq", [128, E], F32)
    cum = k.sb(sa, "p2_cum", [128, E], F32)
    le1 = k.sb(sa, "p2_le1", [128, E], F32)
    mtm = k.sb(sa, "p2_mtm", [128, E], BF16)
    ysb = k.sb(sa, "p2_ysb", [128, D], F32)
    psT = k.psum(sa, "p2_psT", [128, 512], F32)
    psTb = k.psum(sa, "p2_psTb", [128, 512], BF16)
    psR = k.psum(sa, "p2_psR", [E, 512], F32)
    psY = k.psum(sa, "p2_psY", [128, 1024], F32)
    identf = k.sb(sa, "p2_identf", [128, 128], F32)
    emit_identity(k, identf)

    emit_bcast_row(k, lng, lngrow, psT, ones1, D)
    emit_bcast_row(k, lnb, lnbrow, psT, ones1, D)

    for tb in range(NTB):
        t0 = tb * 512
        k.dma("sp", [xin], [g["x1"]],
              xin.ap[:].rearrange("p (f d) -> p f d", d=D),
              g["x1"].ap[t0:t0 + 512, :].rearrange("(f p) d -> p f d", p=128))
        for j in range(4):
            x_ap = xin.ap[:, j * D:(j + 1) * D]
            mu_ap, rstd_ap = emit_ln(k, mv, stats, rstd, xin, x_ap, D)
            k.op("dve", "tensor_scalar", [xnp], [xin, mv, rstd],
                 xnp.ap[:, j * D:(j + 1) * D], x_ap, mu_ap, rstd_ap,
                 ALU.subtract, ALU.mult)
            for ic in range(NIC):
                k.transpose(psT, psT.ap[0:128, 0:128], xnp,
                            xnp.ap[:, j * D + ic * 128: j * D + (ic + 1) * 128], identf)
                k.op("dve", "tensor_copy", [xnpf], [psT],
                     xnpf.ap[:, ic * 512 + j * 128: ic * 512 + (j + 1) * 128],
                     psT.ap[0:128, 0:128])
            xp_ap = xnp.ap[:, j * D:(j + 1) * D]
            mu2, rstd2 = emit_ln(k, mv, stats, rstd, xnp, xp_ap, D)
            k.op("dve", "tensor_scalar", [tmp], [xnp, mv, rstd],
                 tmp.ap[:], xp_ap, mu2, rstd2, ALU.subtract, ALU.mult)
            k.op("dve", "tensor_tensor", [tmp], [tmp, lng],
                 tmp.ap[:], tmp.ap[:], lng.ap[:], ALU.mult)
            k.op("dve", "tensor_tensor", [xn2t], [tmp, lnb],
                 xn2t.ap[:], tmp.ap[:], lnb.ap[:], ALU.add)
            k.dma("sp", [xn2tok], [xn2t],
                  xn2tok.ap[t0 + j * 128: t0 + (j + 1) * 128, :], xn2t.ap[:])
        k.mm(psR, [(psR.ap[:], wr.ap[:, ic * E:(ic + 1) * E],
                    xnpf.ap[:, ic * 512:(ic + 1) * 512]) for ic in range(NIC)],
             [wr, xnpf])
        k.op("act", "activation", [lgf], [psR, brow],
             lgf.ap[:], psR.ap[:], AF.Identity, bias=brow.ap[:])
        for j in range(4):
            tt = t0 // 128 + j
            k.transpose(psT, psT.ap[0:128, 0:E], lgf,
                        lgf.ap[:, j * 128:(j + 1) * 128], identf)
            k.op("dve", "tensor_copy", [lt], [psT], lt.ap[:], psT.ap[0:128, 0:E])
            k.op("dve", "tensor_reduce", [m1], [lt], m1.ap[:], lt.ap[:], AX.X, ALU.max)
            k.op("dve", "tensor_scalar", [negm], [m1], negm.ap[:], m1.ap[:],
                 -1.0, None, ALU.mult)
            k.op("act", "activation", [eexp, ssum], [lt, negm],
                 eexp.ap[:], lt.ap[:], AF.Exp, bias=negm.ap[:],
                 accum_out=ssum.ap[:])
            k.op("dve", "reciprocal", [gates], [ssum],
                 gates.ap[:, tt:tt + 1], ssum.ap[:])
            k.op("dve", "tensor_scalar", [eq], [lt, m1], eq.ap[:], lt.ap[:],
                 m1.ap[:], None, ALU.is_equal)
            k.op("dve", "tensor_tensor_scan", [cum], [eq],
                 cum.ap[:], eq.ap[:], eq.ap[:], 0.0, ALU.add, ALU.bypass)
            k.op("dve", "tensor_scalar", [le1], [cum], le1.ap[:], cum.ap[:],
                 1.0, None, ALU.is_le)
            k.op("dve", "tensor_tensor", [mtm], [eq, le1], mtm.ap[:], eq.ap[:],
                 le1.ap[:], ALU.mult)
            k.transpose(psTb, psTb.ap[0:E, 0:128], mtm, mtm.ap[:], ident)
            k.op("dve", "tensor_copy", [maskf], [psTb],
                 maskf.ap[:, t0 + j * 128: t0 + (j + 1) * 128], psTb.ap[0:E, 0:128])
            for half in range(2):
                k.mm(psY, [(psY.ap[:, half * 512:(half + 1) * 512],
                            maskf.ap[:, t0 + j * 128: t0 + (j + 1) * 128],
                            b2sb.ap[:, half * 512:(half + 1) * 512])],
                     [maskf, b2sb])
            k.op("dve", "tensor_copy", [ysb], [psY], ysb.ap[:], psY.ap[:])
            k.dma("sp", [ysel], [ysb],
                  ysel.ap[t0 + j * 128: t0 + (j + 1) * 128, :], ysb.ap[:])

    mf32 = k.sb(sa, "p2_mf32", [E, NTOK], F32)
    ranks = k.sb(sa, "p2_ranks", [E, NTOK], F32)
    c8 = k.sb(sa, "p2_c8", [E, 1], F32)
    ntl = k.sb(sa, "p2_ntl", [E, 1], F32)
    t8 = k.sb(sa, "p2_t8", [E, 1], F32)
    row8 = k.sb(sa, "p2_row8", [1, 3 * E], F32)
    row24 = k.sb(sa, "p2_row24", [1, 2 * NT1], F32)
    iop = k.sb(sa, "p2_iop", [128, 1], I32)
    iopf = k.sb(sa, "p2_iopf", [128, 1], F32)
    ekbf = k.sb(sa, "p2_ekbf", [128, NT1], F32)
    NTI = NTOK // 128
    NN = max(NT1, NTI)
    i64r = k.sb(sa, "p2_i64r", [1, NN], I32)
    i64f = k.sb(sa, "p2_i64f", [1, NN], F32)
    tokid = k.sb(sa, "p2_tokid", [128, NN], F32)
    tokidu = k.sb(sa, "p2_tokidu", [128, NTI], U32)
    slotm = k.sb(sa, "p2_slotm", [128, NTI], F32)
    slotu = k.sb(sa, "p2_slotu", [128, NTI], U32)
    srow = k.sb(sa, "p2_srow", [1, 512], F32)
    ones8f = k.sb(sa, "p2_o8f", [8, 1], F32)
    oobt = k.sb(sa, "p2_oobt", [128, NSLOT // 128], U32)

    k.op("dve", "memset", [ones8f], [], ones8f.ap[:], 1.0)
    k.op("pool", "iota", [iop], [], iop.ap[:], pattern=[[0, 1]], base=0,
         channel_multiplier=1)
    k.op("dve", "tensor_copy", [iopf], [iop], iopf.ap[:], iop.ap[:])
    k.op("pool", "iota", [i64r], [], i64r.ap[:], pattern=[[1, NN]], base=0,
         channel_multiplier=0)
    k.op("dve", "tensor_copy", [i64f], [i64r], i64f.ap[:], i64r.ap[:])
    k.op("pool", "iota", [oobt], [], oobt.ap[:], pattern=[[1, NSLOT // 128]],
         base=OOBV, channel_multiplier=NSLOT // 128,
         allow_small_or_imprecise_dtypes=True)
    k.dma("pool", [tokof_d], [oobt],
          tokof_d.ap[:].rearrange("(p i) o -> p (i o)", p=128), oobt.ap[:])

    k.op("dve", "tensor_copy", [mf32], [maskf], mf32.ap[:], maskf.ap[:])
    k.op("dve", "tensor_tensor_scan", [ranks], [mf32],
         ranks.ap[:], mf32.ap[:], mf32.ap[:], 0.0, ALU.add, ALU.bypass)
    k.op("dve", "tensor_copy", [c8], [ranks], c8.ap[:], ranks.ap[:, NTOK - 1:NTOK])
    k.op("dve", "tensor_single_scalar", [ntl], [c8], ntl.ap[:], c8.ap[:],
         0.0, ALU.is_gt)
    for kk in range(1, NTOK // 512):
        k.op("dve", "tensor_single_scalar", [t8], [c8], t8.ap[:], c8.ap[:],
             float(512 * kk), ALU.is_gt)
        k.op("dve", "tensor_tensor", [ntl], [ntl, t8], ntl.ap[:], ntl.ap[:],
             t8.ap[:], ALU.add)
    k.transpose(psT, psT.ap[0:1, 0:E], ntl, ntl.ap[:], identf)
    k.op("dve", "tensor_copy", [row8], [psT], row8.ap[:, 0:E], psT.ap[0:1, 0:E])
    k.op("dve", "tensor_tensor_scan", [row8], [row8],
         row8.ap[:, E:2 * E], row8.ap[:, 0:E], row8.ap[:, 0:E],
         0.0, ALU.add, ALU.bypass)
    k.op("dve", "tensor_tensor", [row8], [row8],
         row8.ap[:, E:2 * E], row8.ap[:, E:2 * E], row8.ap[:, 0:E], ALU.subtract)
    iota24 = i64f.ap[:, 0:NT1]
    k.op("dve", "tensor_scalar", [row24], [i64f, row8],
         row24.ap[:, 0:NT1], iota24, row8.ap[:, E:E + 1], None, ALU.is_ge)
    for e in range(1, E):
        k.op("dve", "tensor_scalar", [row24], [i64f, row8],
             row24.ap[:, NT1:2 * NT1], iota24, row8.ap[:, E + e:E + e + 1],
             None, ALU.is_ge)
        k.op("dve", "tensor_tensor", [row24], [row24],
             row24.ap[:, 0:NT1], row24.ap[:, 0:NT1], row24.ap[:, NT1:2 * NT1],
             ALU.add)
    k.op("dve", "tensor_scalar", [row24], [row24], row24.ap[:, 0:NT1],
         row24.ap[:, 0:NT1], -1.0, None, ALU.add)
    k.mm(psT, [(psT.ap[:, 0:NT1], ones1.ap[:], row24.ap[:, 0:NT1])],
         [row24, ones1])
    k.op("dve", "tensor_copy", [ekbf], [psT], ekbf.ap[:], psT.ap[:, 0:NT1])
    k.op("dve", "scalar_tensor_tensor", [tokid], [ekbf, iopf], tokid.ap[:, 0:NT1],
         ekbf.ap[:], 128.0, iopf.ap[:].broadcast_to((128, NT1)), ALU.mult, ALU.add)
    k.op("dve", "tensor_copy", [wofs], [tokid], wofs.ap[:], tokid.ap[:, 0:NT1])
    k.op("dve", "scalar_tensor_tensor", [tokid], [ekbf, iopf], tokid.ap[:, 0:NT1],
         ekbf.ap[:], 32.0, iopf.ap[:].broadcast_to((128, NT1)), ALU.mult, ALU.add)
    k.op("dve", "tensor_copy", [b1ofs], [tokid], b1ofs.ap[:], tokid.ap[:, 0:NT1])
    k.transpose(psT, psT.ap[0:E, 0:1], row8, row8.ap[:, E:2 * E].rearrange(
        "o e -> o e") if False else row8.ap[0:1, E:2 * E], identf)
    k.op("dve", "tensor_copy", [t8], [psT], t8.ap[:], psT.ap[0:E, 0:1])
    k.op("dve", "tensor_scalar", [t8], [t8], t8.ap[:], t8.ap[:], 512.0, -1.0,
         ALU.mult, ALU.add)
    k.op("dve", "tensor_scalar", [ranks], [ranks, t8], ranks.ap[:], ranks.ap[:],
         t8.ap[:], None, ALU.add)
    k.op("dve", "tensor_tensor", [ranks], [ranks, mf32], ranks.ap[:],
         ranks.ap[:], mf32.ap[:], ALU.mult)
    for ch in range(NTOK // 512):
        k.mm(psR, [(psR.ap[0:1, :], ones8f.ap[:],
                    ranks.ap[:, ch * 512:(ch + 1) * 512])], [ranks, ones8f])
        k.op("dve", "tensor_copy", [srow], [psR], srow.ap[:], psR.ap[0:1, :])
        for u in range(4):
            k.transpose(psT, psT.ap[0:128, 0:1], srow,
                        srow.ap[:, u * 128:(u + 1) * 128], identf)
            k.op("dve", "tensor_copy", [slotm], [psT],
                 slotm.ap[:, ch * 4 + u: ch * 4 + u + 1], psT.ap[0:128, 0:1])
    k.op("pool", "iota", [tokid], [], tokid.ap[:, 0:NTI], pattern=[[128, NTI]],
         base=0, channel_multiplier=1, allow_small_or_imprecise_dtypes=True)
    k.op("dve", "tensor_copy", [tokidu], [tokid], tokidu.ap[:], tokid.ap[:, 0:NTI])
    k.op("dve", "tensor_copy", [slotu], [slotm], slotu.ap[:], slotm.ap[:])
    for i in range(NTI):
        k.idma([tokof_d], [tokidu, slotu], tokof_d.ap[:],
               slotu.ap[:, i:i + 1], tokidu.ap[:, i:i + 1], None,
               skip_waw=(i > 0))
    sa.close()
    k.barrier()

    sb_ = ExitStack()
    w1h = [k.sb(sb_, f"p2_w1h{i}", [128, NIC * 2048], BF16) for i in range(2)]
    w2h = [k.sb(sb_, f"p2_w2h{i}", [128, NOC * 512], BF16) for i in range(2)]
    b1row = k.sb(sb_, "p2_b1r", [32, 128], F32)
    b1sb = k.sb(sb_, "p2_b1", [128, NOC], F32)
    identf2 = k.sb(sb_, "p2_identf2", [128, 128], F32)
    emit_identity(k, identf2)
    tkof = k.sb(sb_, "p2_tkof", [128, 4], U32)
    tkj = [k.sb(sb_, f"p2_tkj{j}", [128, 1], U32) for j in range(4)]
    xgt = k.sb(sb_, "p2_xgt", [128, 4 * D], BF16)
    xgf = k.sb(sb_, "p2_xgf", [128, NIC * 512], BF16)
    hm = k.sb(sb_, "p2_hm", [128, NOC * 512], BF16)
    yev = [k.sb(sb_, f"p2_yev{i}", [128, D], F32) for i in range(2)]
    psH = [k.psum(sb_, f"p2_psH{i}", [128, 512], F32) for i in range(2)]
    psYh = [k.psum(sb_, f"p2_psYh{i}", [128, 512], F32) for i in range(2)]
    psM = k.psum(sb_, "p2_psM", [128, 512], F32)
    psTb2 = k.psum(sb_, "p2_psTb2", [128, 512], BF16)

    breg = nc.gpsimd.to_reg(NTOK - 1)
    b1v = g["b1"].ap[:].rearrange("e (c p) -> (e c) p", p=128)

    def gather_w(kt):
        k.idma([w1h[0]], [wofs], w1h[0].ap[:], None,
               g["w1a_bf"].ap[:], wofs.ap[:, kt:kt + 1])
        k.idma([w1h[1]], [wofs], w1h[1].ap[:], None,
               g["w1b_bf"].ap[:], wofs.ap[:, kt:kt + 1])
        k.idma([w2h[0]], [wofs], w2h[0].ap[:], None,
               g["w2a_bf"].ap[:], wofs.ap[:, kt:kt + 1])
        k.idma([w2h[1]], [wofs], w2h[1].ap[:], None,
               g["w2b_bf"].ap[:], wofs.ap[:, kt:kt + 1])
        k.idma([b1row], [b1ofs], b1row.ap[:], None, b1v,
               b1ofs.ap[0:32, kt:kt + 1])

    dbg_xg = k.dram("p2_dbg_xg", [NT1, 128, 4 * D], BF16) if cfg.get("P2_DBG") else None
    if cfg.get("P2_DBG"):
        dbg_w = k.dram("p2_dbg_w", [128, NT1], U32)
        k.dma("sp", [dbg_w], [wofs], dbg_w.ap[:], wofs.ap[:])
        dbg_b = k.dram("p2_dbg_b", [128, NT1], U32)
        k.dma("sp", [dbg_b], [b1ofs], dbg_b.ap[:], b1ofs.ap[:])
        dbg_hm = k.dram("p2_dbg_hm", [NT1, 128, NOC * 512], BF16)
        dbg_w1 = k.dram("p2_dbg_w1", [NT1, 128, 4096], BF16)
        dbg_y0 = k.dram("p2_dbg_y0", [NT1 * 4, D], F32)
        dbg_tk = k.dram("p2_dbg_tk", [NT1, 128, 4], U32)
    for kt in range(cfg.get("P2_TILES", NT1)):
        k.dma("sp", [tkof], [tokof_d], tkof.ap[:],
              tokof_d.ap[kt * 512:(kt + 1) * 512, :]
              .rearrange("(p j) o -> p (j o)", j=4))
        gather_w(kt)
        k.transpose(psM, psM.ap[0:128, 0:NOC], b1row, b1row.ap[0:32, :], identf2)
        k.op("dve", "tensor_copy", [b1sb], [psM], b1sb.ap[:], psM.ap[0:128, 0:NOC])
        for j in range(4):
            k.op("dve", "tensor_copy", [tkj[j]], [tkof], tkj[j].ap[:],
                 tkof.ap[:, j:j + 1])
            k.idma([xgt], [xn2tok, tkj[j]], xgt.ap[:, j * D:(j + 1) * D], None,
                   xn2tok.ap[:], tkj[j].ap[:],
                   bounds_check=breg, oob_is_err=False)
            for ic in range(NIC):
                k.transpose(psTb2, psTb2.ap[0:128, 0:128], xgt,
                            xgt.ap[:, j * D + ic * 128: j * D + (ic + 1) * 128], ident)
                k.op("dve", "tensor_copy", [xgf], [psTb2],
                     xgf.ap[:, ic * 512 + j * 128: ic * 512 + (j + 1) * 128],
                     psTb2.ap[0:128, 0:128])
        if dbg_xg is not None:
            k.dma("sp", [dbg_xg], [xgt], dbg_xg.ap[kt], xgt.ap[:])
            if kt > 0:
                k.dma("sp", [dbg_hm], [hm], dbg_hm.ap[kt - 1], hm.ap[:])
                k.dma("sp", [dbg_w1], [w1h[0]], dbg_w1.ap[kt - 1], w1h[0].ap[:, 0:4096])
        for oc in range(NOC):
            hh = oc // 16
            ocl = oc % 16
            pH = psH[oc % 2]
            k.mm(pH, [(pH.ap[:],
                       w1h[hh].ap[:, ic * 2048 + ocl * 128: ic * 2048 + (ocl + 1) * 128],
                       xgf.ap[:, ic * 512:(ic + 1) * 512]) for ic in range(NIC)],
                 [w1h[hh], xgf])
            k.op("act", "activation", [hm], [pH, b1sb],
                 hm.ap[:, oc * 512:(oc + 1) * 512], pH.ap[:], AF.Relu,
                 bias=b1sb.ap[:, oc:oc + 1])
        if dbg_xg is not None:
            k.dma("sp", [dbg_tk], [tkof], dbg_tk.ap[kt], tkof.ap[:])
        for j in range(4):
            for half in range(2):
                pY = psYh[half]
                k.mm(pY, [(pY.ap[:],
                           hm.ap[:, hc * 512 + j * 128: hc * 512 + (j + 1) * 128],
                           w2h[half].ap[:, hc * 512:(hc + 1) * 512])
                          for hc in range(NOC)],
                     [hm, w2h[half]])
                ev = "dve" if half == 0 else "act"
                if ev == "dve":
                    k.op("dve", "tensor_copy", [yev[j % 2]], [pY],
                         yev[j % 2].ap[:, half * 512:(half + 1) * 512], pY.ap[:])
                else:
                    k.op("act", "copy", [yev[j % 2]], [pY],
                         yev[j % 2].ap[:, half * 512:(half + 1) * 512], pY.ap[:])
            k.idma([yscr], [yev[j % 2], tkj[j]], yscr.ap[:],
                   tkj[j].ap[:], yev[j % 2].ap[:], None,
                   bounds_check=breg, oob_is_err=False, skip_waw=False)

    sb_.close()
    k.barrier()

    sc2 = ExitStack()
    xin2 = [k.sb(sc2, f"p2_xin2{i}", [128, D], F32) for i in range(2)]
    ysum = [k.sb(sc2, f"p2_ysum{i}", [128, D], F32) for i in range(2)]
    yld = [k.sb(sc2, f"p2_yld{i}", [128, D], F32) for i in range(2)]
    out2 = [k.sb(sc2, f"p2_out2{i}", [128, D], F32) for i in range(2)]
    for ti in range(NTOK // 128):
        i = ti % 2
        t0 = ti * 128
        k.dma("sp", [xin2[i]], [g["x1"]], xin2[i].ap[:], g["x1"].ap[t0:t0 + 128, :])
        k.dma("sp", [ysum[i]], [ysel], ysum[i].ap[:], ysel.ap[t0:t0 + 128, :])
        k.dma("sp", [yld[i]], [yscr], yld[i].ap[:], yscr.ap[t0:t0 + 128, :])
        k.op("dve", "tensor_tensor", [ysum[i]], [ysum[i], yld[i]],
             ysum[i].ap[:], ysum[i].ap[:], yld[i].ap[:], ALU.add)
        k.op("dve", "tensor_scalar", [ysum[i]], [ysum[i], gates],
             ysum[i].ap[:], ysum[i].ap[:], gates.ap[:, ti:ti + 1], None, ALU.mult)
        k.op("dve", "tensor_tensor", [out2[i]], [ysum[i], xin2[i]],
             out2[i].ap[:], ysum[i].ap[:], xin2[i].ap[:], ALU.add)
        k.dma("sp", [g["x2"]], [out2[i]],
              g["x2"].ap[t0:t0 + 128, :], out2[i].ap[:])
    sc2.close()
    sc.close()
    k.barrier()


def build_phase3(k, g, cfg):
    nc = k.nc
    B, T, D = cfg["B"], cfg["T"], cfg["D"]
    HG, DK, DV = 4, 1024, 2048
    HK, HV = DK // HG, DV // HG
    C = 64
    NIC = D // 128
    NTB = T // 512
    NTOK = B * T
    NCH = T // C
    GN = 16.0
    sc = ExitStack()

    ident = k.sb(sc, "p3_ident", [128, 128], BF16)
    emit_identity(k, ident)
    identf = k.sb(sc, "p3_identf", [128, 128], F32)
    emit_identity(k, identf)
    triu64 = k.sb(sc, "p3_triu", [64, 64], mybir.dt.int8)
    k.op("pool", "memset", [triu64], [], triu64.ap[:], 1)
    k.op("pool", "affine_select", [triu64], [triu64],
         out=triu64.ap[:], in_=triu64.ap[:], compare_op=ALU.is_ge,
         fill=0, base=0, pattern=[[1, 64]], channel_multiplier=-1)
    z64 = k.sb(sc, "p3_z64", [64, 64], F32)
    k.op("pool", "memset", [z64], [], z64.ap[:], 0.0)
    ones1 = k.sb(sc, "p3_ones1", [1, 128], F32)
    k.op("dve", "memset", [ones1], [], ones1.ap[:], 1.0)

    xnf_scr = k.dram("p3_xnf", [B * NTB, 128, NIC * 512], BF16)
    qt_scr = k.dram("p3_qt", [B * NTB, 128, NIC * 512], BF16)
    ki_scr = k.dram("p3_ki", [B * NTB, 128, NIC * 512], BF16)
    khe_scr = k.dram("p3_khe", [NTOK, DK], BF16)
    v_scr = k.dram("p3_v", [NTOK, DV], BF16)
    o_scr = k.dram("p3_o", [NTOK, DV], BF16)
    Dsb = k.sb(sc, "p3_D", [128, NIC * (NTOK // C // B) * B], F32)

    gkw1 = k.sb(sc, "p3_gkw1", [128, NIC * 16], BF16)
    k.dma("pool", [gkw1], [g["gk_w1"]],
          gkw1.ap[:].rearrange("p (c e) -> p c e", e=16),
          g["gk_w1"].ap[:].rearrange("(c p) e -> p c e", p=128))
    gkw2 = k.sb(sc, "p3_gkw2", [16, DK], BF16)
    k.dma("pool", [gkw2], [g["gk_w2"]], gkw2.ap[:], g["gk_w2"].ap[:])
    gkbrow = k.sb(sc, "p3_gkbr", [1, DK], F32)
    k.dma("sp", [gkbrow], [g["gk_b"]], gkbrow.ap[:], g["gk_b"].ap[0:1, :])
    rmsrow = k.sb(sc, "p3_rmsr", [1, HV], F32)
    k.dma("sp", [rmsrow], [g["rms_w"]], rmsrow.ap[:], g["rms_w"].ap[0:1, :])
    rmsw = k.sb(sc, "p3_rmsw", [128, HV], F32)

    sa = ExitStack()
    wq = k.sb(sa, "p3_wq", [128, NIC * DK], BF16)
    wk_ = k.sb(sa, "p3_wk", [128, NIC * DK], BF16)
    wv = k.sb(sa, "p3_wv", [128, NIC * DV], BF16)
    k.dma("sp", [wq], [g["wq_bf"]], wq.ap[:], g["wq_bf"].ap[:])
    k.dma("sp", [wk_], [g["wk_bf"]], wk_.ap[:], g["wk_bf"].ap[:])
    k.dma("sp", [wv], [g["wv_bf"]], wv.ap[:], g["wv_bf"].ap[:])
    xin = k.sb(sa, "p3_xin", [128, 4 * D], F32)
    xnt = k.sb(sa, "p3_xnt", [128, 4 * D], BF16)
    xnf = k.sb(sa, "p3_xnf_sb", [128, NIC * 512], BF16)
    stats = k.sb(sa, "p3_stats", [128, 12], F32)
    mv = k.sb(sa, "p3_mv", [128, 2], F32)
    rstd = k.sb(sa, "p3_rstd", [128, 1], F32)
    qtf = k.sb(sa, "p3_qtf", [128, NIC * 512], BF16)
    kif = k.sb(sa, "p3_kif", [128, NIC * 512], BF16)
    kef = k.sb(sa, "p3_kef", [128, 512], BF16)
    ket = k.sb(sa, "p3_ket", [64, 8 * 1024], BF16)
    vtm = k.sb(sa, "p3_vtm", [128, 4 * DV], BF16)
    midb = k.sb(sa, "p3_mid", [16, 512], BF16)
    glf = k.sb(sa, "p3_glf", [128, 512], F32)
    gf = k.sb(sa, "p3_gf", [128, NIC * 512], F32)
    cumg = k.sb(sa, "p3_cumg", [128, NIC * 512], F32)
    lam = k.sb(sa, "p3_lam", [128, 512], F32)
    enc = k.sb(sa, "p3_enc", [128, 512], F32)
    t1 = k.sb(sa, "p3_t1", [128, 512], F32)
    t2 = k.sb(sa, "p3_t2", [128, 512], F32)
    dcol = k.sb(sa, "p3_dcol", [128, 8], F32)
    gkb = k.sb(sa, "p3_gkb", [128, NIC], F32)
    gkbr2 = k.sb(sa, "p3_gkbr2", [NIC, 128], F32)
    psT = k.psum(sa, "p3_psT", [128, 512], F32)
    psTb = k.psum(sa, "p3_psTb", [128, 512], BF16)
    psA = k.psum(sa, "p3_psA", [128, 512], F32)
    psV = k.psum(sa, "p3_psV", [128, 1024], F32)
    psM = k.psum(sa, "p3_psM", [16, 512], F32)

    k.dma("sp", [gkbr2], [g["gk_b"]],
          gkbr2.ap[:], g["gk_b"].ap[0:1, :].rearrange("o (c p) -> (o c) p", p=128))
    k.transpose(psT, psT.ap[0:128, 0:NIC], gkbr2, gkbr2.ap[:], identf)
    k.op("dve", "tensor_copy", [gkb], [psT], gkb.ap[:], psT.ap[0:128, 0:NIC])
    emit_bcast_row(k, rmsw, rmsrow, psT, ones1, HV)
    cbe, eps_col = k.constcol(1e-5)
    cb1, one_col = k.constcol(1.0)

    for b in range(B):
        for tb in range(NTB):
            gtb = b * NTB + tb
            t0 = b * T + tb * 512
            k.dma("sp", [xin], [g["x2"]],
                  xin.ap[:].rearrange("p (f d) -> p f d", d=D),
                  g["x2"].ap[t0:t0 + 512, :].rearrange("(f p) d -> p f d", p=128))
            for j in range(4):
                x_ap = xin.ap[:, j * D:(j + 1) * D]
                mu_ap, rstd_ap = emit_ln(k, mv, stats, rstd, xin, x_ap, D)
                k.op("dve", "tensor_scalar", [xnt], [xin, mv, rstd],
                     xnt.ap[:, j * D:(j + 1) * D], x_ap, mu_ap, rstd_ap,
                     ALU.subtract, ALU.mult)
                for ic in range(NIC):
                    k.transpose(psTb, psTb.ap[0:128, 0:128], xnt,
                                xnt.ap[:, j * D + ic * 128: j * D + (ic + 1) * 128], ident)
                    k.op("dve", "tensor_copy", [xnf], [psTb],
                         xnf.ap[:, ic * 512 + j * 128: ic * 512 + (j + 1) * 128],
                         psTb.ap[0:128, 0:128])
            k.dma("sp", [xnf_scr], [xnf], xnf_scr.ap[gtb], xnf.ap[:])
            for j in range(4):
                for q4 in range(4):
                    half = q4 % 2
                    k.mm(psV, [(psV.ap[:, half * 512:(half + 1) * 512],
                                xnf.ap[:, ic * 512 + j * 128: ic * 512 + (j + 1) * 128],
                                wv.ap[:, ic * DV + q4 * 512: ic * DV + (q4 + 1) * 512])
                               for ic in range(NIC)], [wv, xnf])
                    if half == 1:
                        k.op("dve", "tensor_copy", [vtm], [psV],
                             vtm.ap[:, j * DV + (q4 - 1) * 512: j * DV + (q4 + 1) * 512],
                             psV.ap[:])
            k.dma("sp", [v_scr], [vtm],
                  v_scr.ap[t0:t0 + 512, :].rearrange("(f p) d -> p f d", p=128),
                  vtm.ap[:].rearrange("p (f d) -> p f d", d=DV))
            k.mm(psM, [(psM.ap[:], gkw1.ap[:, ic * 16:(ic + 1) * 16],
                        xnf.ap[:, ic * 512:(ic + 1) * 512]) for ic in range(NIC)],
                 [gkw1, xnf])
            k.op("dve", "tensor_copy", [midb], [psM], midb.ap[:], psM.ap[:])
            for pc in range(NIC):
                k.mm(psA, [(psA.ap[:], gkw2.ap[:, pc * 128:(pc + 1) * 128],
                            midb.ap[:])], [gkw2, midb])
                k.op("act", "activation", [glf], [psA, gkb],
                     glf.ap[:], psA.ap[:], AF.Identity, bias=gkb.ap[:, pc:pc + 1])
                k.op("act", "activation", [t1], [glf], t1.ap[:], glf.ap[:], AF.Abs)
                k.op("act", "activation", [t1], [t1], t1.ap[:], t1.ap[:], AF.Exp,
                     scale=-1.0)
                k.op("act", "activation", [t1], [t1, cb1], t1.ap[:], t1.ap[:], AF.Ln,
                     bias=one_col)
                k.op("dve", "tensor_single_scalar", [t2], [glf], t2.ap[:], glf.ap[:],
                     0.0, ALU.min)
                k.op("dve", "tensor_tensor", [t2], [t2, t1], t2.ap[:], t2.ap[:],
                     t1.ap[:], ALU.subtract)
                k.op("dve", "tensor_scalar", [gf], [t2], gf.ap[:, pc * 512:(pc + 1) * 512],
                     t2.ap[:], 1.0 / GN, None, ALU.mult)
                for c in range(8):
                    k.op("dve", "tensor_tensor_scan", [cumg], [gf],
                         cumg.ap[:, pc * 512 + c * C: pc * 512 + (c + 1) * C],
                         gf.ap[:, pc * 512 + c * C: pc * 512 + (c + 1) * C],
                         gf.ap[:, pc * 512 + c * C: pc * 512 + (c + 1) * C],
                         0.0, ALU.add, ALU.bypass)
                cg = cumg.ap[:, pc * 512:(pc + 1) * 512]
                k.op("dve", "tensor_copy", [dcol], [cumg], dcol.ap[:],
                     cumg.ap[:, pc * 512 + C - 1: (pc + 1) * 512: C])
                k.op("act", "activation", [Dsb], [dcol],
                     Dsb.ap[:, pc * (B * NCH) + b * NCH + tb * 8:
                            pc * (B * NCH) + b * NCH + tb * 8 + 8],
                     dcol.ap[:], AF.Exp)
                k.op("act", "activation", [lam], [cumg], lam.ap[:], cg, AF.Exp)
                k.mm(psA, [(psA.ap[:], wq.ap[:, ic * DK + pc * 128: ic * DK + (pc + 1) * 128],
                            xnf.ap[:, ic * 512:(ic + 1) * 512]) for ic in range(NIC)],
                     [wq, xnf])
                k.op("dve", "scalar_tensor_tensor", [qtf], [psA, lam],
                     qtf.ap[:, pc * 512:(pc + 1) * 512], psA.ap[:], 0.0625,
                     lam.ap[:], ALU.mult, ALU.mult)
                k.mm(psA, [(psA.ap[:], wk_.ap[:, ic * DK + pc * 128: ic * DK + (pc + 1) * 128],
                            xnf.ap[:, ic * 512:(ic + 1) * 512]) for ic in range(NIC)],
                     [wk_, xnf])
                k.op("act", "activation", [enc], [cumg], enc.ap[:], cg, AF.Exp,
                     scale=-1.0)
                k.op("dve", "tensor_tensor", [kif], [psA, enc],
                     kif.ap[:, pc * 512:(pc + 1) * 512], psA.ap[:], enc.ap[:], ALU.mult)
                for c in range(8):
                    k.op("dve", "tensor_scalar", [t1], [cumg],
                         t1.ap[:, c * C:(c + 1) * C], cg[:, c * C:(c + 1) * C],
                         cg[:, c * C + C - 1: c * C + C], None, ALU.subtract)
                k.op("act", "activation", [t1], [t1], t1.ap[:], t1.ap[:], AF.Exp,
                     scale=-1.0)
                k.op("dve", "tensor_tensor", [kef], [psA, t1],
                     kef.ap[:], psA.ap[:], t1.ap[:], ALU.mult)
                for c in range(8):
                    k.transpose(psTb, psTb.ap[0:64, 0:128], kef,
                                kef.ap[:, c * C:(c + 1) * C], ident)
                    k.op("dve", "tensor_copy", [ket], [psTb],
                         ket.ap[0:64, c * DK + pc * 128: c * DK + (pc + 1) * 128],
                         psTb.ap[0:64, 0:128])
            k.dma("sp", [qt_scr], [qtf], qt_scr.ap[gtb], qtf.ap[:])
            k.dma("sp", [ki_scr], [kif], ki_scr.ap[gtb], kif.ap[:])
            k.dma("sp", [khe_scr], [ket],
                  khe_scr.ap[t0:t0 + 512, :].rearrange("(c p) d -> p c d", p=64),
                  ket.ap[:].rearrange("p (c d) -> p c d", d=DK))
    sa.close()
    k.barrier()

    sb_ = ExitStack()
    SBT = 1024
    NSB = T // SBT
    NCB = SBT // C
    NS2 = 2
    qtb = [k.sb(sb_, f"p3b_qt{s}", [128, 2 * SBT], BF16) for s in range(NS2)]
    kib = [k.sb(sb_, f"p3b_ki{s}", [128, 2 * SBT], BF16) for s in range(NS2)]
    keb = [k.sb(sb_, f"p3b_ke{s}", [64, NCB * HK], BF16) for s in range(NS2)]
    vb = [k.sb(sb_, f"p3b_v{s}", [64, NCB * HV], BF16) for s in range(NS2)]
    onb = [k.sb(sb_, f"p3b_on{s}", [64, NCB * HV], BF16) for s in range(NS2)]
    Sm = [k.sb(sb_, f"p3b_Sm{s}", [128, 2 * HV], F32) for s in range(NS2)]
    Sbf = [k.sb(sb_, f"p3b_Sbf{s}", [128, 2 * HV], BF16) for s in range(NS2)]
    stb = [k.sb(sb_, f"p3b_st{s}", [64, 64], BF16) for s in range(NS2)]
    sqs = [k.sb(sb_, f"p3b_sqs{s}", [64, HV], BF16) for s in range(NS2)]
    ssq = [k.sb(sb_, f"p3b_ssq{s}", [64, 1], F32) for s in range(NS2)]
    rso = [k.sb(sb_, f"p3b_rso{s}", [64, 1], F32) for s in range(NS2)]
    psST = [k.psum(sb_, f"p3b_psST{s}", [64, 64], F32) for s in range(NS2)]
    psO = [k.psum(sb_, f"p3b_psO{s}", [64, 512], F32) for s in range(NS2)]
    psU = [k.psum(sb_, f"p3b_psU{s}", [128, 1024], F32) for s in range(NS2)]

    def sb3_load(s, h, b, sb2):
        base_t = b * T + sb2 * SBT
        for i, tbl in enumerate(range(sb2 * 2, sb2 * 2 + 2)):
            gtb = b * NTB + tbl
            k.dma("sp", [qtb[s]], [qt_scr],
                  qtb[s].ap[:].rearrange("p (q t) -> p q t", q=2)[:, :, i * 512:(i + 1) * 512],
                  qt_scr.ap[gtb].rearrange("p (q t) -> p q t", q=NIC)[:, 2 * h:2 * h + 2, :])
            k.dma("sp", [kib[s]], [ki_scr],
                  kib[s].ap[:].rearrange("p (q t) -> p q t", q=2)[:, :, i * 512:(i + 1) * 512],
                  ki_scr.ap[gtb].rearrange("p (q t) -> p q t", q=NIC)[:, 2 * h:2 * h + 2, :])
        k.dma("sp", [keb[s]], [khe_scr],
              keb[s].ap[:].rearrange("p (c d) -> p c d", d=HK),
              khe_scr.ap[base_t:base_t + SBT, h * HK:(h + 1) * HK]
              .rearrange("(c p) d -> p c d", p=64))
        k.dma("sp", [vb[s]], [v_scr],
              vb[s].ap[:].rearrange("p (c d) -> p c d", d=HV),
              v_scr.ap[base_t:base_t + SBT, h * HV:(h + 1) * HV]
              .rearrange("(c p) d -> p c d", p=64))

    def sb3_chunk(s, h, b, sb2, c):
        tof = c * C
        gch = b * NCH + sb2 * NCB + c
        q0 = qtb[s].ap[:, tof:tof + C]
        q1 = qtb[s].ap[:, SBT + tof:SBT + tof + C]
        k0 = kib[s].ap[:, tof:tof + C]
        k1 = kib[s].ap[:, SBT + tof:SBT + tof + C]
        k.mm(psST[s], [(psST[s].ap[:], k0, q0), (psST[s].ap[:], k1, q1)],
             [qtb[s], kib[s]])
        k.op("dve", "select", [stb[s]], [psST[s], triu64, z64],
             stb[s].ap[:], triu64.ap[:], psST[s].ap[:], z64.ap[:],
             add_drain=True)
        k.mm(psO[s], [(psO[s].ap[:], q0, Sbf[s].ap[0:128, 0:HV]),
                      (psO[s].ap[:], q1, Sbf[s].ap[0:128, HV:2 * HV]),
                      (psO[s].ap[:], stb[s].ap[:],
                       vb[s].ap[0:64, c * HV:(c + 1) * HV])],
             [qtb[s], Sbf[s], stb[s], vb[s]])
        k.op("act", "activation", [sqs[s], ssq[s]], [psO[s]],
             sqs[s].ap[:], psO[s].ap[:], AF.Square, accum_out=ssq[s].ap[:])
        k.op("act", "activation", [rso[s]], [ssq[s], k.cpool],
             rso[s].ap[:], ssq[s].ap[:], AF.Ln, bias=eps_col[0:64, :],
             scale=1.0 / HV)
        k.op("act", "activation", [rso[s]], [rso[s]], rso[s].ap[:], rso[s].ap[:],
             AF.Exp, scale=-0.5)
        k.op("dve", "scalar_tensor_tensor", [onb[s]], [psO[s], rso[s], rmsw],
             onb[s].ap[:, c * HV:(c + 1) * HV], psO[s].ap[:], rso[s].ap[:],
             rmsw.ap[0:64, :], ALU.mult, ALU.mult)
        k.mm(psU[s], [(psU[s].ap[:, 0:HV], keb[s].ap[:, c * HK: c * HK + 128],
                       vb[s].ap[0:64, c * HV:(c + 1) * HV])], [keb[s], vb[s]])
        k.mm(psU[s], [(psU[s].ap[:, HV:2 * HV],
                       keb[s].ap[:, c * HK + 128: c * HK + 256],
                       vb[s].ap[0:64, c * HV:(c + 1) * HV])], [keb[s], vb[s]])
        for half in range(2):
            k.op("dve", "scalar_tensor_tensor", [Sm[s]], [Sm[s], Dsb, psU[s]],
                 Sm[s].ap[:, half * HV:(half + 1) * HV],
                 Sm[s].ap[:, half * HV:(half + 1) * HV],
                 Dsb.ap[:, (2 * h + half) * (B * NCH) + gch:
                        (2 * h + half) * (B * NCH) + gch + 1],
                 psU[s].ap[:, half * HV:(half + 1) * HV], ALU.mult, ALU.add)
        k.op("dve", "tensor_copy", [Sbf[s]], [Sm[s]], Sbf[s].ap[:], Sm[s].ap[:])

    def sb3_store(s, h, b, sb2):
        base_t = b * T + sb2 * SBT
        k.dma("sp", [o_scr], [onb[s]],
              o_scr.ap[base_t:base_t + SBT, h * HV:(h + 1) * HV]
              .rearrange("(c p) d -> p c d", p=64),
              onb[s].ap[:].rearrange("p (c d) -> p c d", d=HV))

    for b in range(B):
        for hp in range(HG // 2):
            h0, h1 = 2 * hp, 2 * hp + 1
            for s, h in ((0, h0), (1, h1)):
                k.op("dve", "memset", [Sm[s]], [], Sm[s].ap[:], 0.0)
                k.op("dve", "memset", [Sbf[s]], [], Sbf[s].ap[:], 0.0)
            for sb2 in range(NSB):
                sb3_load(0, h0, b, sb2)
                sb3_load(1, h1, b, sb2)
                for c in range(NCB):
                    sb3_chunk(0, h0, b, sb2, c)
                    sb3_chunk(1, h1, b, sb2, c)
                sb3_store(0, h0, b, sb2)
                sb3_store(1, h1, b, sb2)
    sb_.close()
    k.barrier()

    sc3 = ExitStack()
    wgs = k.sb(sc3, "p3c_wg", [128, NIC * DV], BF16)
    wos = k.sb(sc3, "p3c_wo", [128, (DV // 128) * D], BF16)
    k.dma("sp", [wgs], [g["wg_bf"]], wgs.ap[:], g["wg_bf"].ap[:])
    k.dma("sp", [wos], [g["wo_bf"]], wos.ap[:], g["wo_bf"].ap[:])
    xnfb = k.sb(sc3, "p3c_xnf", [128, NIC * 512], BF16)
    ob = k.sb(sc3, "p3c_ob", [128, 4 * DV], BF16)
    ogt = k.sb(sc3, "p3c_ogt", [128, DV], BF16)
    ogf = k.sb(sc3, "p3c_ogf", [128, (DV // 128) * 512], BF16)
    xres = k.sb(sc3, "p3c_xres", [128, 4 * D], F32)
    out3 = k.sb(sc3, "p3c_out", [128, D], F32)
    psG = k.psum(sc3, "p3c_psG", [128, 1024], F32)
    psY = k.psum(sc3, "p3c_psY", [128, 1024], F32)
    psTb2 = k.psum(sc3, "p3c_psTb", [128, 512], BF16)

    for b in range(B):
        for tb in range(NTB):
            gtb = b * NTB + tb
            t0 = b * T + tb * 512
            k.dma("sp", [xnfb], [xnf_scr], xnfb.ap[:], xnf_scr.ap[gtb])
            k.dma("sp", [ob], [o_scr],
                  ob.ap[:].rearrange("p (f d) -> p f d", d=DV),
                  o_scr.ap[t0:t0 + 512, :].rearrange("(f p) d -> p f d", p=128))
            k.dma("sp", [xres], [g["x2"]],
                  xres.ap[:].rearrange("p (f d) -> p f d", d=D),
                  g["x2"].ap[t0:t0 + 512, :].rearrange("(f p) d -> p f d", p=128))
            for j in range(4):
                for q4 in range(4):
                    half = q4 % 2
                    k.mm(psG, [(psG.ap[:, half * 512:(half + 1) * 512],
                                xnfb.ap[:, ic * 512 + j * 128: ic * 512 + (j + 1) * 128],
                                wgs.ap[:, ic * DV + q4 * 512: ic * DV + (q4 + 1) * 512])
                               for ic in range(NIC)], [wgs, xnfb])
                    if half == 1:
                        for hh in range(2):
                            k.op("act", "activation", [ogt], [psG],
                                 ogt.ap[:, (q4 - 1 + hh) * 512:(q4 + hh) * 512],
                                 psG.ap[:, hh * 512:(hh + 1) * 512], AF.Silu)
                k.op("dve", "tensor_tensor", [ogt], [ogt, ob],
                     ogt.ap[:], ogt.ap[:], ob.ap[:, j * DV:(j + 1) * DV], ALU.mult)
                for ic in range(DV // 128):
                    k.transpose(psTb2, psTb2.ap[0:128, 0:128], ogt,
                                ogt.ap[:, ic * 128:(ic + 1) * 128], ident)
                    k.op("dve", "tensor_copy", [ogf], [psTb2],
                         ogf.ap[:, ic * 512 + j * 128: ic * 512 + (j + 1) * 128],
                         psTb2.ap[0:128, 0:128])
            for j in range(4):
                for half in range(2):
                    k.mm(psY, [(psY.ap[:, half * 512:(half + 1) * 512],
                                ogf.ap[:, ic * 512 + j * 128: ic * 512 + (j + 1) * 128],
                                wos.ap[:, ic * D + half * 512: ic * D + (half + 1) * 512])
                               for ic in range(DV // 128)], [ogf, wos])
                k.op("dve", "tensor_tensor", [out3], [psY, xres],
                     out3.ap[:], psY.ap[:], xres.ap[:, j * D:(j + 1) * D], ALU.add)
                k.dma("sp", [g["x3"]], [out3],
                      g["x3"].ap[t0 + j * 128: t0 + (j + 1) * 128, :], out3.ap[:])
    sc3.close()
    sc.close()
    k.barrier()


def emit_quant(k, e, x_buf, x_ap, w, bufs):
    u, cc, G, sbc = bufs["u"], bufs["cc"], bufs["G"], bufs["sb"]
    sg, cb, gb = bufs["sg"], bufs["cb"], bufs["gb"]
    nb = w // 16
    c3 = cc.ap[:, 0:w].rearrange("p (n c) -> p n c", c=16)
    s3 = sbc.ap[:, 0:w // 2].rearrange("p (n c) -> p n c", c=8)
    k.op("act", "activation", [cc], [x_buf], cc.ap[:, 0:w], x_ap, AF.Abs)
    k.op("dve", "tensor_tensor", [sbc], [cc], s3[:, :, 0:8], c3[:, :, 0:8],
         c3[:, :, 8:16], ALU.max)
    k.op("dve", "tensor_tensor", [sbc], [sbc], s3[:, :, 0:4], s3[:, :, 0:4],
         s3[:, :, 4:8], ALU.max)
    k.op("dve", "tensor_tensor", [sbc], [sbc], s3[:, :, 0:2], s3[:, :, 0:2],
         s3[:, :, 2:4], ALU.max)
    k.op("dve", "tensor_tensor", [sbc], [sbc], s3[:, :, 0:1], s3[:, :, 0:1],
         s3[:, :, 1:2], ALU.max)
    k.op("dve", "tensor_scalar", [sbc], [sbc],
         sbc.ap[:, w // 2: w // 2 + nb], sbc.ap[:, 0:w // 2][:, 0::8],
         1.0 / 6.0, 1e-12, ALU.mult, ALU.max)
    k.op("dve", "reciprocal", [sbc], [sbc],
         sbc.ap[:, w // 2 + nb: w // 2 + 2 * nb],
         sbc.ap[:, w // 2: w // 2 + nb])
    k.op("dve", "tensor_copy", [gb], [sbc], gb.ap[:, 0:nb],
         sbc.ap[:, w // 2: w // 2 + nb])
    s_b = gb.ap[:, 0:nb].rearrange("p (n o) -> p n o", o=1).broadcast_to((128, nb, 16))
    rs_b = sbc.ap[:, w // 2 + nb: w // 2 + 2 * nb].rearrange(
        "p (n o) -> p n o", o=1).broadcast_to((128, nb, 16))
    k.op("dve", "tensor_tensor", [u], [cc, sbc],
         u.ap[:, 0:w].rearrange("p (n c) -> p n c", c=16), c3, rs_b, ALU.mult)
    uap = u.ap[:, 0:w]
    k.op("dve", "tensor_single_scalar", [G], [u], G.ap[:, 0:w], uap, 0.25, ALU.is_gt)
    k.op("dve", "tensor_single_scalar", [cb], [u], cb.ap[:, 0:w], uap, 0.75, ALU.is_gt)
    k.op("dve", "tensor_single_scalar", [sg], [u], sg.ap[:, 0:w], uap, 1.25, ALU.is_gt)
    k.op("dve", "tensor_tensor", [G], [G, cb], G.ap[:, 0:w], G.ap[:, 0:w],
         cb.ap[:, 0:w], ALU.add)
    k.op("dve", "tensor_single_scalar", [cb], [u], cb.ap[:, 0:w], uap, 1.75, ALU.is_gt)
    k.op("dve", "tensor_tensor", [sg], [sg, cb], sg.ap[:, 0:w], sg.ap[:, 0:w],
         cb.ap[:, 0:w], ALU.add)
    k.op("dve", "tensor_tensor", [G], [G, sg], G.ap[:, 0:w], G.ap[:, 0:w],
         sg.ap[:, 0:w], ALU.add)
    k.op("dve", "tensor_scalar", [G], [G], G.ap[:, 0:w], G.ap[:, 0:w], 0.5,
         None, ALU.mult)
    cbA, m25 = k.constcol(-2.5)
    cbB, m35 = k.constcol(-3.5)
    cbC, m50 = k.constcol(-5.0)
    k.op("act", "activation", [sg], [u, cbA], sg.ap[:, 0:w], uap, AF.Sign,
         bias=m25)
    k.op("act", "activation", [cb], [u, cbB], cb.ap[:, 0:w], uap, AF.Sign,
         bias=m35)
    k.op("dve", "tensor_tensor", [sg], [sg, cb], sg.ap[:, 0:w], sg.ap[:, 0:w],
         cb.ap[:, 0:w], ALU.add)
    k.op("act", "activation", [cb], [u, cbC], cb.ap[:, 0:w], uap, AF.Sign,
         bias=m50)
    k.op("dve", "scalar_tensor_tensor", [sg], [sg, cb], sg.ap[:, 0:w],
         sg.ap[:, 0:w], 0.5, cb.ap[:, 0:w], ALU.mult, ALU.add)
    k.op("dve", "tensor_tensor", [G], [G, sg], G.ap[:, 0:w], G.ap[:, 0:w],
         sg.ap[:, 0:w], ALU.add)
    k.op("dve", "tensor_scalar", [G], [G], G.ap[:, 0:w], G.ap[:, 0:w], 2.0,
         None, ALU.add)
    k.op("act", "activation", [cb], [x_buf], cb.ap[:, 0:w], x_ap, AF.Sign)
    k.op("dve", "tensor_tensor", [G], [G, cb], G.ap[:, 0:w], G.ap[:, 0:w],
         cb.ap[:, 0:w], ALU.mult)
    k.op("dve", "tensor_tensor", [bufs["out_buf"]], [G, gb],
         bufs["out_ap"].rearrange("p (n c) -> p n c", c=16),
         G.ap[:, 0:w].rearrange("p (n c) -> p n c", c=16), s_b, ALU.mult)


def build_phase4(k, g, cfg):
    nc = k.nc
    B, T, D = cfg["B"], cfg["T"], cfg["D"]
    I = 2816
    NIC = D // 128
    NHC = I // 128
    NTB = B * T // 512
    HW = I // 2
    sc = ExitStack()

    ident = k.sb(sc, "p4_ident", [128, 128], BF16)
    emit_identity(k, ident)

    wgate = k.sb(sc, "p4_wg", [128, NIC * I], BF16)
    wup = k.sb(sc, "p4_wu", [128, NIC * I], BF16)
    wdown = k.sb(sc, "p4_wd", [128, NHC * D], BF16)

    I4 = 2816
    u = k.sb(sc, "p4_u", [128, I4], F32)
    ccb = k.sb(sc, "p4_cc", [128, I4], F32)
    G = k.sb(sc, "p4_G", [128, I4], BF16)
    sbc = k.sb(sc, "p4_sb", [128, I4 // 2 + 2 * (I4 // 16)], F32)
    sgb = k.sb(sc, "p4_sg", [128, I4], BF16)
    cbb = k.sb(sc, "p4_cb", [128, I4], BF16)
    gbb = k.sb(sc, "p4_gb", [128, I4 // 16], BF16)
    qb = {"u": u, "cc": ccb, "G": G, "sb": sbc, "sg": sgb, "cb": cbb,
          "gb": gbb}

    sa = ExitStack()
    wrow = k.sb(sa, "p4a_wrow", [128, I], F32)
    wq_t = k.sb(sa, "p4a_wqt", [128, I], BF16)
    psTb = k.psum(sa, "p4a_psT", [128, 512], BF16)
    for wsrc, wdst in ((g["w_gate"], wgate), (g["w_up"], wup)):
        for r in range(NHC):
            k.dma("sp", [wrow], [wsrc], wrow.ap[:, 0:D],
                  wsrc.ap[r * 128:(r + 1) * 128, :])
            e = "dve" if r % 2 == 0 else "pool"
            emit_quant(k, e, wrow, wrow.ap[:, 0:D], D,
                       dict(qb, out_buf=wq_t, out_ap=wq_t.ap[:, 0:D]))
            for ic in range(NIC):
                k.transpose(psTb, psTb.ap[0:128, 0:128], wq_t,
                            wq_t.ap[:, ic * 128:(ic + 1) * 128], ident)
                k.op("dve", "tensor_copy", [wdst], [psTb],
                     wdst.ap[:, ic * I + r * 128: ic * I + (r + 1) * 128],
                     psTb.ap[0:128, 0:128])
    for r in range(NIC):
        k.dma("sp", [wrow], [g["w_down"]], wrow.ap[:],
              g["w_down"].ap[r * 128:(r + 1) * 128, :])
        for half in range(2):
            e = "dve" if half == 0 else "pool"
            emit_quant(k, e, wrow, wrow.ap[:, half * HW:(half + 1) * HW], HW,
                       dict(qb, out_buf=wq_t,
                            out_ap=wq_t.ap[:, half * HW:(half + 1) * HW]))
        for hc in range(NHC):
            k.transpose(psTb, psTb.ap[0:128, 0:128], wq_t,
                        wq_t.ap[:, hc * 128:(hc + 1) * 128], ident)
            k.op("dve", "tensor_copy", [wdown], [psTb],
                 wdown.ap[:, hc * D + r * 128: hc * D + (r + 1) * 128],
                 psTb.ap[0:128, 0:128])
    sa.close()
    k.barrier()

    sb_ = ExitStack()
    xin = k.sb(sb_, "p4_xin", [128, D], F32)
    xqf = k.sb(sb_, "p4_xqf", [128, NIC * 128], BF16)
    xqfv = [Buf(xqf.ap[:], "p4_xqf_a"), Buf(xqf.ap[:], "p4_xqf_b")]
    hh = k.sb(sb_, "p4_hh", [128, I], F32)
    hqf = k.sb(sb_, "p4_hqf", [128, 22 * 128], BF16)
    hqfv = [Buf(hqf.ap[:], "p4_hqf_a"), Buf(hqf.ap[:], "p4_hqf_b")]
    stats = k.sb(sb_, "p4_stats", [128, 12], F32)
    mv = k.sb(sb_, "p4_mv", [128, 2], F32)
    rstd = k.sb(sb_, "p4_rstd", [128, 1], F32)
    out4 = k.sb(sb_, "p4_out", [128, D], F32)
    psG = k.psum(sb_, "p4_psG", [128, 1408], F32)
    psY = k.psum(sb_, "p4_psY", [128, 1024], F32)
    psT2l = [k.psum(sb_, f"p4_psT2{i}", [128, 512], BF16) for i in range(2)]

    for tb in range(NTB):
        for j in range(4):
            ti = tb * 4 + j
            t0 = ti * 128
            k.dma("sp", [xin], [g["x3"]], xin.ap[:], g["x3"].ap[t0:t0 + 128, :])
            mu_ap, rstd_ap = emit_ln(k, mv, stats, rstd, xin, xin.ap[:], D)
            k.op("dve", "tensor_scalar", [hh], [xin, mv, rstd],
                 hh.ap[:, 0:D], xin.ap[:], mu_ap, rstd_ap, ALU.subtract, ALU.mult)
            emit_quant(k, "dve", hh, hh.ap[:, 0:D], D,
                       dict(qb, out_buf=sgb, out_ap=sgb.ap[:, 0:D]))
            for ic in range(NIC):
                pT = psT2l[ic % 2]
                k.transpose(pT, pT.ap[0:128, 0:128], sgb,
                            sgb.ap[:, ic * 128:(ic + 1) * 128], ident)
                k.op("dve", "tensor_copy", [xqfv[ic % 2]], [pT],
                     xqf.ap[:, ic * 128:(ic + 1) * 128],
                     pT.ap[0:128, 0:128])
            for half in range(2):
                cols = [(0, 512), (512, 1024), (1024, 1408)]
                for wsb, is_up in ((wgate, False), (wup, True)):
                    for (c0, c1) in cols:
                        k.mm(psG, [(psG.ap[:, c0:c1],
                                    xqf.ap[:, ic * 128:(ic + 1) * 128],
                                    wsb.ap[:, ic * I + half * HW + c0:
                                           ic * I + half * HW + c1])
                                   for ic in range(NIC)],
                             [wsb, xqfv[0], xqfv[1]])
                    if not is_up:
                        for (c0, c1) in cols:
                            k.op("act", "activation", [hh], [psG],
                                 hh.ap[:, half * HW + c0: half * HW + c1],
                                 psG.ap[:, c0:c1], AF.Silu)
                    else:
                        k.op("dve", "tensor_tensor", [hh], [psG, hh],
                             hh.ap[:, half * HW:(half + 1) * HW], psG.ap[:],
                             hh.ap[:, half * HW:(half + 1) * HW], ALU.mult)
            emit_quant(k, "dve", hh, hh.ap[:], I,
                       dict(qb, out_buf=sgb, out_ap=sgb.ap[:, 0:I]))
            for hc in range(22):
                pT = psT2l[hc % 2]
                k.transpose(pT, pT.ap[0:128, 0:128], sgb,
                            sgb.ap[:, hc * 128:(hc + 1) * 128], ident)
                k.op("dve", "tensor_copy", [hqfv[hc % 2]], [pT],
                     hqf.ap[:, hc * 128:(hc + 1) * 128],
                     pT.ap[0:128, 0:128])
            for colh in range(2):
                k.mm(psY, [(psY.ap[:, colh * 512:(colh + 1) * 512],
                            hqf.ap[:, hc2 * 128:(hc2 + 1) * 128],
                            wdown.ap[:, hc2 * D + colh * 512:
                                     hc2 * D + (colh + 1) * 512])
                           for hc2 in range(22)],
                     [hqfv[0], hqfv[1], wdown])
            k.dma("sp", [xin], [g["x3"]], xin.ap[:], g["x3"].ap[t0:t0 + 128, :])
            k.op("dve", "tensor_tensor", [out4], [psY, xin],
                 out4.ap[:], psY.ap[:], xin.ap[:], ALU.add)
            k.dma("sp", [g["y"]], [out4], g["y"].ap[t0:t0 + 128, :], out4.ap[:])
    sb_.close()
    sc.close()
    k.barrier()


def layout_l0_attn(w_kqv, w_proj):
    w = np.asarray(w_kqv).reshape(16, 64, 3, 1024)
    wk = w[:, :, 0, :].reshape(1024, 1024)
    wq = w[:, :, 1, :].reshape(1024, 1024)
    wv = w[:, :, 2, :].reshape(1024, 1024)
    wkqv_fm = np.concatenate([wk.T, wq.T, wv.T], axis=1)
    wproj_fm = np.ascontiguousarray(np.asarray(w_proj).T)
    return np.ascontiguousarray(wkqv_fm), wproj_fm


def sbready(w_fm):
    K, N = w_fm.shape
    return np.ascontiguousarray(
        np.asarray(w_fm).reshape(K // 128, 128, N).transpose(1, 0, 2)
        .reshape(128, (K // 128) * N))


CFG = {"B": 4, "T": 2048, "D": 1024}
FUSED = True
NCORES = 8

_P1_W = [("wkqv_fm", [128, 8 * 3072]), ("wproj_fm", [128, 8 * 1024]),
         ("b_proj", [1, 1024])]
_P2_W = [("Wr", [1024, 8]), ("br", [1, 8]), ("ln_g", [1, 1024]), ("ln_b", [1, 1024]),
         ("W1", [8 * 128, 8 * 4096]), ("b1", [8, 4096]), ("W2", [8 * 128, 32 * 1024]),
         ("b2", [8, 1024])]
_P3_W = [("Wq", [128, 8 * 1024]), ("Wk", [128, 8 * 1024]), ("Wv", [128, 8 * 2048]),
         ("Wg", [128, 8 * 2048]), ("gk_w1", [1024, 16]), ("gk_w2", [16, 1024]),
         ("gk_b", [1, 1024]), ("Wo", [128, 16 * 1024]), ("rms_w", [1, 512])]
_P4_W = [("w_gate", [2816, 1024]), ("w_up", [2816, 1024]), ("w_down", [1024, 2816])]


def _weights_from_inputs(inputs):
    wkqv_fm, wproj_fm = layout_l0_attn(inputs["L0_attn_w_kqv"], inputs["L0_attn_w_proj"])
    r = lambda a: np.ascontiguousarray(np.asarray(a, np.float32).reshape(1, -1))
    c = lambda a: np.ascontiguousarray(np.asarray(a, np.float32))
    W1 = np.asarray(inputs["L0_ffn_W1"], np.float32)
    W2 = np.asarray(inputs["L0_ffn_W2"], np.float32)
    w1r = np.ascontiguousarray(
        W1.reshape(8, 8, 128, 4096).transpose(0, 2, 1, 3).reshape(8 * 128, 8 * 4096))
    w2r = np.ascontiguousarray(
        W2.reshape(8, 32, 128, 1024).transpose(0, 2, 1, 3).reshape(8 * 128, 32 * 1024))
    return {
        "wkqv_fm": sbready(wkqv_fm.astype(np.float32)),
        "wproj_fm": sbready(wproj_fm.astype(np.float32)),
        "b_proj": r(inputs["L0_attn_b_proj"]),
        "Wr": c(inputs["L0_ffn_Wr"]), "br": r(inputs["L0_ffn_br"]),
        "ln_g": r(inputs["L0_ffn_ln_g"]), "ln_b": r(inputs["L0_ffn_ln_b"]),
        "W1": w1r, "b1": c(inputs["L0_ffn_b1"]),
        "W2": w2r, "b2": c(inputs["L0_ffn_b2"]),
        "Wq": sbready(c(inputs["L1_attn_Wq"])), "Wk": sbready(c(inputs["L1_attn_Wk"])),
        "Wv": sbready(c(inputs["L1_attn_Wv"])), "Wg": sbready(c(inputs["L1_attn_Wg"])),
        "gk_w1": c(inputs["L1_attn_gk_w1"]), "gk_w2": c(inputs["L1_attn_gk_w2"]),
        "gk_b": r(inputs["L1_attn_gk_b"]), "Wo": sbready(c(inputs["L1_attn_Wo"])),
        "rms_w": r(inputs["L1_attn_rms_w"]),
        "w_gate": c(inputs["L1_ffn_w_gate"]), "w_up": c(inputs["L1_ffn_w_up"]),
        "w_down": c(inputs["L1_ffn_w_down"]),
    }


def _run_phase(build_fn, wspecs, in_name, out_name, xs, wmap):
    NT = CFG["B"] * CFG["T"]
    nc = bass.Bass("TRN2", target_bir_lowering=False)
    k = K(nc)
    g = {in_name: k.dram(in_name, [NT, CFG["D"]], F32, kind="ExternalInput"),
         out_name: k.dram(out_name, [NT, CFG["D"]], F32, kind="ExternalOutput")}
    for n, s in wspecs:
        g[n] = k.dram(n, s, F32, kind="ExternalInput")
    build_fn(k, g, CFG)
    k.finish_program()
    k.es.close()
    nc.finalize()
    in_maps = []
    for c in range(NCORES):
        m = {in_name: xs[c]}
        for n, _ in wspecs:
            m[n] = wmap[n]
        in_maps.append(m)
    res = run_bass_kernel_spmd(nc, in_maps, core_ids=list(range(NCORES)))
    return [r[out_name] for r in res.results]


def _run_fused(xs, wmap):
    NT = CFG["B"] * CFG["T"]
    D = CFG["D"]
    nc = bass.Bass("TRN2", target_bir_lowering=False)
    k = K(nc)
    g = {"x0": k.dram("x0", [NT, D], F32, kind="ExternalInput"),
         "y": k.dram("y", [NT, D], F32, kind="ExternalOutput"),
         "x1": k.dram("x1", [NT, D], F32, kind="Internal"),
         "x2": k.dram("x2", [NT, D], F32, kind="Internal"),
         "x3": k.dram("x3", [NT, D], F32, kind="Internal")}
    for n, s in _P1_W + _P2_W + _P3_W + _P4_W:
        g[n] = k.dram(n, s, F32, kind="ExternalInput")
    build_phase0(k, g, CFG)
    build_phase1(k, g, CFG)
    build_phase2(k, g, CFG)
    build_phase3(k, g, CFG)
    build_phase4(k, g, CFG)
    k.finish_program()
    k.es.close()
    nc.finalize()
    in_maps = []
    for c in range(NCORES):
        m = {"x0": xs[c]}
        for n, _ in _P1_W + _P2_W + _P3_W + _P4_W:
            m[n] = wmap[n]
        in_maps.append(m)
    res = run_bass_kernel_spmd(nc, in_maps, core_ids=list(range(NCORES)))
    return [r["y"] for r in res.results]


def kernel(**inputs):
    x = np.asarray(inputs["L0_attn_x"], np.float32)
    D = CFG["D"]
    xs = [np.ascontiguousarray(x[4 * c:4 * c + 4].reshape(-1, D)) for c in range(NCORES)]
    wmap = _weights_from_inputs(inputs)
    ys = _run_fused(xs, wmap)
    out = np.concatenate([y.reshape(4, CFG["T"], D) for y in ys], axis=0)
    return out.astype(np.float32)


# version 24
# speedup vs baseline: 1.2003x; 1.0120x over previous
import math
from contextlib import ExitStack

import numpy as np
import concourse.bass as bass
import concourse.mybir as mybir
from concourse.bass_utils import run_bass_kernel_spmd

F32 = mybir.dt.float32
BF16 = mybir.dt.bfloat16
AF = mybir.ActivationFunctionType
ALU = mybir.AluOpType
AX = mybir.AxisListType
U8 = mybir.dt.uint8
U32 = mybir.dt.uint32
I32 = mybir.dt.int32


class Buf:
    __slots__ = ("ap", "writes", "reads", "name", "dsem", "dqueue")

    def __init__(self, ap, name=""):
        self.ap = ap
        self.writes = {}
        self.reads = {}
        self.name = name
        self.dsem = None
        self.dqueue = None


class K:
    def __init__(self, nc):
        self.nc = nc
        self.eng = {
            "pe": nc.tensor, "dve": nc.vector, "act": nc.scalar,
            "pool": nc.gpsimd, "sp": nc.sync,
        }
        self.es = ExitStack()
        self.sem = {}
        self.cnt = {}
        for t in ("pe", "dve", "act", "pool"):
            self.sem[t] = self.es.enter_context(nc.semaphore(t))
            self.cnt[t] = 0
        self.memo = {}
        self.drained = {}
        t = self.es.enter_context(nc.sbuf_tensor("k_cpool", [128, 16], F32))
        self.cpool = Buf(t.ap(), "k_cpool")
        self.cvals = {}

    def _dsem_for(self, buf, queue):
        if buf.dsem is None:
            tag = "d_" + buf.name
            buf.dsem = tag
            buf.dqueue = queue
            self.sem[tag] = self.es.enter_context(self.nc.semaphore(tag))
            self.cnt[tag] = 0
        else:
            assert buf.dqueue == queue, \
                f"buffer {buf.name} DMA'd from both HWDGE and SWDGE queues"
        return buf.dsem

    def constcol(self, val):
        if val not in self.cvals:
            i = len(self.cvals)
            self.cvals[val] = i
            self.op("pool", "memset", [self.cpool], [],
                    self.cpool.ap[:, i:i + 1], float(val))
        i = self.cvals[val]
        return self.cpool, self.cpool.ap[:, i:i + 1]

    def sb(self, scope, name, shape, dtype):
        t = scope.enter_context(self.nc.sbuf_tensor(name, shape, dtype))
        return Buf(t.ap(), name)

    def psum(self, scope, name, shape, dtype=F32):
        t = scope.enter_context(self.nc.psum_tensor(name, shape, dtype))
        return Buf(t.ap(), name)

    def dram(self, name, shape, dtype, kind="Internal"):
        t = self.nc.dram_tensor(name, shape, dtype, kind=kind)
        return Buf(t.ap(), name)

    def _wait_deps(self, etag, outs, ins, skip_tag=None):
        deps = {}
        for b in ins:
            for t, v in b.writes.items():
                if deps.get(t, 0) < v:
                    deps[t] = v
        for b in outs:
            for src in (b.writes, b.reads):
                for t, v in src.items():
                    if t == skip_tag:
                        continue
                    if deps.get(t, 0) < v:
                        deps[t] = v
        eng = self.eng[etag]
        need_drain = False
        for t, v in deps.items():
            if t == etag:
                if etag == "pe":
                    continue
                if self.drained.get(etag, 0) < v:
                    need_drain = True
            else:
                if t not in ("pe", "dve", "act", "pool"):
                    v = self.cnt[t]
                if self.memo.get((etag, t), 0) < v:
                    eng.wait_ge(self.sem[t], v)
                    self.memo[(etag, t)] = v
        if need_drain:
            eng.drain()
            self.drained[etag] = self.cnt[etag]

    def _finish(self, etag, inst, outs, ins, tag, step):
        self.cnt[tag] += step
        inst.then_inc(self.sem[tag], step)
        tok_v = self.cnt[tag]
        for b in outs:
            if step == 16 and tag == b.dsem:
                b.writes[tag] = tok_v
            else:
                b.writes = {tag: tok_v}
            b.reads = {}
        for b in ins:
            b.reads[tag] = tok_v
        return (tag, tok_v)

    def barrier(self):
        for etag in ("pe", "dve", "act", "pool", "sp"):
            eng = self.eng[etag]
            for t in self.sem:
                v = self.cnt[t]
                if v and t != etag and self.memo.get((etag, t), 0) < v:
                    eng.wait_ge(self.sem[t], v)
                    self.memo[(etag, t)] = v
            if etag in ("dve", "act", "pool"):
                eng.drain()
                self.drained[etag] = self.cnt[etag]

    def finish_program(self):
        for t in list(self.sem):
            v = self.cnt[t]
            if v and t != "sp" and self.memo.get(("sp", t), 0) < v:
                self.eng["sp"].wait_ge(self.sem[t], v)
                self.memo[("sp", t)] = v

    def op(self, etag, name, outs, ins, *args, **kw):
        self._wait_deps(etag, outs, ins)
        inst = getattr(self.eng[etag], name)(*args, **kw)
        return self._finish(etag, inst, outs, ins, etag, 1)

    def idma(self, outs, ins, out_ap, out_off, in_ap, in_off, skip_waw=True, **kw):
        from concourse.bass import IndirectOffsetOnAxis
        tag = self._dsem_for(outs[0], "sw")
        self._wait_deps("pool", outs, ins, skip_tag=tag if skip_waw else None)
        inst = self.eng["pool"].indirect_dma_start(
            out_ap,
            IndirectOffsetOnAxis(out_off, 0) if out_off is not None else None,
            in_ap,
            IndirectOffsetOnAxis(in_off, 0) if in_off is not None else None,
            **kw)
        return self._finish("pool", inst, outs, ins, tag, 16)

    def dma(self, etag, outs, ins, out_ap, in_ap, **kw):
        queue = "hw" if etag in ("sp", "act") else "sw"
        tag = self._dsem_for(outs[0], queue)
        self._wait_deps(etag, outs, ins, skip_tag=tag)
        inst = self.eng[etag].dma_start(out=out_ap, in_=in_ap, **kw)
        return self._finish(etag, inst, outs, ins, tag, 16)

    def mm(self, psum_buf, items, ins, accum=False, stop=True):
        self._wait_deps("pe", [psum_buf], ins)
        n = len(items)
        inst = None
        for i, (ps_ap, lhsT, rhs) in enumerate(items):
            inst = self.nc.tensor.matmul(
                ps_ap, lhsT, rhs,
                start=(i == 0 and not accum), stop=(stop and i == n - 1))
        return self._finish("pe", inst, [psum_buf], ins, "pe", 1)

    def transpose(self, psum_buf, ps_ap, in_buf, in_ap, ident):
        kdim = in_ap.shape[0]
        self._wait_deps("pe", [psum_buf], [in_buf, ident])
        inst = self.nc.tensor.transpose(ps_ap, in_ap, ident.ap[0:kdim, 0:kdim])
        return self._finish("pe", inst, [psum_buf], [in_buf, ident], "pe", 1)


def emit_bcast_row(k, out_buf, row_buf, ps_buf, ones1, D):
    for c in range(0, D, 512):
        w = min(512, D - c)
        k.mm(ps_buf, [(ps_buf.ap[:, 0:w], ones1.ap[:], row_buf.ap[:, c:c + w])],
             [row_buf, ones1])
        k.op("dve", "tensor_copy", [out_buf], [ps_buf],
             out_buf.ap[:, c:c + w], ps_buf.ap[:, 0:w])


def emit_identity(k, ident, n=128):
    k.op("pool", "memset", [ident], [], ident.ap[:], 0.0)
    k.op("pool", "affine_select", [ident], [ident],
         out=ident.ap[:], in_=ident.ap[:], compare_op=ALU.not_equal,
         fill=1.0, base=0, pattern=[[-1, n]], channel_multiplier=1)


def emit_tril(k, buf, n=128, val=1.0):
    k.op("pool", "memset", [buf], [], buf.ap[:], val)
    k.op("pool", "affine_select", [buf], [buf],
         out=buf.ap[:], in_=buf.ap[:], compare_op=ALU.is_ge,
         fill=0.0, base=0, pattern=[[-1, n]], channel_multiplier=1)


def emit_ln(k, mv, stats, rstd, xin_buf, x_ap, D, eps=1e-5):
    nch = (D + 511) // 512
    for c in range(nch):
        k.op("dve", "bn_stats", [stats], [xin_buf],
             stats.ap[:, 6 * c:6 * c + 6], x_ap[:, 512 * c:512 * (c + 1)])
    k.op("dve", "bn_aggr", [mv], [stats], mv.ap[:], stats.ap[:, 0:6 * nch])
    cb, eps_ap = k.constcol(eps)
    k.op("act", "activation", [rstd], [mv, cb], rstd.ap[:], mv.ap[:, 1:2], AF.Ln, bias=eps_ap)
    k.op("act", "activation", [rstd], [rstd], rstd.ap[:], rstd.ap[:], AF.Exp, scale=-0.5)
    return mv.ap[:, 0:1], rstd.ap[:]


P0_ITEMS = [("wkqv_fm", "wkqv_bf"), ("wproj_fm", "wproj_bf"),
            ("Wq", "wq_bf"), ("Wk", "wk_bf"), ("Wv", "wv_bf"),
            ("Wg", "wg_bf"), ("Wo", "wo_bf")]


def build_phase0(k, g, cfg, items=None, moe=True):
    items = items if items is not None else P0_ITEMS
    sc = ExitStack()
    CH = 8192
    bi = [k.sb(sc, f"p0_in{i}", [128, CH], F32) for i in range(2)]
    bo = [k.sb(sc, f"p0_out{i}", [128, CH], BF16) for i in range(2)]
    n = 0

    def cast_stream(pairs):
        nonlocal n
        for (sap, srcb, db, r0, c0, w) in pairs:
            i = n % 2
            k.dma("sp", [bi[i]], [srcb], bi[i].ap[:, 0:w], sap)
            k.op("dve", "tensor_copy", [bo[i]], [bi[i]], bo[i].ap[:, 0:w],
                 bi[i].ap[:, 0:w])
            k.dma("act", [db], [bo[i]], db.ap[r0:r0 + 128, c0:c0 + w],
                 bo[i].ap[:, 0:w])
            n += 1

    for srcn, dstn in items:
        sb = g[srcn]
        P, X = sb.ap.shape
        db = k.dram(dstn, [P, X], BF16)
        g[dstn] = db
        pairs = []
        for r0 in range(0, P, 128):
            for c0 in range(0, X, CH):
                w = min(CH, X - c0)
                pairs.append((sb.ap[r0:r0 + 128, c0:c0 + w], sb, db, r0, c0, w))
        cast_stream(pairs)
    if moe:
        w1v = g["W1"].ap[:].rearrange("r (c f) -> r c f", f=4096)
        w2v = g["W2"].ap[:].rearrange("r (c d) -> r c d", d=1024)
        for half in range(2):
            d1 = k.dram(f"w1{'ab'[half]}_bf", [1024, 8 * 2048], BF16)
            g[f"w1{'ab'[half]}_bf"] = d1
            pairs = []
            for r0 in range(0, 1024, 128):
                for ic in range(8):
                    pairs.append((w1v[r0:r0 + 128, ic, half * 2048:(half + 1) * 2048],
                                  g["W1"], d1, r0, ic * 2048, 2048))
            cast_stream(pairs)
        for half in range(2):
            d2 = k.dram(f"w2{'ab'[half]}_bf", [1024, 32 * 512], BF16)
            g[f"w2{'ab'[half]}_bf"] = d2
            pairs = []
            for r0 in range(0, 1024, 128):
                for hc0 in range(0, 32, 4):
                    sap = w2v[r0:r0 + 128, hc0:hc0 + 4, half * 512:(half + 1) * 512]
                    pairs.append((sap, g["W2"], d2, r0, hc0 * 512, 2048))
            cast_stream(pairs)
    sc.close()
    k.barrier()


def build_phase1(k, g, cfg):
    nc = k.nc
    B, T, D = cfg["B"], cfg["T"], cfg["D"]
    H, Dh = 16, 64
    NIC = D // 128
    NTB = T // 512
    sc = ExitStack()

    cb20, neg20 = k.constcol(-20.0)
    ident = k.sb(sc, "p1_ident", [128, 128], BF16)
    emit_identity(k, ident)
    triu = k.sb(sc, "p1_triu", [128, 128], BF16)
    k.op("pool", "memset", [triu], [], triu.ap[:], 1.0)
    k.op("pool", "affine_select", [triu], [triu],
         out=triu.ap[:], in_=triu.ap[:], compare_op=ALU.is_ge,
         fill=0.0, base=0, pattern=[[1, 128]], channel_multiplier=-1)

    wkqv = k.sb(sc, "p1_wkqv", [128, NIC * 3 * D], BF16)
    k.dma("sp", [wkqv], [g["wkqv_bf"]], wkqv.ap[:], g["wkqv_bf"].ap[:])
    wproj = k.sb(sc, "p1_wproj", [128, NIC * D], BF16)
    k.dma("sp", [wproj], [g["wproj_bf"]], wproj.ap[:], g["wproj_bf"].ap[:])
    bproj = k.sb(sc, "p1_bproj", [128, D], F32)

    qf = k.sb(sc, "p1_q", [128, NIC * T], BF16)
    kf = k.sb(sc, "p1_k", [128, NIC * T], BF16)
    vt = k.sb(sc, "p1_v", [128, (T // 128) * D], BF16)
    xin = k.sb(sc, "p1_xin", [128, D], F32)
    xnt = k.sb(sc, "p1_xnt", [128, D], BF16)
    xnf = k.sb(sc, "p1_xnf", [128, NIC * 512], BF16)
    stats = k.sb(sc, "p1_stats", [128, 12], F32)
    mv = k.sb(sc, "p1_mv", [128, 2], F32)
    rstd = k.sb(sc, "p1_rstd", [128, 1], F32)
    ctxf = xnf
    lparts = k.sb(sc, "p1_lparts", [128, 4 * 8], F32)
    ptb = k.sb(sc, "p1_ptb", [128, 16 * 512], BF16)
    lr = k.sb(sc, "p1_lr", [1, 2 * 512], F32)
    linv64 = k.sb(sc, "p1_linv64", [128, 512], F32)
    ones128b = k.sb(sc, "p1_o128", [128, 1], BF16)
    k.op("dve", "memset", [ones128b], [], ones128b.ap[:], 1.0)
    ones1f = k.sb(sc, "p1_o1f", [1, 128], F32)
    k.op("dve", "memset", [ones1f], [], ones1f.ap[:], 1.0)
    res = k.sb(sc, "p1_res", [128, D], F32)

    psA = [k.psum(sc, f"p1_psA{i}", [128, 512], F32) for i in range(2)]
    psC = k.psum(sc, "p1_psC", [128, 1024], F32)
    _psT0 = k.psum(sc, "p1_psT0", [128, 512], BF16)
    psT = [_psT0, _psT0]
    psO = k.psum(sc, "p1_psO", [128, 512], F32)
    psL = k.psum(sc, "p1_psL", [1, 512], F32)
    psLB = k.psum(sc, "p1_psLB", [128, 512], F32)

    sub = ExitStack()
    ones1 = k.sb(sub, "p1_ones1", [1, 128], F32)
    k.dma("sp", [res], [g["b_proj"]], res.ap[0:1, :], g["b_proj"].ap[0:1, :])
    k.op("dve", "memset", [ones1], [], ones1.ap[:], 1.0)
    for c in range(0, D, 512):
        k.mm(psA[0], [(psA[0].ap[:, 0:512], ones1.ap[:], res.ap[0:1, c:c + 512])],
             [res, ones1])
        k.op("dve", "tensor_copy", [bproj], [psA[0]],
             bproj.ap[:, c:c + 512], psA[0].ap[:, 0:512])
    k.barrier()
    sub.close()

    for b in range(B):
        for tb in range(NTB):
            t0 = b * T + tb * 512
            for j in range(4):
                k.dma("sp", [xin], [g["x0"]], xin.ap[:],
                      g["x0"].ap[t0 + j * 128: t0 + (j + 1) * 128, :])
                mu_ap, rstd_ap = emit_ln(k, mv, stats, rstd, xin, xin.ap[:], D)
                k.op("dve", "tensor_scalar", [xnt], [xin, mv, rstd],
                     xnt.ap[:], xin.ap[:], mu_ap, rstd_ap,
                     ALU.subtract, ALU.mult)
                for ic in range(NIC):
                    pT = psT[ic % 2]
                    k.transpose(pT, pT.ap[0:128, 0:128], xnt,
                                xnt.ap[:, ic * 128:(ic + 1) * 128], ident)
                    k.op("dve", "tensor_copy", [xnf], [pT],
                         xnf.ap[:, ic * 512 + j * 128: ic * 512 + (j + 1) * 128],
                         pT.ap[0:128, 0:128])
            for oc in range(2 * NIC):
                dst = kf if oc < NIC else qf
                oc_l = oc % NIC
                pA = psA[oc % 2]
                items = [(pA.ap[:],
                          wkqv.ap[:, ic * 3 * D + oc * 128: ic * 3 * D + (oc + 1) * 128],
                          xnf.ap[:, ic * 512:(ic + 1) * 512]) for ic in range(NIC)]
                k.mm(pA, items, [wkqv, xnf])
                k.op("dve", "tensor_copy", [dst], [pA],
                     dst.ap[:, oc_l * T + tb * 512: oc_l * T + (tb + 1) * 512], pA.ap[:])
            for j in range(4):
                for h2 in range(2):
                    items = [(psC.ap[:, h2 * 512:(h2 + 1) * 512],
                              xnf.ap[:, ic * 512 + j * 128: ic * 512 + (j + 1) * 128],
                              wkqv.ap[:, ic * 3 * D + 2 * D + h2 * 512:
                                      ic * 3 * D + 2 * D + (h2 + 1) * 512])
                             for ic in range(NIC)]
                    k.mm(psC, items, [wkqv, xnf], accum=(False))
                tt = tb * 4 + j
                k.op("dve", "tensor_copy", [vt], [psC],
                     vt.ap[:, tt * D:(tt + 1) * D], psC.ap[:])

        for qb in range(NTB):
            nkb = qb + 1
            ntile = 4 * nkb
            for h in range(H):
                hc, hr = h // 2, (h % 2) * 64
                for tkg in range(ntile):
                    diag = (tkg // 4 == qb)
                    c0 = (tkg % 4) * 128 if diag else 0
                    pA = psA[tkg % 2]
                    k.mm(pA, [(pA.ap[:, c0:512],
                               kf.ap[hr:hr + 64, hc * T + tkg * 128:
                                     hc * T + tkg * 128 + 128],
                               qf.ap[hr:hr + 64, hc * T + qb * 512 + c0:
                                     hc * T + qb * 512 + 512])],
                         [qf, kf])
                    if c0 > 0:
                        k.op("dve", "memset", [ptb], [],
                             ptb.ap[:, tkg * 512: tkg * 512 + c0], 0.0)
                    k.op("act", "activation", [ptb], [pA, cb20],
                         ptb.ap[:, tkg * 512 + c0: (tkg + 1) * 512],
                         pA.ap[:, c0:512], AF.Exp, scale=0.125, bias=neg20)
                    if diag:
                        k.op("dve", "tensor_tensor", [ptb], [ptb, triu],
                             ptb.ap[:, tkg * 512 + c0: tkg * 512 + c0 + 128],
                             ptb.ap[:, tkg * 512 + c0: tkg * 512 + c0 + 128],
                             triu.ap[:], ALU.mult)
                for tkg in range(ntile):
                    k.mm(psL, [(psL.ap[:], ones128b.ap[:],
                                ptb.ap[:, tkg * 512:(tkg + 1) * 512])],
                         [ptb, ones128b], accum=(tkg > 0), stop=(tkg == ntile - 1))
                hp = h % 2
                k.op("dve", "tensor_copy", [lr], [psL],
                     lr.ap[:, 0:512], psL.ap[:])
                for tkg in range(ntile):
                    k.mm(psO, [(psO.ap[hr:hr + 64, :],
                                vt.ap[:, tkg * D + h * Dh:
                                      tkg * D + (h + 1) * Dh],
                                ptb.ap[:, tkg * 512:(tkg + 1) * 512])],
                         [vt, ptb], accum=(tkg > 0), stop=(tkg == ntile - 1))
                k.op("act", "activation", [lr], [lr], lr.ap[:, 512:1024],
                     lr.ap[:, 0:512], AF.Ln)
                k.op("act", "activation", [lr], [lr], lr.ap[:, 512:1024],
                     lr.ap[:, 512:1024], AF.Exp, scale=-1.0)
                k.mm(psLB, [(psLB.ap[hr:hr + 64, :], ones1f.ap[:, 0:64],
                             lr.ap[:, 512:1024])], [lr, ones1f])
                if hp == 1:
                    k.op("dve", "tensor_copy", [linv64], [psLB],
                         linv64.ap[:], psLB.ap[:])
                    k.op("dve", "tensor_tensor", [ctxf], [psO, linv64],
                         ctxf.ap[:, hc * 512:(hc + 1) * 512],
                         psO.ap[:], linv64.ap[:], ALU.mult)
            t0 = b * T + qb * 512
            for j in range(4):
                for half in range(2):
                    items = [(psC.ap[:, half * 512:(half + 1) * 512],
                              ctxf.ap[:, ic * 512 + j * 128: ic * 512 + (j + 1) * 128],
                              wproj.ap[:, ic * D + half * 512: ic * D + (half + 1) * 512])
                             for ic in range(NIC)]
                    k.mm(psC, items, [ctxf, wproj])
                k.dma("sp", [xin], [g["x0"]], xin.ap[:],
                      g["x0"].ap[t0 + j * 128: t0 + (j + 1) * 128, :])
                k.op("dve", "tensor_tensor", [res], [psC, bproj],
                     res.ap[:], psC.ap[:], bproj.ap[:], ALU.add)
                k.op("dve", "tensor_tensor", [res], [res, xin],
                     res.ap[:], res.ap[:], xin.ap[:], ALU.add)
                k.dma("sp", [g["x1"]], [res],
                      g["x1"].ap[t0 + j * 128: t0 + (j + 1) * 128, :], res.ap[:])
    sc.close()
    k.barrier()


def build_phase2(k, g, cfg):
    nc = k.nc
    B, T, D = cfg["B"], cfg["T"], cfg["D"]
    E, Fh = 8, 4096
    NIC = D // 128
    NOC = Fh // 128
    NTOK = B * T
    NTB = NTOK // 512
    NT1 = NTOK // 512 + E
    NSLOT = NT1 * 512
    OOBV = NTOK
    sc = ExitStack()

    ident = k.sb(sc, "p2_ident", [128, 128], BF16)
    emit_identity(k, ident)

    xn2tok = k.dram("p2_xn2tok", [NTOK, D], BF16)
    ysel = k.dram("p2_ysel", [NTOK, D], F32)
    yscr = k.dram("p2_yscr", [NTOK, D], F32)
    tokof_d = k.dram("p2_tokof", [NSLOT, 1], U32)

    wr = k.sb(sc, "p2_wr", [128, NIC * E], F32)
    k.dma("sp", [wr], [g["Wr"]], wr.ap[:].rearrange("p (c e) -> p c e", e=E),
          g["Wr"].ap[:].rearrange("(c p) e -> p c e", p=128))
    brow = k.sb(sc, "p2_br", [E, 1], F32)
    k.dma("sp", [brow], [g["br"]], brow.ap[:], g["br"].ap[0:1, :].rearrange("o e -> e o"))
    maskf = k.sb(sc, "p2_maskf", [E, NTOK], BF16)
    gates = k.sb(sc, "p2_gates", [128, NTOK // 128], F32)
    wofs = k.sb(sc, "p2_wofs", [128, NT1], U32)
    b1ofs = k.sb(sc, "p2_b1ofs", [128, NT1], U32)

    sa = ExitStack()
    ones1 = k.sb(sa, "p2_ones1", [1, 128], F32)
    k.op("dve", "memset", [ones1], [], ones1.ap[:], 1.0)
    b2sb = k.sb(sa, "p2_b2", [E, D], BF16)
    k.dma("pool", [b2sb], [g["b2"]], b2sb.ap[:], g["b2"].ap[:])
    lngrow = k.sb(sa, "p2_lngr", [1, D], F32)
    k.dma("sp", [lngrow], [g["ln_g"]], lngrow.ap[:], g["ln_g"].ap[0:1, :])
    lnbrow = k.sb(sa, "p2_lnbr", [1, D], F32)
    k.dma("sp", [lnbrow], [g["ln_b"]], lnbrow.ap[:], g["ln_b"].ap[0:1, :])
    lng = k.sb(sa, "p2_lng", [128, D], F32)
    lnb = k.sb(sa, "p2_lnb", [128, D], F32)
    xin = k.sb(sa, "p2_xin", [128, 4 * D], F32)
    xnp = k.sb(sa, "p2_xnp", [128, 4 * D], F32)
    xnpf = k.sb(sa, "p2_xnpf", [128, NIC * 512], F32)
    xn2t = k.sb(sa, "p2_xn2t", [128, D], BF16)
    tmp = k.sb(sa, "p2_tmp", [128, D], F32)
    stats = k.sb(sa, "p2_stats", [128, 12], F32)
    mv = k.sb(sa, "p2_mv", [128, 2], F32)
    rstd = k.sb(sa, "p2_rstd", [128, 1], F32)
    lgf = k.sb(sa, "p2_lgf", [E, 512], F32)
    lt = k.sb(sa, "p2_lt", [128, E], F32)
    ssum = k.sb(sa, "p2_ssum", [128, 1], F32)
    eexp = k.sb(sa, "p2_eexp", [128, E], F32)
    m1 = k.sb(sa, "p2_m1", [128, 1], F32)
    negm = k.sb(sa, "p2_negm", [128, 1], F32)
    eq = k.sb(sa, "p2_eq", [128, E], F32)
    cum = k.sb(sa, "p2_cum", [128, E], F32)
    le1 = k.sb(sa, "p2_le1", [128, E], F32)
    mtm = k.sb(sa, "p2_mtm", [128, E], BF16)
    ysb = k.sb(sa, "p2_ysb", [128, D], F32)
    psT = k.psum(sa, "p2_psT", [128, 512], F32)
    psTb = k.psum(sa, "p2_psTb", [128, 512], BF16)
    psR = k.psum(sa, "p2_psR", [E, 512], F32)
    psY = k.psum(sa, "p2_psY", [128, 1024], F32)
    identf = k.sb(sa, "p2_identf", [128, 128], F32)
    emit_identity(k, identf)

    emit_bcast_row(k, lng, lngrow, psT, ones1, D)
    emit_bcast_row(k, lnb, lnbrow, psT, ones1, D)

    for tb in range(NTB):
        t0 = tb * 512
        k.dma("sp", [xin], [g["x1"]],
              xin.ap[:].rearrange("p (f d) -> p f d", d=D),
              g["x1"].ap[t0:t0 + 512, :].rearrange("(f p) d -> p f d", p=128))
        for j in range(4):
            x_ap = xin.ap[:, j * D:(j + 1) * D]
            mu_ap, rstd_ap = emit_ln(k, mv, stats, rstd, xin, x_ap, D)
            k.op("dve", "tensor_scalar", [xnp], [xin, mv, rstd],
                 xnp.ap[:, j * D:(j + 1) * D], x_ap, mu_ap, rstd_ap,
                 ALU.subtract, ALU.mult)
            for ic in range(NIC):
                k.transpose(psT, psT.ap[0:128, 0:128], xnp,
                            xnp.ap[:, j * D + ic * 128: j * D + (ic + 1) * 128], identf)
                k.op("dve", "tensor_copy", [xnpf], [psT],
                     xnpf.ap[:, ic * 512 + j * 128: ic * 512 + (j + 1) * 128],
                     psT.ap[0:128, 0:128])
            xp_ap = xnp.ap[:, j * D:(j + 1) * D]
            mu2, rstd2 = emit_ln(k, mv, stats, rstd, xnp, xp_ap, D)
            k.op("dve", "tensor_scalar", [tmp], [xnp, mv, rstd],
                 tmp.ap[:], xp_ap, mu2, rstd2, ALU.subtract, ALU.mult)
            k.op("dve", "tensor_tensor", [tmp], [tmp, lng],
                 tmp.ap[:], tmp.ap[:], lng.ap[:], ALU.mult)
            k.op("dve", "tensor_tensor", [xn2t], [tmp, lnb],
                 xn2t.ap[:], tmp.ap[:], lnb.ap[:], ALU.add)
            k.dma("sp", [xn2tok], [xn2t],
                  xn2tok.ap[t0 + j * 128: t0 + (j + 1) * 128, :], xn2t.ap[:])
        k.mm(psR, [(psR.ap[:], wr.ap[:, ic * E:(ic + 1) * E],
                    xnpf.ap[:, ic * 512:(ic + 1) * 512]) for ic in range(NIC)],
             [wr, xnpf])
        k.op("act", "activation", [lgf], [psR, brow],
             lgf.ap[:], psR.ap[:], AF.Identity, bias=brow.ap[:])
        for j in range(4):
            tt = t0 // 128 + j
            k.transpose(psT, psT.ap[0:128, 0:E], lgf,
                        lgf.ap[:, j * 128:(j + 1) * 128], identf)
            k.op("dve", "tensor_copy", [lt], [psT], lt.ap[:], psT.ap[0:128, 0:E])
            k.op("dve", "tensor_reduce", [m1], [lt], m1.ap[:], lt.ap[:], AX.X, ALU.max)
            k.op("dve", "tensor_scalar", [negm], [m1], negm.ap[:], m1.ap[:],
                 -1.0, None, ALU.mult)
            k.op("act", "activation", [eexp, ssum], [lt, negm],
                 eexp.ap[:], lt.ap[:], AF.Exp, bias=negm.ap[:],
                 accum_out=ssum.ap[:])
            k.op("dve", "reciprocal", [gates], [ssum],
                 gates.ap[:, tt:tt + 1], ssum.ap[:])
            k.op("dve", "tensor_scalar", [eq], [lt, m1], eq.ap[:], lt.ap[:],
                 m1.ap[:], None, ALU.is_equal)
            k.op("dve", "tensor_tensor_scan", [cum], [eq],
                 cum.ap[:], eq.ap[:], eq.ap[:], 0.0, ALU.add, ALU.bypass)
            k.op("dve", "tensor_scalar", [le1], [cum], le1.ap[:], cum.ap[:],
                 1.0, None, ALU.is_le)
            k.op("dve", "tensor_tensor", [mtm], [eq, le1], mtm.ap[:], eq.ap[:],
                 le1.ap[:], ALU.mult)
            k.transpose(psTb, psTb.ap[0:E, 0:128], mtm, mtm.ap[:], ident)
            k.op("dve", "tensor_copy", [maskf], [psTb],
                 maskf.ap[:, t0 + j * 128: t0 + (j + 1) * 128], psTb.ap[0:E, 0:128])
            for half in range(2):
                k.mm(psY, [(psY.ap[:, half * 512:(half + 1) * 512],
                            maskf.ap[:, t0 + j * 128: t0 + (j + 1) * 128],
                            b2sb.ap[:, half * 512:(half + 1) * 512])],
                     [maskf, b2sb])
            k.op("dve", "tensor_copy", [ysb], [psY], ysb.ap[:], psY.ap[:])
            k.dma("sp", [ysel], [ysb],
                  ysel.ap[t0 + j * 128: t0 + (j + 1) * 128, :], ysb.ap[:])

    mf32 = k.sb(sa, "p2_mf32", [E, NTOK], F32)
    ranks = k.sb(sa, "p2_ranks", [E, NTOK], F32)
    c8 = k.sb(sa, "p2_c8", [E, 1], F32)
    ntl = k.sb(sa, "p2_ntl", [E, 1], F32)
    t8 = k.sb(sa, "p2_t8", [E, 1], F32)
    row8 = k.sb(sa, "p2_row8", [1, 3 * E], F32)
    row24 = k.sb(sa, "p2_row24", [1, 2 * NT1], F32)
    iop = k.sb(sa, "p2_iop", [128, 1], I32)
    iopf = k.sb(sa, "p2_iopf", [128, 1], F32)
    ekbf = k.sb(sa, "p2_ekbf", [128, NT1], F32)
    NTI = NTOK // 128
    NN = max(NT1, NTI)
    i64r = k.sb(sa, "p2_i64r", [1, NN], I32)
    i64f = k.sb(sa, "p2_i64f", [1, NN], F32)
    tokid = k.sb(sa, "p2_tokid", [128, NN], F32)
    tokidu = k.sb(sa, "p2_tokidu", [128, NTI], U32)
    slotm = k.sb(sa, "p2_slotm", [128, NTI], F32)
    slotu = k.sb(sa, "p2_slotu", [128, NTI], U32)
    srow = k.sb(sa, "p2_srow", [1, 512], F32)
    ones8f = k.sb(sa, "p2_o8f", [8, 1], F32)
    oobt = k.sb(sa, "p2_oobt", [128, NSLOT // 128], U32)

    k.op("dve", "memset", [ones8f], [], ones8f.ap[:], 1.0)
    k.op("pool", "iota", [iop], [], iop.ap[:], pattern=[[0, 1]], base=0,
         channel_multiplier=1)
    k.op("dve", "tensor_copy", [iopf], [iop], iopf.ap[:], iop.ap[:])
    k.op("pool", "iota", [i64r], [], i64r.ap[:], pattern=[[1, NN]], base=0,
         channel_multiplier=0)
    k.op("dve", "tensor_copy", [i64f], [i64r], i64f.ap[:], i64r.ap[:])
    k.op("pool", "iota", [oobt], [], oobt.ap[:], pattern=[[1, NSLOT // 128]],
         base=OOBV, channel_multiplier=NSLOT // 128,
         allow_small_or_imprecise_dtypes=True)
    k.dma("pool", [tokof_d], [oobt],
          tokof_d.ap[:].rearrange("(p i) o -> p (i o)", p=128), oobt.ap[:])

    k.op("dve", "tensor_copy", [mf32], [maskf], mf32.ap[:], maskf.ap[:])
    k.op("dve", "tensor_tensor_scan", [ranks], [mf32],
         ranks.ap[:], mf32.ap[:], mf32.ap[:], 0.0, ALU.add, ALU.bypass)
    k.op("dve", "tensor_copy", [c8], [ranks], c8.ap[:], ranks.ap[:, NTOK - 1:NTOK])
    k.op("dve", "tensor_single_scalar", [ntl], [c8], ntl.ap[:], c8.ap[:],
         0.0, ALU.is_gt)
    for kk in range(1, NTOK // 512):
        k.op("dve", "tensor_single_scalar", [t8], [c8], t8.ap[:], c8.ap[:],
             float(512 * kk), ALU.is_gt)
        k.op("dve", "tensor_tensor", [ntl], [ntl, t8], ntl.ap[:], ntl.ap[:],
             t8.ap[:], ALU.add)
    k.transpose(psT, psT.ap[0:1, 0:E], ntl, ntl.ap[:], identf)
    k.op("dve", "tensor_copy", [row8], [psT], row8.ap[:, 0:E], psT.ap[0:1, 0:E])
    k.op("dve", "tensor_tensor_scan", [row8], [row8],
         row8.ap[:, E:2 * E], row8.ap[:, 0:E], row8.ap[:, 0:E],
         0.0, ALU.add, ALU.bypass)
    k.op("dve", "tensor_tensor", [row8], [row8],
         row8.ap[:, E:2 * E], row8.ap[:, E:2 * E], row8.ap[:, 0:E], ALU.subtract)
    iota24 = i64f.ap[:, 0:NT1]
    k.op("dve", "tensor_scalar", [row24], [i64f, row8],
         row24.ap[:, 0:NT1], iota24, row8.ap[:, E:E + 1], None, ALU.is_ge)
    for e in range(1, E):
        k.op("dve", "tensor_scalar", [row24], [i64f, row8],
             row24.ap[:, NT1:2 * NT1], iota24, row8.ap[:, E + e:E + e + 1],
             None, ALU.is_ge)
        k.op("dve", "tensor_tensor", [row24], [row24],
             row24.ap[:, 0:NT1], row24.ap[:, 0:NT1], row24.ap[:, NT1:2 * NT1],
             ALU.add)
    k.op("dve", "tensor_scalar", [row24], [row24], row24.ap[:, 0:NT1],
         row24.ap[:, 0:NT1], -1.0, None, ALU.add)
    k.mm(psT, [(psT.ap[:, 0:NT1], ones1.ap[:], row24.ap[:, 0:NT1])],
         [row24, ones1])
    k.op("dve", "tensor_copy", [ekbf], [psT], ekbf.ap[:], psT.ap[:, 0:NT1])
    k.op("dve", "scalar_tensor_tensor", [tokid], [ekbf, iopf], tokid.ap[:, 0:NT1],
         ekbf.ap[:], 128.0, iopf.ap[:].broadcast_to((128, NT1)), ALU.mult, ALU.add)
    k.op("dve", "tensor_copy", [wofs], [tokid], wofs.ap[:], tokid.ap[:, 0:NT1])
    k.op("dve", "scalar_tensor_tensor", [tokid], [ekbf, iopf], tokid.ap[:, 0:NT1],
         ekbf.ap[:], 32.0, iopf.ap[:].broadcast_to((128, NT1)), ALU.mult, ALU.add)
    k.op("dve", "tensor_copy", [b1ofs], [tokid], b1ofs.ap[:], tokid.ap[:, 0:NT1])
    k.transpose(psT, psT.ap[0:E, 0:1], row8, row8.ap[:, E:2 * E].rearrange(
        "o e -> o e") if False else row8.ap[0:1, E:2 * E], identf)
    k.op("dve", "tensor_copy", [t8], [psT], t8.ap[:], psT.ap[0:E, 0:1])
    k.op("dve", "tensor_scalar", [t8], [t8], t8.ap[:], t8.ap[:], 512.0, -1.0,
         ALU.mult, ALU.add)
    k.op("dve", "tensor_scalar", [ranks], [ranks, t8], ranks.ap[:], ranks.ap[:],
         t8.ap[:], None, ALU.add)
    k.op("dve", "tensor_tensor", [ranks], [ranks, mf32], ranks.ap[:],
         ranks.ap[:], mf32.ap[:], ALU.mult)
    for ch in range(NTOK // 512):
        k.mm(psR, [(psR.ap[0:1, :], ones8f.ap[:],
                    ranks.ap[:, ch * 512:(ch + 1) * 512])], [ranks, ones8f])
        k.op("dve", "tensor_copy", [srow], [psR], srow.ap[:], psR.ap[0:1, :])
        for u in range(4):
            k.transpose(psT, psT.ap[0:128, 0:1], srow,
                        srow.ap[:, u * 128:(u + 1) * 128], identf)
            k.op("dve", "tensor_copy", [slotm], [psT],
                 slotm.ap[:, ch * 4 + u: ch * 4 + u + 1], psT.ap[0:128, 0:1])
    k.op("pool", "iota", [tokid], [], tokid.ap[:, 0:NTI], pattern=[[128, NTI]],
         base=0, channel_multiplier=1, allow_small_or_imprecise_dtypes=True)
    k.op("dve", "tensor_copy", [tokidu], [tokid], tokidu.ap[:], tokid.ap[:, 0:NTI])
    k.op("dve", "tensor_copy", [slotu], [slotm], slotu.ap[:], slotm.ap[:])
    for i in range(NTI):
        k.idma([tokof_d], [tokidu, slotu], tokof_d.ap[:],
               slotu.ap[:, i:i + 1], tokidu.ap[:, i:i + 1], None,
               skip_waw=(i > 0))
    sa.close()
    k.barrier()

    sb_ = ExitStack()
    w1h = [k.sb(sb_, f"p2_w1h{i}", [128, NIC * 2048], BF16) for i in range(2)]
    w2h = [k.sb(sb_, f"p2_w2h{i}", [128, NOC * 512], BF16) for i in range(2)]
    b1row = k.sb(sb_, "p2_b1r", [32, 128], F32)
    b1sb = k.sb(sb_, "p2_b1", [128, NOC], F32)
    identf2 = k.sb(sb_, "p2_identf2", [128, 128], F32)
    emit_identity(k, identf2)
    tkof = k.sb(sb_, "p2_tkof", [128, 4], U32)
    tkj = [k.sb(sb_, f"p2_tkj{j}", [128, 1], U32) for j in range(4)]
    xgt = k.sb(sb_, "p2_xgt", [128, 4 * D], BF16)
    xgf = k.sb(sb_, "p2_xgf", [128, NIC * 512], BF16)
    hm = k.sb(sb_, "p2_hm", [128, NOC * 512], BF16)
    yev = [k.sb(sb_, f"p2_yev{i}", [128, D], F32) for i in range(2)]
    psH = [k.psum(sb_, f"p2_psH{i}", [128, 512], F32) for i in range(2)]
    psYh = [k.psum(sb_, f"p2_psYh{i}", [128, 512], F32) for i in range(2)]
    psM = k.psum(sb_, "p2_psM", [128, 512], F32)
    psTb2 = k.psum(sb_, "p2_psTb2", [128, 512], BF16)

    breg = nc.gpsimd.to_reg(NTOK - 1)
    b1v = g["b1"].ap[:].rearrange("e (c p) -> (e c) p", p=128)

    def gather_w(kt):
        k.idma([w1h[0]], [wofs], w1h[0].ap[:], None,
               g["w1a_bf"].ap[:], wofs.ap[:, kt:kt + 1])
        k.idma([w1h[1]], [wofs], w1h[1].ap[:], None,
               g["w1b_bf"].ap[:], wofs.ap[:, kt:kt + 1])
        k.idma([w2h[0]], [wofs], w2h[0].ap[:], None,
               g["w2a_bf"].ap[:], wofs.ap[:, kt:kt + 1])
        k.idma([w2h[1]], [wofs], w2h[1].ap[:], None,
               g["w2b_bf"].ap[:], wofs.ap[:, kt:kt + 1])
        k.idma([b1row], [b1ofs], b1row.ap[:], None, b1v,
               b1ofs.ap[0:32, kt:kt + 1])

    dbg_xg = k.dram("p2_dbg_xg", [NT1, 128, 4 * D], BF16) if cfg.get("P2_DBG") else None
    if cfg.get("P2_DBG"):
        dbg_w = k.dram("p2_dbg_w", [128, NT1], U32)
        k.dma("sp", [dbg_w], [wofs], dbg_w.ap[:], wofs.ap[:])
        dbg_b = k.dram("p2_dbg_b", [128, NT1], U32)
        k.dma("sp", [dbg_b], [b1ofs], dbg_b.ap[:], b1ofs.ap[:])
        dbg_hm = k.dram("p2_dbg_hm", [NT1, 128, NOC * 512], BF16)
        dbg_w1 = k.dram("p2_dbg_w1", [NT1, 128, 4096], BF16)
        dbg_y0 = k.dram("p2_dbg_y0", [NT1 * 4, D], F32)
        dbg_tk = k.dram("p2_dbg_tk", [NT1, 128, 4], U32)
    for kt in range(cfg.get("P2_TILES", NT1)):
        k.dma("sp", [tkof], [tokof_d], tkof.ap[:],
              tokof_d.ap[kt * 512:(kt + 1) * 512, :]
              .rearrange("(p j) o -> p (j o)", j=4))
        gather_w(kt)
        k.transpose(psM, psM.ap[0:128, 0:NOC], b1row, b1row.ap[0:32, :], identf2)
        k.op("dve", "tensor_copy", [b1sb], [psM], b1sb.ap[:], psM.ap[0:128, 0:NOC])
        for j in range(4):
            k.op("dve", "tensor_copy", [tkj[j]], [tkof], tkj[j].ap[:],
                 tkof.ap[:, j:j + 1])
            k.idma([xgt], [xn2tok, tkj[j]], xgt.ap[:, j * D:(j + 1) * D], None,
                   xn2tok.ap[:], tkj[j].ap[:],
                   bounds_check=breg, oob_is_err=False)
            for ic in range(NIC):
                k.transpose(psTb2, psTb2.ap[0:128, 0:128], xgt,
                            xgt.ap[:, j * D + ic * 128: j * D + (ic + 1) * 128], ident)
                k.op("dve", "tensor_copy", [xgf], [psTb2],
                     xgf.ap[:, ic * 512 + j * 128: ic * 512 + (j + 1) * 128],
                     psTb2.ap[0:128, 0:128])
        if dbg_xg is not None:
            k.dma("sp", [dbg_xg], [xgt], dbg_xg.ap[kt], xgt.ap[:])
            if kt > 0:
                k.dma("sp", [dbg_hm], [hm], dbg_hm.ap[kt - 1], hm.ap[:])
                k.dma("sp", [dbg_w1], [w1h[0]], dbg_w1.ap[kt - 1], w1h[0].ap[:, 0:4096])
        for oc in range(NOC):
            hh = oc // 16
            ocl = oc % 16
            pH = psH[oc % 2]
            k.mm(pH, [(pH.ap[:],
                       w1h[hh].ap[:, ic * 2048 + ocl * 128: ic * 2048 + (ocl + 1) * 128],
                       xgf.ap[:, ic * 512:(ic + 1) * 512]) for ic in range(NIC)],
                 [w1h[hh], xgf])
            k.op("act", "activation", [hm], [pH, b1sb],
                 hm.ap[:, oc * 512:(oc + 1) * 512], pH.ap[:], AF.Relu,
                 bias=b1sb.ap[:, oc:oc + 1])
        if dbg_xg is not None:
            k.dma("sp", [dbg_tk], [tkof], dbg_tk.ap[kt], tkof.ap[:])
        for j in range(4):
            for half in range(2):
                pY = psYh[half]
                k.mm(pY, [(pY.ap[:],
                           hm.ap[:, hc * 512 + j * 128: hc * 512 + (j + 1) * 128],
                           w2h[half].ap[:, hc * 512:(hc + 1) * 512])
                          for hc in range(NOC)],
                     [hm, w2h[half]])
                ev = "dve" if half == 0 else "act"
                if ev == "dve":
                    k.op("dve", "tensor_copy", [yev[j % 2]], [pY],
                         yev[j % 2].ap[:, half * 512:(half + 1) * 512], pY.ap[:])
                else:
                    k.op("act", "copy", [yev[j % 2]], [pY],
                         yev[j % 2].ap[:, half * 512:(half + 1) * 512], pY.ap[:])
            k.idma([yscr], [yev[j % 2], tkj[j]], yscr.ap[:],
                   tkj[j].ap[:], yev[j % 2].ap[:], None,
                   bounds_check=breg, oob_is_err=False, skip_waw=False)

    sb_.close()
    k.barrier()

    gates_d = k.dram("p2_gates_d", [128, NTOK // 128], F32)
    k.dma("sp", [gates_d], [gates], gates_d.ap[:], gates.ap[:])
    g["ysel"] = ysel
    g["yscr"] = yscr
    g["gates_d"] = gates_d
    sc.close()
    k.barrier()


def build_phase3(k, g, cfg):
    nc = k.nc
    B, T, D = cfg["B"], cfg["T"], cfg["D"]
    HG, DK, DV = 4, 1024, 2048
    HK, HV = DK // HG, DV // HG
    C = 64
    NIC = D // 128
    NTB = T // 512
    NTOK = B * T
    NCH = T // C
    GN = 16.0
    sc = ExitStack()

    ident = k.sb(sc, "p3_ident", [128, 128], BF16)
    emit_identity(k, ident)
    identf = k.sb(sc, "p3_identf", [128, 128], F32)
    emit_identity(k, identf)
    triu64 = k.sb(sc, "p3_triu", [64, 64], mybir.dt.int8)
    k.op("pool", "memset", [triu64], [], triu64.ap[:], 1)
    k.op("pool", "affine_select", [triu64], [triu64],
         out=triu64.ap[:], in_=triu64.ap[:], compare_op=ALU.is_ge,
         fill=0, base=0, pattern=[[1, 64]], channel_multiplier=-1)
    z64 = k.sb(sc, "p3_z64", [64, 64], F32)
    k.op("pool", "memset", [z64], [], z64.ap[:], 0.0)
    ones1 = k.sb(sc, "p3_ones1", [1, 128], F32)
    k.op("dve", "memset", [ones1], [], ones1.ap[:], 1.0)

    xnf_scr = k.dram("p3_xnf", [B * NTB, 128, NIC * 512], BF16)
    qt_scr = k.dram("p3_qt", [B * NTB, 128, NIC * 512], BF16)
    ki_scr = k.dram("p3_ki", [B * NTB, 128, NIC * 512], BF16)
    khe_scr = k.dram("p3_khe", [NTOK, DK], BF16)
    v_scr = k.dram("p3_v", [NTOK, DV], BF16)
    o_scr = k.dram("p3_o", [NTOK, DV], BF16)
    Dsb = k.sb(sc, "p3_D", [128, NIC * (NTOK // C // B) * B], F32)

    gkw1 = k.sb(sc, "p3_gkw1", [128, NIC * 16], BF16)
    k.dma("pool", [gkw1], [g["gk_w1"]],
          gkw1.ap[:].rearrange("p (c e) -> p c e", e=16),
          g["gk_w1"].ap[:].rearrange("(c p) e -> p c e", p=128))
    gkw2 = k.sb(sc, "p3_gkw2", [16, DK], BF16)
    k.dma("pool", [gkw2], [g["gk_w2"]], gkw2.ap[:], g["gk_w2"].ap[:])
    gkbrow = k.sb(sc, "p3_gkbr", [1, DK], F32)
    k.dma("sp", [gkbrow], [g["gk_b"]], gkbrow.ap[:], g["gk_b"].ap[0:1, :])
    rmsrow = k.sb(sc, "p3_rmsr", [1, HV], F32)
    k.dma("sp", [rmsrow], [g["rms_w"]], rmsrow.ap[:], g["rms_w"].ap[0:1, :])
    rmsw = k.sb(sc, "p3_rmsw", [128, HV], F32)

    sa = ExitStack()
    wq = k.sb(sa, "p3_wq", [128, NIC * DK], BF16)
    wk_ = k.sb(sa, "p3_wk", [128, NIC * DK], BF16)
    wv = k.sb(sa, "p3_wv", [128, NIC * DV], BF16)
    k.dma("sp", [wq], [g["wq_bf"]], wq.ap[:], g["wq_bf"].ap[:])
    k.dma("sp", [wk_], [g["wk_bf"]], wk_.ap[:], g["wk_bf"].ap[:])
    k.dma("sp", [wv], [g["wv_bf"]], wv.ap[:], g["wv_bf"].ap[:])
    xin = k.sb(sa, "p3_xin", [128, 4 * D], F32)
    yse = k.sb(sa, "p3_yse", [128, 4 * D], F32)
    ysc_ = k.sb(sa, "p3_ysc", [128, 4 * D], F32)
    gt3 = k.sb(sa, "p3_gt", [128, B * T // 128], F32)
    xnt = k.sb(sa, "p3_xnt", [128, D], BF16)
    xnf = k.sb(sa, "p3_xnf_sb", [128, NIC * 512], BF16)
    stats = k.sb(sa, "p3_stats", [128, 12], F32)
    mv = k.sb(sa, "p3_mv", [128, 2], F32)
    rstd = k.sb(sa, "p3_rstd", [128, 1], F32)
    qtf = k.sb(sa, "p3_qtf", [128, NIC * 512], BF16)
    kif = k.sb(sa, "p3_kif", [128, NIC * 512], BF16)
    kef = k.sb(sa, "p3_kef", [128, 512], BF16)
    ket = k.sb(sa, "p3_ket", [64, 8 * 1024], BF16)
    vtm = k.sb(sa, "p3_vtm", [128, 4 * DV], BF16)
    midb = k.sb(sa, "p3_mid", [16, 512], BF16)
    glf = k.sb(sa, "p3_glf", [128, 512], F32)
    gf = k.sb(sa, "p3_gf", [128, 512], F32)
    cumg = k.sb(sa, "p3_cumg", [128, 512], F32)
    lam = k.sb(sa, "p3_lam", [128, 512], F32)
    enc = k.sb(sa, "p3_enc", [128, 512], F32)
    t1 = k.sb(sa, "p3_t1", [128, 512], F32)
    t2 = k.sb(sa, "p3_t2", [128, 512], F32)
    dcol = k.sb(sa, "p3_dcol", [128, 8], F32)
    gkb = k.sb(sa, "p3_gkb", [128, NIC], F32)
    gkbr2 = k.sb(sa, "p3_gkbr2", [NIC, 128], F32)
    psT = k.psum(sa, "p3_psT", [128, 512], F32)
    psTb = k.psum(sa, "p3_psTb", [128, 512], BF16)
    psA = k.psum(sa, "p3_psA", [128, 512], F32)
    psV = k.psum(sa, "p3_psV", [128, 1024], F32)
    psM = k.psum(sa, "p3_psM", [16, 512], F32)

    k.dma("sp", [gkbr2], [g["gk_b"]],
          gkbr2.ap[:], g["gk_b"].ap[0:1, :].rearrange("o (c p) -> (o c) p", p=128))
    k.transpose(psT, psT.ap[0:128, 0:NIC], gkbr2, gkbr2.ap[:], identf)
    k.op("dve", "tensor_copy", [gkb], [psT], gkb.ap[:], psT.ap[0:128, 0:NIC])
    emit_bcast_row(k, rmsw, rmsrow, psT, ones1, HV)
    cbe, eps_col = k.constcol(1e-5)
    cb1, one_col = k.constcol(1.0)

    k.dma("sp", [gt3], [g["gates_d"]], gt3.ap[:], g["gates_d"].ap[:])
    for b in range(B):
        for tb in range(NTB):
            gtb = b * NTB + tb
            t0 = b * T + tb * 512
            k.dma("sp", [xin], [g["x1"]],
                  xin.ap[:].rearrange("p (f d) -> p f d", d=D),
                  g["x1"].ap[t0:t0 + 512, :].rearrange("(f p) d -> p f d", p=128))
            k.dma("sp", [yse], [g["ysel"]],
                  yse.ap[:].rearrange("p (f d) -> p f d", d=D),
                  g["ysel"].ap[t0:t0 + 512, :].rearrange("(f p) d -> p f d", p=128))
            k.dma("sp", [ysc_], [g["yscr"]],
                  ysc_.ap[:].rearrange("p (f d) -> p f d", d=D),
                  g["yscr"].ap[t0:t0 + 512, :].rearrange("(f p) d -> p f d", p=128))
            for j in range(4):
                ti = t0 // 128 + j
                k.op("dve", "tensor_tensor", [yse], [yse, ysc_],
                     yse.ap[:, j * D:(j + 1) * D], yse.ap[:, j * D:(j + 1) * D],
                     ysc_.ap[:, j * D:(j + 1) * D], ALU.add)
                k.op("dve", "tensor_scalar", [yse], [yse, gt3],
                     yse.ap[:, j * D:(j + 1) * D], yse.ap[:, j * D:(j + 1) * D],
                     gt3.ap[:, ti:ti + 1], None, ALU.mult)
                k.op("dve", "tensor_tensor", [xin], [yse, xin],
                     xin.ap[:, j * D:(j + 1) * D], yse.ap[:, j * D:(j + 1) * D],
                     xin.ap[:, j * D:(j + 1) * D], ALU.add)
                k.dma("sp", [g["x2"]], [xin],
                      g["x2"].ap[t0 + j * 128: t0 + (j + 1) * 128, :],
                      xin.ap[:, j * D:(j + 1) * D])
                x_ap = xin.ap[:, j * D:(j + 1) * D]
                mu_ap, rstd_ap = emit_ln(k, mv, stats, rstd, xin, x_ap, D)
                k.op("dve", "tensor_scalar", [xnt], [xin, mv, rstd],
                     xnt.ap[:], x_ap, mu_ap, rstd_ap,
                     ALU.subtract, ALU.mult)
                for ic in range(NIC):
                    k.transpose(psTb, psTb.ap[0:128, 0:128], xnt,
                                xnt.ap[:, ic * 128:(ic + 1) * 128], ident)
                    k.op("dve", "tensor_copy", [xnf], [psTb],
                         xnf.ap[:, ic * 512 + j * 128: ic * 512 + (j + 1) * 128],
                         psTb.ap[0:128, 0:128])
            k.dma("sp", [xnf_scr], [xnf], xnf_scr.ap[gtb], xnf.ap[:])
            for j in range(4):
                for q4 in range(4):
                    half = q4 % 2
                    k.mm(psV, [(psV.ap[:, half * 512:(half + 1) * 512],
                                xnf.ap[:, ic * 512 + j * 128: ic * 512 + (j + 1) * 128],
                                wv.ap[:, ic * DV + q4 * 512: ic * DV + (q4 + 1) * 512])
                               for ic in range(NIC)], [wv, xnf])
                    if half == 1:
                        k.op("dve", "tensor_copy", [vtm], [psV],
                             vtm.ap[:, j * DV + (q4 - 1) * 512: j * DV + (q4 + 1) * 512],
                             psV.ap[:])
            k.dma("sp", [v_scr], [vtm],
                  v_scr.ap[t0:t0 + 512, :].rearrange("(f p) d -> p f d", p=128),
                  vtm.ap[:].rearrange("p (f d) -> p f d", d=DV))
            k.mm(psM, [(psM.ap[:], gkw1.ap[:, ic * 16:(ic + 1) * 16],
                        xnf.ap[:, ic * 512:(ic + 1) * 512]) for ic in range(NIC)],
                 [gkw1, xnf])
            k.op("dve", "tensor_copy", [midb], [psM], midb.ap[:], psM.ap[:])
            for pc in range(NIC):
                k.mm(psA, [(psA.ap[:], gkw2.ap[:, pc * 128:(pc + 1) * 128],
                            midb.ap[:])], [gkw2, midb])
                k.op("act", "activation", [glf], [psA, gkb],
                     glf.ap[:], psA.ap[:], AF.Identity, bias=gkb.ap[:, pc:pc + 1])
                k.op("act", "activation", [t1], [glf], t1.ap[:], glf.ap[:], AF.Abs)
                k.op("act", "activation", [t1], [t1], t1.ap[:], t1.ap[:], AF.Exp,
                     scale=-1.0)
                k.op("act", "activation", [t1], [t1, cb1], t1.ap[:], t1.ap[:], AF.Ln,
                     bias=one_col)
                k.op("dve", "tensor_single_scalar", [t2], [glf], t2.ap[:], glf.ap[:],
                     0.0, ALU.min)
                k.op("dve", "tensor_tensor", [t2], [t2, t1], t2.ap[:], t2.ap[:],
                     t1.ap[:], ALU.subtract)
                k.op("dve", "tensor_scalar", [gf], [t2], gf.ap[:],
                     t2.ap[:], 1.0 / GN, None, ALU.mult)
                for c in range(8):
                    k.op("dve", "tensor_tensor_scan", [cumg], [gf],
                         cumg.ap[:, c * C:(c + 1) * C],
                         gf.ap[:, c * C:(c + 1) * C],
                         gf.ap[:, c * C:(c + 1) * C],
                         0.0, ALU.add, ALU.bypass)
                cg = cumg.ap[:, 0:512]
                k.op("dve", "tensor_copy", [dcol], [cumg], dcol.ap[:],
                     cumg.ap[:, C - 1: 512: C])
                k.op("act", "activation", [Dsb], [dcol],
                     Dsb.ap[:, pc * (B * NCH) + b * NCH + tb * 8:
                            pc * (B * NCH) + b * NCH + tb * 8 + 8],
                     dcol.ap[:], AF.Exp)
                k.op("act", "activation", [lam], [cumg], lam.ap[:], cg, AF.Exp)
                k.mm(psA, [(psA.ap[:], wq.ap[:, ic * DK + pc * 128: ic * DK + (pc + 1) * 128],
                            xnf.ap[:, ic * 512:(ic + 1) * 512]) for ic in range(NIC)],
                     [wq, xnf])
                k.op("dve", "scalar_tensor_tensor", [qtf], [psA, lam],
                     qtf.ap[:, pc * 512:(pc + 1) * 512], psA.ap[:], 0.0625,
                     lam.ap[:], ALU.mult, ALU.mult)
                k.mm(psA, [(psA.ap[:], wk_.ap[:, ic * DK + pc * 128: ic * DK + (pc + 1) * 128],
                            xnf.ap[:, ic * 512:(ic + 1) * 512]) for ic in range(NIC)],
                     [wk_, xnf])
                k.op("act", "activation", [enc], [cumg], enc.ap[:], cg, AF.Exp,
                     scale=-1.0)
                k.op("dve", "tensor_tensor", [kif], [psA, enc],
                     kif.ap[:, pc * 512:(pc + 1) * 512], psA.ap[:], enc.ap[:], ALU.mult)
                for c in range(8):
                    k.op("dve", "tensor_scalar", [t1], [cumg],
                         t1.ap[:, c * C:(c + 1) * C], cg[:, c * C:(c + 1) * C],
                         cg[:, c * C + C - 1: c * C + C], None, ALU.subtract)
                k.op("act", "activation", [t1], [t1], t1.ap[:], t1.ap[:], AF.Exp,
                     scale=-1.0)
                k.op("dve", "tensor_tensor", [kef], [psA, t1],
                     kef.ap[:], psA.ap[:], t1.ap[:], ALU.mult)
                for c in range(8):
                    k.transpose(psTb, psTb.ap[0:64, 0:128], kef,
                                kef.ap[:, c * C:(c + 1) * C], ident)
                    k.op("dve", "tensor_copy", [ket], [psTb],
                         ket.ap[0:64, c * DK + pc * 128: c * DK + (pc + 1) * 128],
                         psTb.ap[0:64, 0:128])
            k.dma("sp", [qt_scr], [qtf], qt_scr.ap[gtb], qtf.ap[:])
            k.dma("sp", [ki_scr], [kif], ki_scr.ap[gtb], kif.ap[:])
            k.dma("sp", [khe_scr], [ket],
                  khe_scr.ap[t0:t0 + 512, :].rearrange("(c p) d -> p c d", p=64),
                  ket.ap[:].rearrange("p (c d) -> p c d", d=DK))
    sa.close()
    k.barrier()

    sb_ = ExitStack()
    SBT = 1024
    NSB = T // SBT
    NCB = SBT // C
    NS2 = 2
    qtb = [k.sb(sb_, f"p3b_qt{s}", [128, 2 * SBT], BF16) for s in range(NS2)]
    kib = [k.sb(sb_, f"p3b_ki{s}", [128, 2 * SBT], BF16) for s in range(NS2)]
    keb = [k.sb(sb_, f"p3b_ke{s}", [64, NCB * HK], BF16) for s in range(NS2)]
    vb = [k.sb(sb_, f"p3b_v{s}", [64, NCB * HV], BF16) for s in range(NS2)]
    onb = [k.sb(sb_, f"p3b_on{s}", [64, NCB * HV], BF16) for s in range(NS2)]
    Sm = [k.sb(sb_, f"p3b_Sm{s}", [128, 2 * HV], F32) for s in range(NS2)]
    Sbf = [k.sb(sb_, f"p3b_Sbf{s}", [128, 2 * HV], BF16) for s in range(NS2)]
    stb = [k.sb(sb_, f"p3b_st{s}", [64, 64], BF16) for s in range(NS2)]
    sqs = [k.sb(sb_, f"p3b_sqs{s}", [64, HV], BF16) for s in range(NS2)]
    ssq = [k.sb(sb_, f"p3b_ssq{s}", [64, 1], F32) for s in range(NS2)]
    rso = [k.sb(sb_, f"p3b_rso{s}", [64, 1], F32) for s in range(NS2)]
    psST = [k.psum(sb_, f"p3b_psST{s}", [64, 64], F32) for s in range(NS2)]
    psO = [k.psum(sb_, f"p3b_psO{s}", [64, 512], F32) for s in range(NS2)]
    psU = [k.psum(sb_, f"p3b_psU{s}", [128, 1024], F32) for s in range(NS2)]

    def sb3_load(s, h, b, sb2):
        base_t = b * T + sb2 * SBT
        for i, tbl in enumerate(range(sb2 * 2, sb2 * 2 + 2)):
            gtb = b * NTB + tbl
            k.dma("sp", [qtb[s]], [qt_scr],
                  qtb[s].ap[:].rearrange("p (q t) -> p q t", q=2)[:, :, i * 512:(i + 1) * 512],
                  qt_scr.ap[gtb].rearrange("p (q t) -> p q t", q=NIC)[:, 2 * h:2 * h + 2, :])
            k.dma("sp", [kib[s]], [ki_scr],
                  kib[s].ap[:].rearrange("p (q t) -> p q t", q=2)[:, :, i * 512:(i + 1) * 512],
                  ki_scr.ap[gtb].rearrange("p (q t) -> p q t", q=NIC)[:, 2 * h:2 * h + 2, :])
        k.dma("sp", [keb[s]], [khe_scr],
              keb[s].ap[:].rearrange("p (c d) -> p c d", d=HK),
              khe_scr.ap[base_t:base_t + SBT, h * HK:(h + 1) * HK]
              .rearrange("(c p) d -> p c d", p=64))
        k.dma("sp", [vb[s]], [v_scr],
              vb[s].ap[:].rearrange("p (c d) -> p c d", d=HV),
              v_scr.ap[base_t:base_t + SBT, h * HV:(h + 1) * HV]
              .rearrange("(c p) d -> p c d", p=64))

    def sb3_chunk(s, h, b, sb2, c):
        tof = c * C
        gch = b * NCH + sb2 * NCB + c
        q0 = qtb[s].ap[:, tof:tof + C]
        q1 = qtb[s].ap[:, SBT + tof:SBT + tof + C]
        k0 = kib[s].ap[:, tof:tof + C]
        k1 = kib[s].ap[:, SBT + tof:SBT + tof + C]
        k.mm(psST[s], [(psST[s].ap[:], k0, q0), (psST[s].ap[:], k1, q1)],
             [qtb[s], kib[s]])
        k.op("dve", "select", [stb[s]], [psST[s], triu64, z64],
             stb[s].ap[:], triu64.ap[:], psST[s].ap[:], z64.ap[:],
             add_drain=True)
        k.mm(psO[s], [(psO[s].ap[:], q0, Sbf[s].ap[0:128, 0:HV]),
                      (psO[s].ap[:], q1, Sbf[s].ap[0:128, HV:2 * HV]),
                      (psO[s].ap[:], stb[s].ap[:],
                       vb[s].ap[0:64, c * HV:(c + 1) * HV])],
             [qtb[s], Sbf[s], stb[s], vb[s]])
        k.op("act", "activation", [sqs[s], ssq[s]], [psO[s]],
             sqs[s].ap[:], psO[s].ap[:], AF.Square, accum_out=ssq[s].ap[:])
        k.op("act", "activation", [rso[s]], [ssq[s], k.cpool],
             rso[s].ap[:], ssq[s].ap[:], AF.Ln, bias=eps_col[0:64, :],
             scale=1.0 / HV)
        k.op("act", "activation", [rso[s]], [rso[s]], rso[s].ap[:], rso[s].ap[:],
             AF.Exp, scale=-0.5)
        k.op("dve", "scalar_tensor_tensor", [onb[s]], [psO[s], rso[s], rmsw],
             onb[s].ap[:, c * HV:(c + 1) * HV], psO[s].ap[:], rso[s].ap[:],
             rmsw.ap[0:64, :], ALU.mult, ALU.mult)
        k.mm(psU[s], [(psU[s].ap[:, 0:HV], keb[s].ap[:, c * HK: c * HK + 128],
                       vb[s].ap[0:64, c * HV:(c + 1) * HV])], [keb[s], vb[s]])
        k.mm(psU[s], [(psU[s].ap[:, HV:2 * HV],
                       keb[s].ap[:, c * HK + 128: c * HK + 256],
                       vb[s].ap[0:64, c * HV:(c + 1) * HV])], [keb[s], vb[s]])
        for half in range(2):
            k.op("dve", "scalar_tensor_tensor", [Sm[s]], [Sm[s], Dsb, psU[s]],
                 Sm[s].ap[:, half * HV:(half + 1) * HV],
                 Sm[s].ap[:, half * HV:(half + 1) * HV],
                 Dsb.ap[:, (2 * h + half) * (B * NCH) + gch:
                        (2 * h + half) * (B * NCH) + gch + 1],
                 psU[s].ap[:, half * HV:(half + 1) * HV], ALU.mult, ALU.add)
        k.op("dve", "tensor_copy", [Sbf[s]], [Sm[s]], Sbf[s].ap[:], Sm[s].ap[:])

    def sb3_store(s, h, b, sb2):
        base_t = b * T + sb2 * SBT
        k.dma("sp", [o_scr], [onb[s]],
              o_scr.ap[base_t:base_t + SBT, h * HV:(h + 1) * HV]
              .rearrange("(c p) d -> p c d", p=64),
              onb[s].ap[:].rearrange("p (c d) -> p c d", d=HV))

    for b in range(B):
        for hp in range(HG // 2):
            h0, h1 = 2 * hp, 2 * hp + 1
            for s, h in ((0, h0), (1, h1)):
                k.op("dve", "memset", [Sm[s]], [], Sm[s].ap[:], 0.0)
                k.op("dve", "memset", [Sbf[s]], [], Sbf[s].ap[:], 0.0)
            for sb2 in range(NSB):
                sb3_load(0, h0, b, sb2)
                sb3_load(1, h1, b, sb2)
                for c in range(NCB):
                    sb3_chunk(0, h0, b, sb2, c)
                    sb3_chunk(1, h1, b, sb2, c)
                sb3_store(0, h0, b, sb2)
                sb3_store(1, h1, b, sb2)
    sb_.close()
    k.barrier()

    sc3 = ExitStack()
    wgs = k.sb(sc3, "p3c_wg", [128, NIC * DV], BF16)
    wos = k.sb(sc3, "p3c_wo", [128, (DV // 128) * D], BF16)
    k.dma("sp", [wgs], [g["wg_bf"]], wgs.ap[:], g["wg_bf"].ap[:])
    k.dma("sp", [wos], [g["wo_bf"]], wos.ap[:], g["wo_bf"].ap[:])
    xnfb = k.sb(sc3, "p3c_xnf", [128, NIC * 512], BF16)
    ob = k.sb(sc3, "p3c_ob", [128, 4 * DV], BF16)
    ogt = k.sb(sc3, "p3c_ogt", [128, DV], BF16)
    ogf = k.sb(sc3, "p3c_ogf", [128, (DV // 128) * 512], BF16)
    xres = k.sb(sc3, "p3c_xres", [128, 4 * D], F32)
    out3 = k.sb(sc3, "p3c_out", [128, D], F32)
    psG = k.psum(sc3, "p3c_psG", [128, 1024], F32)
    psY = k.psum(sc3, "p3c_psY", [128, 1024], F32)
    psTb2 = k.psum(sc3, "p3c_psTb", [128, 512], BF16)

    for b in range(B):
        for tb in range(NTB):
            gtb = b * NTB + tb
            t0 = b * T + tb * 512
            k.dma("sp", [xnfb], [xnf_scr], xnfb.ap[:], xnf_scr.ap[gtb])
            k.dma("sp", [ob], [o_scr],
                  ob.ap[:].rearrange("p (f d) -> p f d", d=DV),
                  o_scr.ap[t0:t0 + 512, :].rearrange("(f p) d -> p f d", p=128))
            k.dma("sp", [xres], [g["x2"]],
                  xres.ap[:].rearrange("p (f d) -> p f d", d=D),
                  g["x2"].ap[t0:t0 + 512, :].rearrange("(f p) d -> p f d", p=128))
            for j in range(4):
                for q4 in range(4):
                    half = q4 % 2
                    k.mm(psG, [(psG.ap[:, half * 512:(half + 1) * 512],
                                xnfb.ap[:, ic * 512 + j * 128: ic * 512 + (j + 1) * 128],
                                wgs.ap[:, ic * DV + q4 * 512: ic * DV + (q4 + 1) * 512])
                               for ic in range(NIC)], [wgs, xnfb])
                    if half == 1:
                        for hh in range(2):
                            k.op("act", "activation", [ogt], [psG],
                                 ogt.ap[:, (q4 - 1 + hh) * 512:(q4 + hh) * 512],
                                 psG.ap[:, hh * 512:(hh + 1) * 512], AF.Silu)
                k.op("dve", "tensor_tensor", [ogt], [ogt, ob],
                     ogt.ap[:], ogt.ap[:], ob.ap[:, j * DV:(j + 1) * DV], ALU.mult)
                for ic in range(DV // 128):
                    k.transpose(psTb2, psTb2.ap[0:128, 0:128], ogt,
                                ogt.ap[:, ic * 128:(ic + 1) * 128], ident)
                    k.op("dve", "tensor_copy", [ogf], [psTb2],
                         ogf.ap[:, ic * 512 + j * 128: ic * 512 + (j + 1) * 128],
                         psTb2.ap[0:128, 0:128])
            for j in range(4):
                for half in range(2):
                    k.mm(psY, [(psY.ap[:, half * 512:(half + 1) * 512],
                                ogf.ap[:, ic * 512 + j * 128: ic * 512 + (j + 1) * 128],
                                wos.ap[:, ic * D + half * 512: ic * D + (half + 1) * 512])
                               for ic in range(DV // 128)], [ogf, wos])
                k.op("dve", "tensor_tensor", [out3], [psY, xres],
                     out3.ap[:], psY.ap[:], xres.ap[:, j * D:(j + 1) * D], ALU.add)
                k.dma("sp", [g["x3"]], [out3],
                      g["x3"].ap[t0 + j * 128: t0 + (j + 1) * 128, :], out3.ap[:])
    sc3.close()
    sc.close()
    k.barrier()


def emit_quant(k, e, x_buf, x_ap, w, bufs):
    u, cc, G, sbc = bufs["u"], bufs["cc"], bufs["G"], bufs["sb"]
    sg, cb, gb = bufs["sg"], bufs["cb"], bufs["gb"]
    nb = w // 16
    c3 = cc.ap[:, 0:w].rearrange("p (n c) -> p n c", c=16)
    s3 = sbc.ap[:, 0:w // 2].rearrange("p (n c) -> p n c", c=8)
    k.op("act", "activation", [cc], [x_buf], cc.ap[:, 0:w], x_ap, AF.Abs)
    k.op("dve", "tensor_tensor", [sbc], [cc], s3[:, :, 0:8], c3[:, :, 0:8],
         c3[:, :, 8:16], ALU.max)
    k.op("dve", "tensor_tensor", [sbc], [sbc], s3[:, :, 0:4], s3[:, :, 0:4],
         s3[:, :, 4:8], ALU.max)
    k.op("dve", "tensor_tensor", [sbc], [sbc], s3[:, :, 0:2], s3[:, :, 0:2],
         s3[:, :, 2:4], ALU.max)
    k.op("dve", "tensor_tensor", [sbc], [sbc], s3[:, :, 0:1], s3[:, :, 0:1],
         s3[:, :, 1:2], ALU.max)
    k.op("dve", "tensor_scalar", [sbc], [sbc],
         sbc.ap[:, w // 2: w // 2 + nb], sbc.ap[:, 0:w // 2][:, 0::8],
         1.0 / 6.0, 1e-12, ALU.mult, ALU.max)
    k.op("dve", "reciprocal", [sbc], [sbc],
         sbc.ap[:, w // 2 + nb: w // 2 + 2 * nb],
         sbc.ap[:, w // 2: w // 2 + nb])
    k.op("dve", "tensor_copy", [gb], [sbc], gb.ap[:, 0:nb],
         sbc.ap[:, w // 2: w // 2 + nb])
    s_b = gb.ap[:, 0:nb].rearrange("p (n o) -> p n o", o=1).broadcast_to((128, nb, 16))
    rs_b = sbc.ap[:, w // 2 + nb: w // 2 + 2 * nb].rearrange(
        "p (n o) -> p n o", o=1).broadcast_to((128, nb, 16))
    k.op("dve", "tensor_tensor", [u], [cc, sbc],
         u.ap[:, 0:w].rearrange("p (n c) -> p n c", c=16), c3, rs_b, ALU.mult)
    uap = u.ap[:, 0:w]
    k.op("dve", "tensor_single_scalar", [G], [u], G.ap[:, 0:w], uap, 0.25, ALU.is_gt)
    k.op("dve", "tensor_single_scalar", [cb], [u], cb.ap[:, 0:w], uap, 0.75, ALU.is_gt)
    k.op("dve", "tensor_single_scalar", [sg], [u], sg.ap[:, 0:w], uap, 1.25, ALU.is_gt)
    k.op("dve", "tensor_tensor", [G], [G, cb], G.ap[:, 0:w], G.ap[:, 0:w],
         cb.ap[:, 0:w], ALU.add)
    k.op("dve", "tensor_single_scalar", [cb], [u], cb.ap[:, 0:w], uap, 1.75, ALU.is_gt)
    k.op("dve", "tensor_tensor", [sg], [sg, cb], sg.ap[:, 0:w], sg.ap[:, 0:w],
         cb.ap[:, 0:w], ALU.add)
    k.op("dve", "tensor_tensor", [G], [G, sg], G.ap[:, 0:w], G.ap[:, 0:w],
         sg.ap[:, 0:w], ALU.add)
    k.op("dve", "tensor_scalar", [G], [G], G.ap[:, 0:w], G.ap[:, 0:w], 0.5,
         None, ALU.mult)
    cbA, m25 = k.constcol(-2.5)
    cbB, m35 = k.constcol(-3.5)
    cbC, m50 = k.constcol(-5.0)
    k.op("act", "activation", [sg], [u, cbA], sg.ap[:, 0:w], uap, AF.Sign,
         bias=m25)
    k.op("act", "activation", [cb], [u, cbB], cb.ap[:, 0:w], uap, AF.Sign,
         bias=m35)
    k.op("dve", "tensor_tensor", [sg], [sg, cb], sg.ap[:, 0:w], sg.ap[:, 0:w],
         cb.ap[:, 0:w], ALU.add)
    k.op("act", "activation", [cb], [u, cbC], cb.ap[:, 0:w], uap, AF.Sign,
         bias=m50)
    k.op("dve", "scalar_tensor_tensor", [sg], [sg, cb], sg.ap[:, 0:w],
         sg.ap[:, 0:w], 0.5, cb.ap[:, 0:w], ALU.mult, ALU.add)
    k.op("dve", "tensor_tensor", [G], [G, sg], G.ap[:, 0:w], G.ap[:, 0:w],
         sg.ap[:, 0:w], ALU.add)
    k.op("dve", "tensor_scalar", [G], [G], G.ap[:, 0:w], G.ap[:, 0:w], 2.0,
         None, ALU.add)
    k.op("act", "activation", [cb], [x_buf], cb.ap[:, 0:w], x_ap, AF.Sign)
    k.op("dve", "tensor_tensor", [G], [G, cb], G.ap[:, 0:w], G.ap[:, 0:w],
         cb.ap[:, 0:w], ALU.mult)
    k.op("dve", "tensor_tensor", [bufs["out_buf"]], [G, gb],
         bufs["out_ap"].rearrange("p (n c) -> p n c", c=16),
         G.ap[:, 0:w].rearrange("p (n c) -> p n c", c=16), s_b, ALU.mult)


def build_phase4(k, g, cfg):
    nc = k.nc
    B, T, D = cfg["B"], cfg["T"], cfg["D"]
    I = 2816
    NIC = D // 128
    NHC = I // 128
    NTB = B * T // 512
    HW = I // 2
    sc = ExitStack()

    ident = k.sb(sc, "p4_ident", [128, 128], BF16)
    emit_identity(k, ident)

    wgate = k.sb(sc, "p4_wg", [128, NIC * I], BF16)
    wup = k.sb(sc, "p4_wu", [128, NIC * I], BF16)
    wdown = k.sb(sc, "p4_wd", [128, NHC * D], BF16)

    I4 = 2816
    u = k.sb(sc, "p4_u", [128, I4], F32)
    ccb = k.sb(sc, "p4_cc", [128, I4], F32)
    G = k.sb(sc, "p4_G", [128, I4], BF16)
    sbc = k.sb(sc, "p4_sb", [128, I4 // 2 + 2 * (I4 // 16)], F32)
    sgb = k.sb(sc, "p4_sg", [128, I4], BF16)
    cbb = k.sb(sc, "p4_cb", [128, I4], BF16)
    gbb = k.sb(sc, "p4_gb", [128, I4 // 16], BF16)
    qb = {"u": u, "cc": ccb, "G": G, "sb": sbc, "sg": sgb, "cb": cbb,
          "gb": gbb}

    sa = ExitStack()
    wrow = k.sb(sa, "p4a_wrow", [128, I], F32)
    wq_t = k.sb(sa, "p4a_wqt", [128, I], BF16)
    psTb = k.psum(sa, "p4a_psT", [128, 512], BF16)
    for wsrc, wdst in ((g["w_gate"], wgate), (g["w_up"], wup)):
        for r in range(NHC):
            k.dma("sp", [wrow], [wsrc], wrow.ap[:, 0:D],
                  wsrc.ap[r * 128:(r + 1) * 128, :])
            e = "dve" if r % 2 == 0 else "pool"
            emit_quant(k, e, wrow, wrow.ap[:, 0:D], D,
                       dict(qb, out_buf=wq_t, out_ap=wq_t.ap[:, 0:D]))
            for ic in range(NIC):
                k.transpose(psTb, psTb.ap[0:128, 0:128], wq_t,
                            wq_t.ap[:, ic * 128:(ic + 1) * 128], ident)
                k.op("dve", "tensor_copy", [wdst], [psTb],
                     wdst.ap[:, ic * I + r * 128: ic * I + (r + 1) * 128],
                     psTb.ap[0:128, 0:128])
    for r in range(NIC):
        k.dma("sp", [wrow], [g["w_down"]], wrow.ap[:],
              g["w_down"].ap[r * 128:(r + 1) * 128, :])
        for half in range(2):
            e = "dve" if half == 0 else "pool"
            emit_quant(k, e, wrow, wrow.ap[:, half * HW:(half + 1) * HW], HW,
                       dict(qb, out_buf=wq_t,
                            out_ap=wq_t.ap[:, half * HW:(half + 1) * HW]))
        for hc in range(NHC):
            k.transpose(psTb, psTb.ap[0:128, 0:128], wq_t,
                        wq_t.ap[:, hc * 128:(hc + 1) * 128], ident)
            k.op("dve", "tensor_copy", [wdown], [psTb],
                 wdown.ap[:, hc * D + r * 128: hc * D + (r + 1) * 128],
                 psTb.ap[0:128, 0:128])
    sa.close()
    k.barrier()

    sb_ = ExitStack()
    xin = k.sb(sb_, "p4_xin", [128, D], F32)
    xqf = k.sb(sb_, "p4_xqf", [128, NIC * 128], BF16)
    xqfv = [Buf(xqf.ap[:], "p4_xqf_a"), Buf(xqf.ap[:], "p4_xqf_b")]
    hh = k.sb(sb_, "p4_hh", [128, I], F32)
    hqf = k.sb(sb_, "p4_hqf", [128, 22 * 128], BF16)
    hqfv = [Buf(hqf.ap[:], "p4_hqf_a"), Buf(hqf.ap[:], "p4_hqf_b")]
    stats = k.sb(sb_, "p4_stats", [128, 12], F32)
    mv = k.sb(sb_, "p4_mv", [128, 2], F32)
    rstd = k.sb(sb_, "p4_rstd", [128, 1], F32)
    out4 = k.sb(sb_, "p4_out", [128, D], F32)
    psG = k.psum(sb_, "p4_psG", [128, 1408], F32)
    psY = k.psum(sb_, "p4_psY", [128, 1024], F32)
    psT2l = [k.psum(sb_, f"p4_psT2{i}", [128, 512], BF16) for i in range(2)]

    for tb in range(NTB):
        for j in range(4):
            ti = tb * 4 + j
            t0 = ti * 128
            k.dma("sp", [xin], [g["x3"]], xin.ap[:], g["x3"].ap[t0:t0 + 128, :])
            mu_ap, rstd_ap = emit_ln(k, mv, stats, rstd, xin, xin.ap[:], D)
            k.op("dve", "tensor_scalar", [hh], [xin, mv, rstd],
                 hh.ap[:, 0:D], xin.ap[:], mu_ap, rstd_ap, ALU.subtract, ALU.mult)
            emit_quant(k, "dve", hh, hh.ap[:, 0:D], D,
                       dict(qb, out_buf=sgb, out_ap=sgb.ap[:, 0:D]))
            for ic in range(NIC):
                pT = psT2l[ic % 2]
                k.transpose(pT, pT.ap[0:128, 0:128], sgb,
                            sgb.ap[:, ic * 128:(ic + 1) * 128], ident)
                k.op("dve", "tensor_copy", [xqfv[ic % 2]], [pT],
                     xqf.ap[:, ic * 128:(ic + 1) * 128],
                     pT.ap[0:128, 0:128])
            for half in range(2):
                cols = [(0, 512), (512, 1024), (1024, 1408)]
                for wsb, is_up in ((wgate, False), (wup, True)):
                    for (c0, c1) in cols:
                        k.mm(psG, [(psG.ap[:, c0:c1],
                                    xqf.ap[:, ic * 128:(ic + 1) * 128],
                                    wsb.ap[:, ic * I + half * HW + c0:
                                           ic * I + half * HW + c1])
                                   for ic in range(NIC)],
                             [wsb, xqfv[0], xqfv[1]])
                    if not is_up:
                        for (c0, c1) in cols:
                            k.op("act", "activation", [hh], [psG],
                                 hh.ap[:, half * HW + c0: half * HW + c1],
                                 psG.ap[:, c0:c1], AF.Silu)
                    else:
                        k.op("dve", "tensor_tensor", [hh], [psG, hh],
                             hh.ap[:, half * HW:(half + 1) * HW], psG.ap[:],
                             hh.ap[:, half * HW:(half + 1) * HW], ALU.mult)
            emit_quant(k, "dve", hh, hh.ap[:], I,
                       dict(qb, out_buf=sgb, out_ap=sgb.ap[:, 0:I]))
            for hc in range(22):
                pT = psT2l[hc % 2]
                k.transpose(pT, pT.ap[0:128, 0:128], sgb,
                            sgb.ap[:, hc * 128:(hc + 1) * 128], ident)
                k.op("dve", "tensor_copy", [hqfv[hc % 2]], [pT],
                     hqf.ap[:, hc * 128:(hc + 1) * 128],
                     pT.ap[0:128, 0:128])
            for colh in range(2):
                k.mm(psY, [(psY.ap[:, colh * 512:(colh + 1) * 512],
                            hqf.ap[:, hc2 * 128:(hc2 + 1) * 128],
                            wdown.ap[:, hc2 * D + colh * 512:
                                     hc2 * D + (colh + 1) * 512])
                           for hc2 in range(22)],
                     [hqfv[0], hqfv[1], wdown])
            k.dma("sp", [xin], [g["x3"]], xin.ap[:], g["x3"].ap[t0:t0 + 128, :])
            k.op("dve", "tensor_tensor", [out4], [psY, xin],
                 out4.ap[:], psY.ap[:], xin.ap[:], ALU.add)
            k.dma("sp", [g["y"]], [out4], g["y"].ap[t0:t0 + 128, :], out4.ap[:])
    sb_.close()
    sc.close()
    k.barrier()


def layout_l0_attn(w_kqv, w_proj):
    w = np.asarray(w_kqv).reshape(16, 64, 3, 1024)
    wk = w[:, :, 0, :].reshape(1024, 1024)
    wq = w[:, :, 1, :].reshape(1024, 1024)
    wv = w[:, :, 2, :].reshape(1024, 1024)
    wkqv_fm = np.concatenate([wk.T, wq.T, wv.T], axis=1)
    wproj_fm = np.ascontiguousarray(np.asarray(w_proj).T)
    return np.ascontiguousarray(wkqv_fm), wproj_fm


def sbready(w_fm):
    K, N = w_fm.shape
    return np.ascontiguousarray(
        np.asarray(w_fm).reshape(K // 128, 128, N).transpose(1, 0, 2)
        .reshape(128, (K // 128) * N))


CFG = {"B": 4, "T": 2048, "D": 1024}
FUSED = True
NCORES = 8

_P1_W = [("wkqv_fm", [128, 8 * 3072]), ("wproj_fm", [128, 8 * 1024]),
         ("b_proj", [1, 1024])]
_P2_W = [("Wr", [1024, 8]), ("br", [1, 8]), ("ln_g", [1, 1024]), ("ln_b", [1, 1024]),
         ("W1", [8 * 128, 8 * 4096]), ("b1", [8, 4096]), ("W2", [8 * 128, 32 * 1024]),
         ("b2", [8, 1024])]
_P3_W = [("Wq", [128, 8 * 1024]), ("Wk", [128, 8 * 1024]), ("Wv", [128, 8 * 2048]),
         ("Wg", [128, 8 * 2048]), ("gk_w1", [1024, 16]), ("gk_w2", [16, 1024]),
         ("gk_b", [1, 1024]), ("Wo", [128, 16 * 1024]), ("rms_w", [1, 512])]
_P4_W = [("w_gate", [2816, 1024]), ("w_up", [2816, 1024]), ("w_down", [1024, 2816])]


def _weights_from_inputs(inputs):
    wkqv_fm, wproj_fm = layout_l0_attn(inputs["L0_attn_w_kqv"], inputs["L0_attn_w_proj"])
    r = lambda a: np.ascontiguousarray(np.asarray(a, np.float32).reshape(1, -1))
    c = lambda a: np.ascontiguousarray(np.asarray(a, np.float32))
    W1 = np.asarray(inputs["L0_ffn_W1"], np.float32)
    W2 = np.asarray(inputs["L0_ffn_W2"], np.float32)
    w1r = np.ascontiguousarray(
        W1.reshape(8, 8, 128, 4096).transpose(0, 2, 1, 3).reshape(8 * 128, 8 * 4096))
    w2r = np.ascontiguousarray(
        W2.reshape(8, 32, 128, 1024).transpose(0, 2, 1, 3).reshape(8 * 128, 32 * 1024))
    return {
        "wkqv_fm": sbready(wkqv_fm.astype(np.float32)),
        "wproj_fm": sbready(wproj_fm.astype(np.float32)),
        "b_proj": r(inputs["L0_attn_b_proj"]),
        "Wr": c(inputs["L0_ffn_Wr"]), "br": r(inputs["L0_ffn_br"]),
        "ln_g": r(inputs["L0_ffn_ln_g"]), "ln_b": r(inputs["L0_ffn_ln_b"]),
        "W1": w1r, "b1": c(inputs["L0_ffn_b1"]),
        "W2": w2r, "b2": c(inputs["L0_ffn_b2"]),
        "Wq": sbready(c(inputs["L1_attn_Wq"])), "Wk": sbready(c(inputs["L1_attn_Wk"])),
        "Wv": sbready(c(inputs["L1_attn_Wv"])), "Wg": sbready(c(inputs["L1_attn_Wg"])),
        "gk_w1": c(inputs["L1_attn_gk_w1"]), "gk_w2": c(inputs["L1_attn_gk_w2"]),
        "gk_b": r(inputs["L1_attn_gk_b"]), "Wo": sbready(c(inputs["L1_attn_Wo"])),
        "rms_w": r(inputs["L1_attn_rms_w"]),
        "w_gate": c(inputs["L1_ffn_w_gate"]), "w_up": c(inputs["L1_ffn_w_up"]),
        "w_down": c(inputs["L1_ffn_w_down"]),
    }


def _run_phase(build_fn, wspecs, in_name, out_name, xs, wmap):
    NT = CFG["B"] * CFG["T"]
    nc = bass.Bass("TRN2", target_bir_lowering=False)
    k = K(nc)
    g = {in_name: k.dram(in_name, [NT, CFG["D"]], F32, kind="ExternalInput"),
         out_name: k.dram(out_name, [NT, CFG["D"]], F32, kind="ExternalOutput")}
    for n, s in wspecs:
        g[n] = k.dram(n, s, F32, kind="ExternalInput")
    build_fn(k, g, CFG)
    k.finish_program()
    k.es.close()
    nc.finalize()
    in_maps = []
    for c in range(NCORES):
        m = {in_name: xs[c]}
        for n, _ in wspecs:
            m[n] = wmap[n]
        in_maps.append(m)
    res = run_bass_kernel_spmd(nc, in_maps, core_ids=list(range(NCORES)))
    return [r[out_name] for r in res.results]


def _run_fused(xs, wmap):
    NT = CFG["B"] * CFG["T"]
    D = CFG["D"]
    nc = bass.Bass("TRN2", target_bir_lowering=False)
    k = K(nc)
    g = {"x0": k.dram("x0", [NT, D], F32, kind="ExternalInput"),
         "y": k.dram("y", [NT, D], F32, kind="ExternalOutput"),
         "x1": k.dram("x1", [NT, D], F32, kind="Internal"),
         "x2": k.dram("x2", [NT, D], F32, kind="Internal"),
         "x3": k.dram("x3", [NT, D], F32, kind="Internal")}
    for n, s in _P1_W + _P2_W + _P3_W + _P4_W:
        g[n] = k.dram(n, s, F32, kind="ExternalInput")
    build_phase0(k, g, CFG)
    build_phase1(k, g, CFG)
    build_phase2(k, g, CFG)
    build_phase3(k, g, CFG)
    build_phase4(k, g, CFG)
    k.finish_program()
    k.es.close()
    nc.finalize()
    in_maps = []
    for c in range(NCORES):
        m = {"x0": xs[c]}
        for n, _ in _P1_W + _P2_W + _P3_W + _P4_W:
            m[n] = wmap[n]
        in_maps.append(m)
    res = run_bass_kernel_spmd(nc, in_maps, core_ids=list(range(NCORES)))
    return [r["y"] for r in res.results]


def kernel(**inputs):
    x = np.asarray(inputs["L0_attn_x"], np.float32)
    D = CFG["D"]
    xs = [np.ascontiguousarray(x[4 * c:4 * c + 4].reshape(-1, D)) for c in range(NCORES)]
    wmap = _weights_from_inputs(inputs)
    ys = _run_fused(xs, wmap)
    out = np.concatenate([y.reshape(4, CFG["T"], D) for y in ys], axis=0)
    return out.astype(np.float32)


# version 25 (final)
# speedup vs baseline: 1.2504x; 1.0418x over previous
import math
from contextlib import ExitStack

import numpy as np
import concourse.bass as bass
import concourse.mybir as mybir
from concourse.bass_utils import run_bass_kernel_spmd

F32 = mybir.dt.float32
BF16 = mybir.dt.bfloat16
AF = mybir.ActivationFunctionType
ALU = mybir.AluOpType
AX = mybir.AxisListType
U8 = mybir.dt.uint8
U32 = mybir.dt.uint32
I32 = mybir.dt.int32


class Buf:
    __slots__ = ("ap", "writes", "reads", "name", "dsem", "dqueue")

    def __init__(self, ap, name=""):
        self.ap = ap
        self.writes = {}
        self.reads = {}
        self.name = name
        self.dsem = None
        self.dqueue = None


class K:
    def __init__(self, nc):
        self.nc = nc
        self.eng = {
            "pe": nc.tensor, "dve": nc.vector, "act": nc.scalar,
            "pool": nc.gpsimd, "sp": nc.sync,
        }
        self.es = ExitStack()
        self.sem = {}
        self.cnt = {}
        for t in ("pe", "dve", "act", "pool"):
            self.sem[t] = self.es.enter_context(nc.semaphore(t))
            self.cnt[t] = 0
        self.memo = {}
        self.drained = {}
        t = self.es.enter_context(nc.sbuf_tensor("k_cpool", [128, 16], F32))
        self.cpool = Buf(t.ap(), "k_cpool")
        self.cvals = {}

    def _dsem_for(self, buf, queue):
        if buf.dsem is None:
            tag = "d_" + buf.name
            buf.dsem = tag
            buf.dqueue = queue
            self.sem[tag] = self.es.enter_context(self.nc.semaphore(tag))
            self.cnt[tag] = 0
        else:
            assert buf.dqueue == queue, \
                f"buffer {buf.name} DMA'd from both HWDGE and SWDGE queues"
        return buf.dsem

    def constcol(self, val):
        if val not in self.cvals:
            i = len(self.cvals)
            self.cvals[val] = i
            self.op("pool", "memset", [self.cpool], [],
                    self.cpool.ap[:, i:i + 1], float(val))
        i = self.cvals[val]
        return self.cpool, self.cpool.ap[:, i:i + 1]

    def sb(self, scope, name, shape, dtype):
        t = scope.enter_context(self.nc.sbuf_tensor(name, shape, dtype))
        return Buf(t.ap(), name)

    def psum(self, scope, name, shape, dtype=F32):
        t = scope.enter_context(self.nc.psum_tensor(name, shape, dtype))
        return Buf(t.ap(), name)

    def dram(self, name, shape, dtype, kind="Internal"):
        t = self.nc.dram_tensor(name, shape, dtype, kind=kind)
        return Buf(t.ap(), name)

    def _wait_deps(self, etag, outs, ins, skip_tag=None):
        deps = {}
        for b in ins:
            for t, v in b.writes.items():
                if deps.get(t, 0) < v:
                    deps[t] = v
        for b in outs:
            for src in (b.writes, b.reads):
                for t, v in src.items():
                    if t == skip_tag:
                        continue
                    if deps.get(t, 0) < v:
                        deps[t] = v
        eng = self.eng[etag]
        need_drain = False
        for t, v in deps.items():
            if t == etag:
                if etag == "pe":
                    continue
                if self.drained.get(etag, 0) < v:
                    need_drain = True
            else:
                if t not in ("pe", "dve", "act", "pool"):
                    v = self.cnt[t]
                if self.memo.get((etag, t), 0) < v:
                    eng.wait_ge(self.sem[t], v)
                    self.memo[(etag, t)] = v
        if need_drain:
            eng.drain()
            self.drained[etag] = self.cnt[etag]

    def _finish(self, etag, inst, outs, ins, tag, step):
        self.cnt[tag] += step
        inst.then_inc(self.sem[tag], step)
        tok_v = self.cnt[tag]
        for b in outs:
            if step == 16 and tag == b.dsem:
                b.writes[tag] = tok_v
            else:
                b.writes = {tag: tok_v}
            b.reads = {}
        for b in ins:
            b.reads[tag] = tok_v
        return (tag, tok_v)

    def barrier(self):
        for etag in ("pe", "dve", "act", "pool", "sp"):
            eng = self.eng[etag]
            for t in self.sem:
                v = self.cnt[t]
                if v and t != etag and self.memo.get((etag, t), 0) < v:
                    eng.wait_ge(self.sem[t], v)
                    self.memo[(etag, t)] = v
            if etag in ("dve", "act", "pool"):
                eng.drain()
                self.drained[etag] = self.cnt[etag]

    def finish_program(self):
        for t in list(self.sem):
            v = self.cnt[t]
            if v and t != "sp" and self.memo.get(("sp", t), 0) < v:
                self.eng["sp"].wait_ge(self.sem[t], v)
                self.memo[("sp", t)] = v

    def op(self, etag, name, outs, ins, *args, **kw):
        self._wait_deps(etag, outs, ins)
        inst = getattr(self.eng[etag], name)(*args, **kw)
        return self._finish(etag, inst, outs, ins, etag, 1)

    def idma(self, outs, ins, out_ap, out_off, in_ap, in_off, skip_waw=True, **kw):
        from concourse.bass import IndirectOffsetOnAxis
        tag = self._dsem_for(outs[0], "sw")
        self._wait_deps("pool", outs, ins, skip_tag=tag if skip_waw else None)
        inst = self.eng["pool"].indirect_dma_start(
            out_ap,
            IndirectOffsetOnAxis(out_off, 0) if out_off is not None else None,
            in_ap,
            IndirectOffsetOnAxis(in_off, 0) if in_off is not None else None,
            **kw)
        return self._finish("pool", inst, outs, ins, tag, 16)

    def dma(self, etag, outs, ins, out_ap, in_ap, **kw):
        queue = "hw" if etag in ("sp", "act") else "sw"
        tag = self._dsem_for(outs[0], queue)
        self._wait_deps(etag, outs, ins, skip_tag=tag)
        inst = self.eng[etag].dma_start(out=out_ap, in_=in_ap, **kw)
        return self._finish(etag, inst, outs, ins, tag, 16)

    def mm(self, psum_buf, items, ins, accum=False, stop=True):
        self._wait_deps("pe", [psum_buf], ins)
        n = len(items)
        inst = None
        for i, (ps_ap, lhsT, rhs) in enumerate(items):
            inst = self.nc.tensor.matmul(
                ps_ap, lhsT, rhs,
                start=(i == 0 and not accum), stop=(stop and i == n - 1))
        return self._finish("pe", inst, [psum_buf], ins, "pe", 1)

    def transpose(self, psum_buf, ps_ap, in_buf, in_ap, ident):
        kdim = in_ap.shape[0]
        self._wait_deps("pe", [psum_buf], [in_buf, ident])
        inst = self.nc.tensor.transpose(ps_ap, in_ap, ident.ap[0:kdim, 0:kdim])
        return self._finish("pe", inst, [psum_buf], [in_buf, ident], "pe", 1)


def emit_bcast_row(k, out_buf, row_buf, ps_buf, ones1, D):
    for c in range(0, D, 512):
        w = min(512, D - c)
        k.mm(ps_buf, [(ps_buf.ap[:, 0:w], ones1.ap[:], row_buf.ap[:, c:c + w])],
             [row_buf, ones1])
        k.op("dve", "tensor_copy", [out_buf], [ps_buf],
             out_buf.ap[:, c:c + w], ps_buf.ap[:, 0:w])


def emit_identity(k, ident, n=128):
    k.op("pool", "memset", [ident], [], ident.ap[:], 0.0)
    k.op("pool", "affine_select", [ident], [ident],
         out=ident.ap[:], in_=ident.ap[:], compare_op=ALU.not_equal,
         fill=1.0, base=0, pattern=[[-1, n]], channel_multiplier=1)


def emit_tril(k, buf, n=128, val=1.0):
    k.op("pool", "memset", [buf], [], buf.ap[:], val)
    k.op("pool", "affine_select", [buf], [buf],
         out=buf.ap[:], in_=buf.ap[:], compare_op=ALU.is_ge,
         fill=0.0, base=0, pattern=[[-1, n]], channel_multiplier=1)


def emit_ln(k, mv, stats, rstd, xin_buf, x_ap, D, eps=1e-5):
    nch = (D + 511) // 512
    for c in range(nch):
        k.op("dve", "bn_stats", [stats], [xin_buf],
             stats.ap[:, 6 * c:6 * c + 6], x_ap[:, 512 * c:512 * (c + 1)])
    k.op("dve", "bn_aggr", [mv], [stats], mv.ap[:], stats.ap[:, 0:6 * nch])
    cb, eps_ap = k.constcol(eps)
    k.op("act", "activation", [rstd], [mv, cb], rstd.ap[:], mv.ap[:, 1:2], AF.Ln, bias=eps_ap)
    k.op("act", "activation", [rstd], [rstd], rstd.ap[:], rstd.ap[:], AF.Exp, scale=-0.5)
    return mv.ap[:, 0:1], rstd.ap[:]


P0_ITEMS = [("wkqv_fm", "wkqv_bf"), ("wproj_fm", "wproj_bf"),
            ("Wq", "wq_bf"), ("Wk", "wk_bf"), ("Wv", "wv_bf"),
            ("Wg", "wg_bf"), ("Wo", "wo_bf")]


def build_phase0(k, g, cfg, items=None):
    items = items if items is not None else P0_ITEMS
    for srcn, dstn in items:
        sb = g[srcn]
        P, X = sb.ap.shape
        db = k.dram(dstn, [P, X], BF16)
        g[dstn] = db
        k.dma("pool", [db], [sb], db.ap[:], sb.ap[:])


def build_phase0_moe(k, g):
    w1v = g["W1"].ap[:].rearrange("r (c f) -> r c f", f=4096)
    w2v = g["W2"].ap[:].rearrange("r (c d) -> r c d", d=1024)
    for half in range(2):
        d1 = k.dram(f"w1{'ab'[half]}_bf", [1024, 8 * 2048], BF16)
        g[f"w1{'ab'[half]}_bf"] = d1
        for r0 in range(0, 1024, 128):
            k.dma("pool", [d1], [g["W1"]], d1.ap[r0:r0 + 128, :],
                  w1v[r0:r0 + 128, :, half * 2048:(half + 1) * 2048])
    for half in range(2):
        d2 = k.dram(f"w2{'ab'[half]}_bf", [1024, 32 * 512], BF16)
        g[f"w2{'ab'[half]}_bf"] = d2
        for r0 in range(0, 1024, 128):
            k.dma("pool", [d2], [g["W2"]], d2.ap[r0:r0 + 128, :],
                  w2v[r0:r0 + 128, :, half * 512:(half + 1) * 512])


def build_phase1(k, g, cfg):
    nc = k.nc
    B, T, D = cfg["B"], cfg["T"], cfg["D"]
    H, Dh = 16, 64
    NIC = D // 128
    NTB = T // 512
    sc = ExitStack()

    cb20, neg20 = k.constcol(-20.0)
    ident = k.sb(sc, "p1_ident", [128, 128], BF16)
    emit_identity(k, ident)
    triu = k.sb(sc, "p1_triu", [128, 128], BF16)
    k.op("pool", "memset", [triu], [], triu.ap[:], 1.0)
    k.op("pool", "affine_select", [triu], [triu],
         out=triu.ap[:], in_=triu.ap[:], compare_op=ALU.is_ge,
         fill=0.0, base=0, pattern=[[1, 128]], channel_multiplier=-1)

    wkqv = k.sb(sc, "p1_wkqv", [128, NIC * 3 * D], BF16)
    k.dma("sp", [wkqv], [g["wkqv_bf"]], wkqv.ap[:], g["wkqv_bf"].ap[:])
    wproj = k.sb(sc, "p1_wproj", [128, NIC * D], BF16)
    k.dma("sp", [wproj], [g["wproj_bf"]], wproj.ap[:], g["wproj_bf"].ap[:])
    bproj = k.sb(sc, "p1_bproj", [128, D], F32)

    qf = k.sb(sc, "p1_q", [128, NIC * T], BF16)
    kf = k.sb(sc, "p1_k", [128, NIC * T], BF16)
    vt = k.sb(sc, "p1_v", [128, (T // 128) * D], BF16)
    xin = k.sb(sc, "p1_xin", [128, D], F32)
    xnt = k.sb(sc, "p1_xnt", [128, D], BF16)
    xnf = k.sb(sc, "p1_xnf", [128, NIC * 512], BF16)
    stats = k.sb(sc, "p1_stats", [128, 12], F32)
    mv = k.sb(sc, "p1_mv", [128, 2], F32)
    rstd = k.sb(sc, "p1_rstd", [128, 1], F32)
    ctxf = xnf
    lparts = k.sb(sc, "p1_lparts", [128, 4 * 8], F32)
    ptb = k.sb(sc, "p1_ptb", [128, 16 * 512], BF16)
    lr = k.sb(sc, "p1_lr", [1, 2 * 512], F32)
    linv64 = k.sb(sc, "p1_linv64", [128, 512], F32)
    ones128b = k.sb(sc, "p1_o128", [128, 1], BF16)
    k.op("dve", "memset", [ones128b], [], ones128b.ap[:], 1.0)
    ones1f = k.sb(sc, "p1_o1f", [1, 128], F32)
    k.op("dve", "memset", [ones1f], [], ones1f.ap[:], 1.0)
    res = k.sb(sc, "p1_res", [128, D], F32)

    psA = [k.psum(sc, f"p1_psA{i}", [128, 512], F32) for i in range(2)]
    psC = k.psum(sc, "p1_psC", [128, 1024], F32)
    _psT0 = k.psum(sc, "p1_psT0", [128, 512], BF16)
    psT = [_psT0, _psT0]
    psO = k.psum(sc, "p1_psO", [128, 512], F32)
    psL = k.psum(sc, "p1_psL", [1, 512], F32)
    psLB = k.psum(sc, "p1_psLB", [128, 512], F32)

    sub = ExitStack()
    ones1 = k.sb(sub, "p1_ones1", [1, 128], F32)
    k.dma("sp", [res], [g["b_proj"]], res.ap[0:1, :], g["b_proj"].ap[0:1, :])
    k.op("dve", "memset", [ones1], [], ones1.ap[:], 1.0)
    for c in range(0, D, 512):
        k.mm(psA[0], [(psA[0].ap[:, 0:512], ones1.ap[:], res.ap[0:1, c:c + 512])],
             [res, ones1])
        k.op("dve", "tensor_copy", [bproj], [psA[0]],
             bproj.ap[:, c:c + 512], psA[0].ap[:, 0:512])
    k.barrier()
    sub.close()
    build_phase0_moe(k, g)

    for b in range(B):
        for tb in range(NTB):
            t0 = b * T + tb * 512
            for j in range(4):
                k.dma("sp", [xin], [g["x0"]], xin.ap[:],
                      g["x0"].ap[t0 + j * 128: t0 + (j + 1) * 128, :])
                mu_ap, rstd_ap = emit_ln(k, mv, stats, rstd, xin, xin.ap[:], D)
                k.op("dve", "tensor_scalar", [xnt], [xin, mv, rstd],
                     xnt.ap[:], xin.ap[:], mu_ap, rstd_ap,
                     ALU.subtract, ALU.mult)
                for ic in range(NIC):
                    pT = psT[ic % 2]
                    k.transpose(pT, pT.ap[0:128, 0:128], xnt,
                                xnt.ap[:, ic * 128:(ic + 1) * 128], ident)
                    k.op("dve", "tensor_copy", [xnf], [pT],
                         xnf.ap[:, ic * 512 + j * 128: ic * 512 + (j + 1) * 128],
                         pT.ap[0:128, 0:128])
            for oc in range(2 * NIC):
                dst = kf if oc < NIC else qf
                oc_l = oc % NIC
                pA = psA[oc % 2]
                items = [(pA.ap[:],
                          wkqv.ap[:, ic * 3 * D + oc * 128: ic * 3 * D + (oc + 1) * 128],
                          xnf.ap[:, ic * 512:(ic + 1) * 512]) for ic in range(NIC)]
                k.mm(pA, items, [wkqv, xnf])
                k.op("dve", "tensor_copy", [dst], [pA],
                     dst.ap[:, oc_l * T + tb * 512: oc_l * T + (tb + 1) * 512], pA.ap[:])
            for j in range(4):
                for h2 in range(2):
                    items = [(psC.ap[:, h2 * 512:(h2 + 1) * 512],
                              xnf.ap[:, ic * 512 + j * 128: ic * 512 + (j + 1) * 128],
                              wkqv.ap[:, ic * 3 * D + 2 * D + h2 * 512:
                                      ic * 3 * D + 2 * D + (h2 + 1) * 512])
                             for ic in range(NIC)]
                    k.mm(psC, items, [wkqv, xnf], accum=(False))
                tt = tb * 4 + j
                k.op("dve", "tensor_copy", [vt], [psC],
                     vt.ap[:, tt * D:(tt + 1) * D], psC.ap[:])

        for qb in range(NTB):
            nkb = qb + 1
            ntile = 4 * nkb
            for h in range(H):
                hc, hr = h // 2, (h % 2) * 64
                for tkg in range(ntile):
                    diag = (tkg // 4 == qb)
                    c0 = (tkg % 4) * 128 if diag else 0
                    pA = psA[tkg % 2]
                    k.mm(pA, [(pA.ap[:, c0:512],
                               kf.ap[hr:hr + 64, hc * T + tkg * 128:
                                     hc * T + tkg * 128 + 128],
                               qf.ap[hr:hr + 64, hc * T + qb * 512 + c0:
                                     hc * T + qb * 512 + 512])],
                         [qf, kf])
                    if c0 > 0:
                        k.op("dve", "memset", [ptb], [],
                             ptb.ap[:, tkg * 512: tkg * 512 + c0], 0.0)
                    k.op("act", "activation", [ptb], [pA, cb20],
                         ptb.ap[:, tkg * 512 + c0: (tkg + 1) * 512],
                         pA.ap[:, c0:512], AF.Exp, scale=0.125, bias=neg20)
                    if diag:
                        k.op("dve", "tensor_tensor", [ptb], [ptb, triu],
                             ptb.ap[:, tkg * 512 + c0: tkg * 512 + c0 + 128],
                             ptb.ap[:, tkg * 512 + c0: tkg * 512 + c0 + 128],
                             triu.ap[:], ALU.mult)
                for tkg in range(ntile):
                    k.mm(psL, [(psL.ap[:], ones128b.ap[:],
                                ptb.ap[:, tkg * 512:(tkg + 1) * 512])],
                         [ptb, ones128b], accum=(tkg > 0), stop=(tkg == ntile - 1))
                hp = h % 2
                k.op("dve", "tensor_copy", [lr], [psL],
                     lr.ap[:, 0:512], psL.ap[:])
                for tkg in range(ntile):
                    k.mm(psO, [(psO.ap[hr:hr + 64, :],
                                vt.ap[:, tkg * D + h * Dh:
                                      tkg * D + (h + 1) * Dh],
                                ptb.ap[:, tkg * 512:(tkg + 1) * 512])],
                         [vt, ptb], accum=(tkg > 0), stop=(tkg == ntile - 1))
                k.op("act", "activation", [lr], [lr], lr.ap[:, 512:1024],
                     lr.ap[:, 0:512], AF.Ln)
                k.op("act", "activation", [lr], [lr], lr.ap[:, 512:1024],
                     lr.ap[:, 512:1024], AF.Exp, scale=-1.0)
                k.mm(psLB, [(psLB.ap[hr:hr + 64, :], ones1f.ap[:, 0:64],
                             lr.ap[:, 512:1024])], [lr, ones1f])
                if hp == 1:
                    k.op("dve", "tensor_copy", [linv64], [psLB],
                         linv64.ap[:], psLB.ap[:])
                    k.op("dve", "tensor_tensor", [ctxf], [psO, linv64],
                         ctxf.ap[:, hc * 512:(hc + 1) * 512],
                         psO.ap[:], linv64.ap[:], ALU.mult)
            t0 = b * T + qb * 512
            for j in range(4):
                for half in range(2):
                    items = [(psC.ap[:, half * 512:(half + 1) * 512],
                              ctxf.ap[:, ic * 512 + j * 128: ic * 512 + (j + 1) * 128],
                              wproj.ap[:, ic * D + half * 512: ic * D + (half + 1) * 512])
                             for ic in range(NIC)]
                    k.mm(psC, items, [ctxf, wproj])
                k.dma("sp", [xin], [g["x0"]], xin.ap[:],
                      g["x0"].ap[t0 + j * 128: t0 + (j + 1) * 128, :])
                k.op("dve", "tensor_tensor", [res], [psC, bproj],
                     res.ap[:], psC.ap[:], bproj.ap[:], ALU.add)
                k.op("dve", "tensor_tensor", [res], [res, xin],
                     res.ap[:], res.ap[:], xin.ap[:], ALU.add)
                k.dma("sp", [g["x1"]], [res],
                      g["x1"].ap[t0 + j * 128: t0 + (j + 1) * 128, :], res.ap[:])
    sc.close()
    k.barrier()


def build_phase2(k, g, cfg):
    nc = k.nc
    B, T, D = cfg["B"], cfg["T"], cfg["D"]
    E, Fh = 8, 4096
    NIC = D // 128
    NOC = Fh // 128
    NTOK = B * T
    NTB = NTOK // 512
    NT1 = NTOK // 512 + E
    NSLOT = NT1 * 512
    OOBV = NTOK
    sc = ExitStack()

    ident = k.sb(sc, "p2_ident", [128, 128], BF16)
    emit_identity(k, ident)

    xn2tok = k.dram("p2_xn2tok", [NTOK, D], BF16)
    ysel = k.dram("p2_ysel", [NTOK, D], F32)
    yscr = k.dram("p2_yscr", [NTOK, D], F32)
    tokof_d = k.dram("p2_tokof", [NSLOT, 1], U32)

    wr = k.sb(sc, "p2_wr", [128, NIC * E], F32)
    k.dma("sp", [wr], [g["Wr"]], wr.ap[:].rearrange("p (c e) -> p c e", e=E),
          g["Wr"].ap[:].rearrange("(c p) e -> p c e", p=128))
    brow = k.sb(sc, "p2_br", [E, 1], F32)
    k.dma("sp", [brow], [g["br"]], brow.ap[:], g["br"].ap[0:1, :].rearrange("o e -> e o"))
    maskf = k.sb(sc, "p2_maskf", [E, NTOK], BF16)
    gates = k.sb(sc, "p2_gates", [128, NTOK // 128], F32)
    wofs = k.sb(sc, "p2_wofs", [128, NT1], U32)
    b1ofs = k.sb(sc, "p2_b1ofs", [128, NT1], U32)

    sa = ExitStack()
    ones1 = k.sb(sa, "p2_ones1", [1, 128], F32)
    k.op("dve", "memset", [ones1], [], ones1.ap[:], 1.0)
    b2sb = k.sb(sa, "p2_b2", [E, D], BF16)
    k.dma("pool", [b2sb], [g["b2"]], b2sb.ap[:], g["b2"].ap[:])
    lngrow = k.sb(sa, "p2_lngr", [1, D], F32)
    k.dma("sp", [lngrow], [g["ln_g"]], lngrow.ap[:], g["ln_g"].ap[0:1, :])
    lnbrow = k.sb(sa, "p2_lnbr", [1, D], F32)
    k.dma("sp", [lnbrow], [g["ln_b"]], lnbrow.ap[:], g["ln_b"].ap[0:1, :])
    lng = k.sb(sa, "p2_lng", [128, D], F32)
    lnb = k.sb(sa, "p2_lnb", [128, D], F32)
    xin = k.sb(sa, "p2_xin", [128, 4 * D], F32)
    xnp = k.sb(sa, "p2_xnp", [128, 4 * D], F32)
    xnpf = k.sb(sa, "p2_xnpf", [128, NIC * 512], F32)
    xn2t = k.sb(sa, "p2_xn2t", [128, D], BF16)
    tmp = k.sb(sa, "p2_tmp", [128, D], F32)
    stats = k.sb(sa, "p2_stats", [128, 12], F32)
    mv = k.sb(sa, "p2_mv", [128, 2], F32)
    rstd = k.sb(sa, "p2_rstd", [128, 1], F32)
    lgf = k.sb(sa, "p2_lgf", [E, 512], F32)
    lt = k.sb(sa, "p2_lt", [128, E], F32)
    ssum = k.sb(sa, "p2_ssum", [128, 1], F32)
    eexp = k.sb(sa, "p2_eexp", [128, E], F32)
    m1 = k.sb(sa, "p2_m1", [128, 1], F32)
    negm = k.sb(sa, "p2_negm", [128, 1], F32)
    eq = k.sb(sa, "p2_eq", [128, E], F32)
    cum = k.sb(sa, "p2_cum", [128, E], F32)
    le1 = k.sb(sa, "p2_le1", [128, E], F32)
    mtm = k.sb(sa, "p2_mtm", [128, E], BF16)
    ysb = k.sb(sa, "p2_ysb", [128, D], F32)
    psT = k.psum(sa, "p2_psT", [128, 512], F32)
    psTb = k.psum(sa, "p2_psTb", [128, 512], BF16)
    psR = k.psum(sa, "p2_psR", [E, 512], F32)
    psY = k.psum(sa, "p2_psY", [128, 1024], F32)
    identf = k.sb(sa, "p2_identf", [128, 128], F32)
    emit_identity(k, identf)

    emit_bcast_row(k, lng, lngrow, psT, ones1, D)
    emit_bcast_row(k, lnb, lnbrow, psT, ones1, D)

    for tb in range(NTB):
        t0 = tb * 512
        k.dma("sp", [xin], [g["x1"]],
              xin.ap[:].rearrange("p (f d) -> p f d", d=D),
              g["x1"].ap[t0:t0 + 512, :].rearrange("(f p) d -> p f d", p=128))
        for j in range(4):
            x_ap = xin.ap[:, j * D:(j + 1) * D]
            mu_ap, rstd_ap = emit_ln(k, mv, stats, rstd, xin, x_ap, D)
            k.op("dve", "tensor_scalar", [xnp], [xin, mv, rstd],
                 xnp.ap[:, j * D:(j + 1) * D], x_ap, mu_ap, rstd_ap,
                 ALU.subtract, ALU.mult)
            for ic in range(NIC):
                k.transpose(psT, psT.ap[0:128, 0:128], xnp,
                            xnp.ap[:, j * D + ic * 128: j * D + (ic + 1) * 128], identf)
                k.op("dve", "tensor_copy", [xnpf], [psT],
                     xnpf.ap[:, ic * 512 + j * 128: ic * 512 + (j + 1) * 128],
                     psT.ap[0:128, 0:128])
            xp_ap = xnp.ap[:, j * D:(j + 1) * D]
            mu2, rstd2 = emit_ln(k, mv, stats, rstd, xnp, xp_ap, D)
            k.op("dve", "tensor_scalar", [tmp], [xnp, mv, rstd],
                 tmp.ap[:], xp_ap, mu2, rstd2, ALU.subtract, ALU.mult)
            k.op("dve", "tensor_tensor", [tmp], [tmp, lng],
                 tmp.ap[:], tmp.ap[:], lng.ap[:], ALU.mult)
            k.op("dve", "tensor_tensor", [xn2t], [tmp, lnb],
                 xn2t.ap[:], tmp.ap[:], lnb.ap[:], ALU.add)
            k.dma("sp", [xn2tok], [xn2t],
                  xn2tok.ap[t0 + j * 128: t0 + (j + 1) * 128, :], xn2t.ap[:])
        k.mm(psR, [(psR.ap[:], wr.ap[:, ic * E:(ic + 1) * E],
                    xnpf.ap[:, ic * 512:(ic + 1) * 512]) for ic in range(NIC)],
             [wr, xnpf])
        k.op("act", "activation", [lgf], [psR, brow],
             lgf.ap[:], psR.ap[:], AF.Identity, bias=brow.ap[:])
        for j in range(4):
            tt = t0 // 128 + j
            k.transpose(psT, psT.ap[0:128, 0:E], lgf,
                        lgf.ap[:, j * 128:(j + 1) * 128], identf)
            k.op("dve", "tensor_copy", [lt], [psT], lt.ap[:], psT.ap[0:128, 0:E])
            k.op("dve", "tensor_reduce", [m1], [lt], m1.ap[:], lt.ap[:], AX.X, ALU.max)
            k.op("dve", "tensor_scalar", [negm], [m1], negm.ap[:], m1.ap[:],
                 -1.0, None, ALU.mult)
            k.op("act", "activation", [eexp, ssum], [lt, negm],
                 eexp.ap[:], lt.ap[:], AF.Exp, bias=negm.ap[:],
                 accum_out=ssum.ap[:])
            k.op("dve", "reciprocal", [gates], [ssum],
                 gates.ap[:, tt:tt + 1], ssum.ap[:])
            k.op("dve", "tensor_scalar", [eq], [lt, m1], eq.ap[:], lt.ap[:],
                 m1.ap[:], None, ALU.is_equal)
            k.op("dve", "tensor_tensor_scan", [cum], [eq],
                 cum.ap[:], eq.ap[:], eq.ap[:], 0.0, ALU.add, ALU.bypass)
            k.op("dve", "tensor_scalar", [le1], [cum], le1.ap[:], cum.ap[:],
                 1.0, None, ALU.is_le)
            k.op("dve", "tensor_tensor", [mtm], [eq, le1], mtm.ap[:], eq.ap[:],
                 le1.ap[:], ALU.mult)
            k.transpose(psTb, psTb.ap[0:E, 0:128], mtm, mtm.ap[:], ident)
            k.op("dve", "tensor_copy", [maskf], [psTb],
                 maskf.ap[:, t0 + j * 128: t0 + (j + 1) * 128], psTb.ap[0:E, 0:128])
            for half in range(2):
                k.mm(psY, [(psY.ap[:, half * 512:(half + 1) * 512],
                            maskf.ap[:, t0 + j * 128: t0 + (j + 1) * 128],
                            b2sb.ap[:, half * 512:(half + 1) * 512])],
                     [maskf, b2sb])
            k.op("dve", "tensor_copy", [ysb], [psY], ysb.ap[:], psY.ap[:])
            k.dma("sp", [ysel], [ysb],
                  ysel.ap[t0 + j * 128: t0 + (j + 1) * 128, :], ysb.ap[:])

    mf32 = k.sb(sa, "p2_mf32", [E, NTOK], F32)
    ranks = k.sb(sa, "p2_ranks", [E, NTOK], F32)
    c8 = k.sb(sa, "p2_c8", [E, 1], F32)
    ntl = k.sb(sa, "p2_ntl", [E, 1], F32)
    t8 = k.sb(sa, "p2_t8", [E, 1], F32)
    row8 = k.sb(sa, "p2_row8", [1, 3 * E], F32)
    row24 = k.sb(sa, "p2_row24", [1, 2 * NT1], F32)
    iop = k.sb(sa, "p2_iop", [128, 1], I32)
    iopf = k.sb(sa, "p2_iopf", [128, 1], F32)
    ekbf = k.sb(sa, "p2_ekbf", [128, NT1], F32)
    NTI = NTOK // 128
    NN = max(NT1, NTI)
    i64r = k.sb(sa, "p2_i64r", [1, NN], I32)
    i64f = k.sb(sa, "p2_i64f", [1, NN], F32)
    tokid = k.sb(sa, "p2_tokid", [128, NN], F32)
    tokidu = k.sb(sa, "p2_tokidu", [128, NTI], U32)
    slotm = k.sb(sa, "p2_slotm", [128, NTI], F32)
    slotu = k.sb(sa, "p2_slotu", [128, NTI], U32)
    srow = k.sb(sa, "p2_srow", [1, 512], F32)
    ones8f = k.sb(sa, "p2_o8f", [8, 1], F32)
    oobt = k.sb(sa, "p2_oobt", [128, NSLOT // 128], U32)

    k.op("dve", "memset", [ones8f], [], ones8f.ap[:], 1.0)
    k.op("pool", "iota", [iop], [], iop.ap[:], pattern=[[0, 1]], base=0,
         channel_multiplier=1)
    k.op("dve", "tensor_copy", [iopf], [iop], iopf.ap[:], iop.ap[:])
    k.op("pool", "iota", [i64r], [], i64r.ap[:], pattern=[[1, NN]], base=0,
         channel_multiplier=0)
    k.op("dve", "tensor_copy", [i64f], [i64r], i64f.ap[:], i64r.ap[:])
    k.op("pool", "iota", [oobt], [], oobt.ap[:], pattern=[[1, NSLOT // 128]],
         base=OOBV, channel_multiplier=NSLOT // 128,
         allow_small_or_imprecise_dtypes=True)
    k.dma("pool", [tokof_d], [oobt],
          tokof_d.ap[:].rearrange("(p i) o -> p (i o)", p=128), oobt.ap[:])

    k.op("dve", "tensor_copy", [mf32], [maskf], mf32.ap[:], maskf.ap[:])
    k.op("dve", "tensor_tensor_scan", [ranks], [mf32],
         ranks.ap[:], mf32.ap[:], mf32.ap[:], 0.0, ALU.add, ALU.bypass)
    k.op("dve", "tensor_copy", [c8], [ranks], c8.ap[:], ranks.ap[:, NTOK - 1:NTOK])
    k.op("dve", "tensor_single_scalar", [ntl], [c8], ntl.ap[:], c8.ap[:],
         0.0, ALU.is_gt)
    for kk in range(1, NTOK // 512):
        k.op("dve", "tensor_single_scalar", [t8], [c8], t8.ap[:], c8.ap[:],
             float(512 * kk), ALU.is_gt)
        k.op("dve", "tensor_tensor", [ntl], [ntl, t8], ntl.ap[:], ntl.ap[:],
             t8.ap[:], ALU.add)
    k.transpose(psT, psT.ap[0:1, 0:E], ntl, ntl.ap[:], identf)
    k.op("dve", "tensor_copy", [row8], [psT], row8.ap[:, 0:E], psT.ap[0:1, 0:E])
    k.op("dve", "tensor_tensor_scan", [row8], [row8],
         row8.ap[:, E:2 * E], row8.ap[:, 0:E], row8.ap[:, 0:E],
         0.0, ALU.add, ALU.bypass)
    k.op("dve", "tensor_tensor", [row8], [row8],
         row8.ap[:, E:2 * E], row8.ap[:, E:2 * E], row8.ap[:, 0:E], ALU.subtract)
    iota24 = i64f.ap[:, 0:NT1]
    k.op("dve", "tensor_scalar", [row24], [i64f, row8],
         row24.ap[:, 0:NT1], iota24, row8.ap[:, E:E + 1], None, ALU.is_ge)
    for e in range(1, E):
        k.op("dve", "tensor_scalar", [row24], [i64f, row8],
             row24.ap[:, NT1:2 * NT1], iota24, row8.ap[:, E + e:E + e + 1],
             None, ALU.is_ge)
        k.op("dve", "tensor_tensor", [row24], [row24],
             row24.ap[:, 0:NT1], row24.ap[:, 0:NT1], row24.ap[:, NT1:2 * NT1],
             ALU.add)
    k.op("dve", "tensor_scalar", [row24], [row24], row24.ap[:, 0:NT1],
         row24.ap[:, 0:NT1], -1.0, None, ALU.add)
    k.mm(psT, [(psT.ap[:, 0:NT1], ones1.ap[:], row24.ap[:, 0:NT1])],
         [row24, ones1])
    k.op("dve", "tensor_copy", [ekbf], [psT], ekbf.ap[:], psT.ap[:, 0:NT1])
    k.op("dve", "scalar_tensor_tensor", [tokid], [ekbf, iopf], tokid.ap[:, 0:NT1],
         ekbf.ap[:], 128.0, iopf.ap[:].broadcast_to((128, NT1)), ALU.mult, ALU.add)
    k.op("dve", "tensor_copy", [wofs], [tokid], wofs.ap[:], tokid.ap[:, 0:NT1])
    k.op("dve", "scalar_tensor_tensor", [tokid], [ekbf, iopf], tokid.ap[:, 0:NT1],
         ekbf.ap[:], 32.0, iopf.ap[:].broadcast_to((128, NT1)), ALU.mult, ALU.add)
    k.op("dve", "tensor_copy", [b1ofs], [tokid], b1ofs.ap[:], tokid.ap[:, 0:NT1])
    k.transpose(psT, psT.ap[0:E, 0:1], row8, row8.ap[:, E:2 * E].rearrange(
        "o e -> o e") if False else row8.ap[0:1, E:2 * E], identf)
    k.op("dve", "tensor_copy", [t8], [psT], t8.ap[:], psT.ap[0:E, 0:1])
    k.op("dve", "tensor_scalar", [t8], [t8], t8.ap[:], t8.ap[:], 512.0, -1.0,
         ALU.mult, ALU.add)
    k.op("dve", "tensor_scalar", [ranks], [ranks, t8], ranks.ap[:], ranks.ap[:],
         t8.ap[:], None, ALU.add)
    k.op("dve", "tensor_tensor", [ranks], [ranks, mf32], ranks.ap[:],
         ranks.ap[:], mf32.ap[:], ALU.mult)
    for ch in range(NTOK // 512):
        k.mm(psR, [(psR.ap[0:1, :], ones8f.ap[:],
                    ranks.ap[:, ch * 512:(ch + 1) * 512])], [ranks, ones8f])
        k.op("dve", "tensor_copy", [srow], [psR], srow.ap[:], psR.ap[0:1, :])
        for u in range(4):
            k.transpose(psT, psT.ap[0:128, 0:1], srow,
                        srow.ap[:, u * 128:(u + 1) * 128], identf)
            k.op("dve", "tensor_copy", [slotm], [psT],
                 slotm.ap[:, ch * 4 + u: ch * 4 + u + 1], psT.ap[0:128, 0:1])
    k.op("pool", "iota", [tokid], [], tokid.ap[:, 0:NTI], pattern=[[128, NTI]],
         base=0, channel_multiplier=1, allow_small_or_imprecise_dtypes=True)
    k.op("dve", "tensor_copy", [tokidu], [tokid], tokidu.ap[:], tokid.ap[:, 0:NTI])
    k.op("dve", "tensor_copy", [slotu], [slotm], slotu.ap[:], slotm.ap[:])
    for i in range(NTI):
        k.idma([tokof_d], [tokidu, slotu], tokof_d.ap[:],
               slotu.ap[:, i:i + 1], tokidu.ap[:, i:i + 1], None,
               skip_waw=(i > 0))
    sa.close()
    k.barrier()

    sb_ = ExitStack()
    w1h = [k.sb(sb_, f"p2_w1h{i}", [128, NIC * 2048], BF16) for i in range(2)]
    w2h = [k.sb(sb_, f"p2_w2h{i}", [128, NOC * 512], BF16) for i in range(2)]
    b1row = k.sb(sb_, "p2_b1r", [32, 128], F32)
    b1sb = k.sb(sb_, "p2_b1", [128, NOC], F32)
    identf2 = k.sb(sb_, "p2_identf2", [128, 128], F32)
    emit_identity(k, identf2)
    tkof = k.sb(sb_, "p2_tkof", [128, 4], U32)
    tkj = [k.sb(sb_, f"p2_tkj{j}", [128, 1], U32) for j in range(4)]
    xgt = k.sb(sb_, "p2_xgt", [128, 4 * D], BF16)
    xgf = k.sb(sb_, "p2_xgf", [128, NIC * 512], BF16)
    hm = k.sb(sb_, "p2_hm", [128, NOC * 512], BF16)
    yev = [k.sb(sb_, f"p2_yev{i}", [128, D], F32) for i in range(2)]
    psH = [k.psum(sb_, f"p2_psH{i}", [128, 512], F32) for i in range(2)]
    psYh = [k.psum(sb_, f"p2_psYh{i}", [128, 512], F32) for i in range(2)]
    psM = k.psum(sb_, "p2_psM", [128, 512], F32)
    psTb2 = k.psum(sb_, "p2_psTb2", [128, 512], BF16)

    breg = nc.gpsimd.to_reg(NTOK - 1)
    b1v = g["b1"].ap[:].rearrange("e (c p) -> (e c) p", p=128)

    def gather_w(kt):
        k.idma([w1h[0]], [wofs], w1h[0].ap[:], None,
               g["w1a_bf"].ap[:], wofs.ap[:, kt:kt + 1])
        k.idma([w1h[1]], [wofs], w1h[1].ap[:], None,
               g["w1b_bf"].ap[:], wofs.ap[:, kt:kt + 1])
        k.idma([w2h[0]], [wofs], w2h[0].ap[:], None,
               g["w2a_bf"].ap[:], wofs.ap[:, kt:kt + 1])
        k.idma([w2h[1]], [wofs], w2h[1].ap[:], None,
               g["w2b_bf"].ap[:], wofs.ap[:, kt:kt + 1])
        k.idma([b1row], [b1ofs], b1row.ap[:], None, b1v,
               b1ofs.ap[0:32, kt:kt + 1])

    dbg_xg = k.dram("p2_dbg_xg", [NT1, 128, 4 * D], BF16) if cfg.get("P2_DBG") else None
    if cfg.get("P2_DBG"):
        dbg_w = k.dram("p2_dbg_w", [128, NT1], U32)
        k.dma("sp", [dbg_w], [wofs], dbg_w.ap[:], wofs.ap[:])
        dbg_b = k.dram("p2_dbg_b", [128, NT1], U32)
        k.dma("sp", [dbg_b], [b1ofs], dbg_b.ap[:], b1ofs.ap[:])
        dbg_hm = k.dram("p2_dbg_hm", [NT1, 128, NOC * 512], BF16)
        dbg_w1 = k.dram("p2_dbg_w1", [NT1, 128, 4096], BF16)
        dbg_y0 = k.dram("p2_dbg_y0", [NT1 * 4, D], F32)
        dbg_tk = k.dram("p2_dbg_tk", [NT1, 128, 4], U32)
    for kt in range(cfg.get("P2_TILES", NT1)):
        k.dma("sp", [tkof], [tokof_d], tkof.ap[:],
              tokof_d.ap[kt * 512:(kt + 1) * 512, :]
              .rearrange("(p j) o -> p (j o)", j=4))
        gather_w(kt)
        k.transpose(psM, psM.ap[0:128, 0:NOC], b1row, b1row.ap[0:32, :], identf2)
        k.op("dve", "tensor_copy", [b1sb], [psM], b1sb.ap[:], psM.ap[0:128, 0:NOC])
        for j in range(4):
            k.op("dve", "tensor_copy", [tkj[j]], [tkof], tkj[j].ap[:],
                 tkof.ap[:, j:j + 1])
            k.idma([xgt], [xn2tok, tkj[j]], xgt.ap[:, j * D:(j + 1) * D], None,
                   xn2tok.ap[:], tkj[j].ap[:],
                   bounds_check=breg, oob_is_err=False)
            for ic in range(NIC):
                k.transpose(psTb2, psTb2.ap[0:128, 0:128], xgt,
                            xgt.ap[:, j * D + ic * 128: j * D + (ic + 1) * 128], ident)
                k.op("dve", "tensor_copy", [xgf], [psTb2],
                     xgf.ap[:, ic * 512 + j * 128: ic * 512 + (j + 1) * 128],
                     psTb2.ap[0:128, 0:128])
        if dbg_xg is not None:
            k.dma("sp", [dbg_xg], [xgt], dbg_xg.ap[kt], xgt.ap[:])
            if kt > 0:
                k.dma("sp", [dbg_hm], [hm], dbg_hm.ap[kt - 1], hm.ap[:])
                k.dma("sp", [dbg_w1], [w1h[0]], dbg_w1.ap[kt - 1], w1h[0].ap[:, 0:4096])
        for oc in range(NOC):
            hh = oc // 16
            ocl = oc % 16
            pH = psH[oc % 2]
            k.mm(pH, [(pH.ap[:],
                       w1h[hh].ap[:, ic * 2048 + ocl * 128: ic * 2048 + (ocl + 1) * 128],
                       xgf.ap[:, ic * 512:(ic + 1) * 512]) for ic in range(NIC)],
                 [w1h[hh], xgf])
            k.op("act", "activation", [hm], [pH, b1sb],
                 hm.ap[:, oc * 512:(oc + 1) * 512], pH.ap[:], AF.Relu,
                 bias=b1sb.ap[:, oc:oc + 1])
        if dbg_xg is not None:
            k.dma("sp", [dbg_tk], [tkof], dbg_tk.ap[kt], tkof.ap[:])
        for j in range(4):
            for half in range(2):
                pY = psYh[half]
                k.mm(pY, [(pY.ap[:],
                           hm.ap[:, hc * 512 + j * 128: hc * 512 + (j + 1) * 128],
                           w2h[half].ap[:, hc * 512:(hc + 1) * 512])
                          for hc in range(NOC)],
                     [hm, w2h[half]])
                ev = "dve" if half == 0 else "act"
                if ev == "dve":
                    k.op("dve", "tensor_copy", [yev[j % 2]], [pY],
                         yev[j % 2].ap[:, half * 512:(half + 1) * 512], pY.ap[:])
                else:
                    k.op("act", "copy", [yev[j % 2]], [pY],
                         yev[j % 2].ap[:, half * 512:(half + 1) * 512], pY.ap[:])
            k.idma([yscr], [yev[j % 2], tkj[j]], yscr.ap[:],
                   tkj[j].ap[:], yev[j % 2].ap[:], None,
                   bounds_check=breg, oob_is_err=False, skip_waw=False)

    sb_.close()
    k.barrier()

    gates_d = k.dram("p2_gates_d", [128, NTOK // 128], F32)
    k.dma("sp", [gates_d], [gates], gates_d.ap[:], gates.ap[:])
    g["ysel"] = ysel
    g["yscr"] = yscr
    g["gates_d"] = gates_d
    sc.close()
    k.barrier()


def build_phase3(k, g, cfg):
    nc = k.nc
    B, T, D = cfg["B"], cfg["T"], cfg["D"]
    HG, DK, DV = 4, 1024, 2048
    HK, HV = DK // HG, DV // HG
    C = 64
    NIC = D // 128
    NTB = T // 512
    NTOK = B * T
    NCH = T // C
    GN = 16.0
    sc = ExitStack()

    ident = k.sb(sc, "p3_ident", [128, 128], BF16)
    emit_identity(k, ident)
    identf = k.sb(sc, "p3_identf", [128, 128], F32)
    emit_identity(k, identf)
    triu64 = k.sb(sc, "p3_triu", [64, 64], mybir.dt.int8)
    k.op("pool", "memset", [triu64], [], triu64.ap[:], 1)
    k.op("pool", "affine_select", [triu64], [triu64],
         out=triu64.ap[:], in_=triu64.ap[:], compare_op=ALU.is_ge,
         fill=0, base=0, pattern=[[1, 64]], channel_multiplier=-1)
    z64 = k.sb(sc, "p3_z64", [64, 64], F32)
    k.op("pool", "memset", [z64], [], z64.ap[:], 0.0)
    ones1 = k.sb(sc, "p3_ones1", [1, 128], F32)
    k.op("dve", "memset", [ones1], [], ones1.ap[:], 1.0)

    xnf_scr = k.dram("p3_xnf", [B * NTB, 128, NIC * 512], BF16)
    qt_scr = k.dram("p3_qt", [B * NTB, 128, NIC * 512], BF16)
    ki_scr = k.dram("p3_ki", [B * NTB, 128, NIC * 512], BF16)
    khe_scr = k.dram("p3_khe", [NTOK, DK], BF16)
    v_scr = k.dram("p3_v", [NTOK, DV], BF16)
    o_scr = k.dram("p3_o", [NTOK, DV], BF16)
    Dsb = k.sb(sc, "p3_D", [128, NIC * (NTOK // C // B) * B], F32)

    gkw1 = k.sb(sc, "p3_gkw1", [128, NIC * 16], BF16)
    k.dma("pool", [gkw1], [g["gk_w1"]],
          gkw1.ap[:].rearrange("p (c e) -> p c e", e=16),
          g["gk_w1"].ap[:].rearrange("(c p) e -> p c e", p=128))
    gkw2 = k.sb(sc, "p3_gkw2", [16, DK], BF16)
    k.dma("pool", [gkw2], [g["gk_w2"]], gkw2.ap[:], g["gk_w2"].ap[:])
    gkbrow = k.sb(sc, "p3_gkbr", [1, DK], F32)
    k.dma("sp", [gkbrow], [g["gk_b"]], gkbrow.ap[:], g["gk_b"].ap[0:1, :])
    rmsrow = k.sb(sc, "p3_rmsr", [1, HV], F32)
    k.dma("sp", [rmsrow], [g["rms_w"]], rmsrow.ap[:], g["rms_w"].ap[0:1, :])
    rmsw = k.sb(sc, "p3_rmsw", [128, HV], F32)

    sa = ExitStack()
    wq = k.sb(sa, "p3_wq", [128, NIC * DK], BF16)
    wk_ = k.sb(sa, "p3_wk", [128, NIC * DK], BF16)
    wv = k.sb(sa, "p3_wv", [128, NIC * DV], BF16)
    k.dma("sp", [wq], [g["wq_bf"]], wq.ap[:], g["wq_bf"].ap[:])
    k.dma("sp", [wk_], [g["wk_bf"]], wk_.ap[:], g["wk_bf"].ap[:])
    k.dma("sp", [wv], [g["wv_bf"]], wv.ap[:], g["wv_bf"].ap[:])
    xin = k.sb(sa, "p3_xin", [128, 4 * D], F32)
    yse = k.sb(sa, "p3_yse", [128, 4 * D], F32)
    ysc_ = k.sb(sa, "p3_ysc", [128, 4 * D], F32)
    gt3 = k.sb(sa, "p3_gt", [128, B * T // 128], F32)
    xnt = k.sb(sa, "p3_xnt", [128, D], BF16)
    xnf = k.sb(sa, "p3_xnf_sb", [128, NIC * 512], BF16)
    stats = k.sb(sa, "p3_stats", [128, 12], F32)
    mv = k.sb(sa, "p3_mv", [128, 2], F32)
    rstd = k.sb(sa, "p3_rstd", [128, 1], F32)
    qtf = k.sb(sa, "p3_qtf", [128, NIC * 512], BF16)
    kif = k.sb(sa, "p3_kif", [128, NIC * 512], BF16)
    kef = k.sb(sa, "p3_kef", [128, 512], BF16)
    ket = k.sb(sa, "p3_ket", [64, 8 * 1024], BF16)
    vtm = k.sb(sa, "p3_vtm", [128, 4 * DV], BF16)
    midb = k.sb(sa, "p3_mid", [16, 512], BF16)
    glf = k.sb(sa, "p3_glf", [128, 512], F32)
    gf = k.sb(sa, "p3_gf", [128, 512], F32)
    cumg = k.sb(sa, "p3_cumg", [128, 512], F32)
    lam = k.sb(sa, "p3_lam", [128, 512], F32)
    enc = k.sb(sa, "p3_enc", [128, 512], F32)
    t1 = k.sb(sa, "p3_t1", [128, 512], F32)
    t2 = k.sb(sa, "p3_t2", [128, 512], F32)
    dcol = k.sb(sa, "p3_dcol", [128, 8], F32)
    gkb = k.sb(sa, "p3_gkb", [128, NIC], F32)
    gkbr2 = k.sb(sa, "p3_gkbr2", [NIC, 128], F32)
    psT = k.psum(sa, "p3_psT", [128, 512], F32)
    psTb = k.psum(sa, "p3_psTb", [128, 512], BF16)
    psA = k.psum(sa, "p3_psA", [128, 512], F32)
    psV = k.psum(sa, "p3_psV", [128, 1024], F32)
    psM = k.psum(sa, "p3_psM", [16, 512], F32)

    k.dma("sp", [gkbr2], [g["gk_b"]],
          gkbr2.ap[:], g["gk_b"].ap[0:1, :].rearrange("o (c p) -> (o c) p", p=128))
    k.transpose(psT, psT.ap[0:128, 0:NIC], gkbr2, gkbr2.ap[:], identf)
    k.op("dve", "tensor_copy", [gkb], [psT], gkb.ap[:], psT.ap[0:128, 0:NIC])
    emit_bcast_row(k, rmsw, rmsrow, psT, ones1, HV)
    cbe, eps_col = k.constcol(1e-5)
    cb1, one_col = k.constcol(1.0)

    k.dma("sp", [gt3], [g["gates_d"]], gt3.ap[:], g["gates_d"].ap[:])
    for b in range(B):
        for tb in range(NTB):
            gtb = b * NTB + tb
            t0 = b * T + tb * 512
            k.dma("sp", [xin], [g["x1"]],
                  xin.ap[:].rearrange("p (f d) -> p f d", d=D),
                  g["x1"].ap[t0:t0 + 512, :].rearrange("(f p) d -> p f d", p=128))
            k.dma("sp", [yse], [g["ysel"]],
                  yse.ap[:].rearrange("p (f d) -> p f d", d=D),
                  g["ysel"].ap[t0:t0 + 512, :].rearrange("(f p) d -> p f d", p=128))
            k.dma("sp", [ysc_], [g["yscr"]],
                  ysc_.ap[:].rearrange("p (f d) -> p f d", d=D),
                  g["yscr"].ap[t0:t0 + 512, :].rearrange("(f p) d -> p f d", p=128))
            for j in range(4):
                ti = t0 // 128 + j
                k.op("dve", "tensor_tensor", [yse], [yse, ysc_],
                     yse.ap[:, j * D:(j + 1) * D], yse.ap[:, j * D:(j + 1) * D],
                     ysc_.ap[:, j * D:(j + 1) * D], ALU.add)
                k.op("dve", "tensor_scalar", [yse], [yse, gt3],
                     yse.ap[:, j * D:(j + 1) * D], yse.ap[:, j * D:(j + 1) * D],
                     gt3.ap[:, ti:ti + 1], None, ALU.mult)
                k.op("dve", "tensor_tensor", [xin], [yse, xin],
                     xin.ap[:, j * D:(j + 1) * D], yse.ap[:, j * D:(j + 1) * D],
                     xin.ap[:, j * D:(j + 1) * D], ALU.add)
                k.dma("sp", [g["x2"]], [xin],
                      g["x2"].ap[t0 + j * 128: t0 + (j + 1) * 128, :],
                      xin.ap[:, j * D:(j + 1) * D])
                x_ap = xin.ap[:, j * D:(j + 1) * D]
                mu_ap, rstd_ap = emit_ln(k, mv, stats, rstd, xin, x_ap, D)
                k.op("dve", "tensor_scalar", [xnt], [xin, mv, rstd],
                     xnt.ap[:], x_ap, mu_ap, rstd_ap,
                     ALU.subtract, ALU.mult)
                for ic in range(NIC):
                    k.transpose(psTb, psTb.ap[0:128, 0:128], xnt,
                                xnt.ap[:, ic * 128:(ic + 1) * 128], ident)
                    k.op("dve", "tensor_copy", [xnf], [psTb],
                         xnf.ap[:, ic * 512 + j * 128: ic * 512 + (j + 1) * 128],
                         psTb.ap[0:128, 0:128])
            k.dma("sp", [xnf_scr], [xnf], xnf_scr.ap[gtb], xnf.ap[:])
            for j in range(4):
                for q4 in range(4):
                    half = q4 % 2
                    k.mm(psV, [(psV.ap[:, half * 512:(half + 1) * 512],
                                xnf.ap[:, ic * 512 + j * 128: ic * 512 + (j + 1) * 128],
                                wv.ap[:, ic * DV + q4 * 512: ic * DV + (q4 + 1) * 512])
                               for ic in range(NIC)], [wv, xnf])
                    if half == 1:
                        k.op("dve", "tensor_copy", [vtm], [psV],
                             vtm.ap[:, j * DV + (q4 - 1) * 512: j * DV + (q4 + 1) * 512],
                             psV.ap[:])
            k.dma("sp", [v_scr], [vtm],
                  v_scr.ap[t0:t0 + 512, :].rearrange("(f p) d -> p f d", p=128),
                  vtm.ap[:].rearrange("p (f d) -> p f d", d=DV))
            k.mm(psM, [(psM.ap[:], gkw1.ap[:, ic * 16:(ic + 1) * 16],
                        xnf.ap[:, ic * 512:(ic + 1) * 512]) for ic in range(NIC)],
                 [gkw1, xnf])
            k.op("dve", "tensor_copy", [midb], [psM], midb.ap[:], psM.ap[:])
            for pc in range(NIC):
                k.mm(psA, [(psA.ap[:], gkw2.ap[:, pc * 128:(pc + 1) * 128],
                            midb.ap[:])], [gkw2, midb])
                k.op("act", "activation", [glf], [psA, gkb],
                     glf.ap[:], psA.ap[:], AF.Identity, bias=gkb.ap[:, pc:pc + 1])
                k.op("act", "activation", [t1], [glf], t1.ap[:], glf.ap[:], AF.Abs)
                k.op("act", "activation", [t1], [t1], t1.ap[:], t1.ap[:], AF.Exp,
                     scale=-1.0)
                k.op("act", "activation", [t1], [t1, cb1], t1.ap[:], t1.ap[:], AF.Ln,
                     bias=one_col)
                k.op("dve", "tensor_single_scalar", [t2], [glf], t2.ap[:], glf.ap[:],
                     0.0, ALU.min)
                k.op("dve", "tensor_tensor", [t2], [t2, t1], t2.ap[:], t2.ap[:],
                     t1.ap[:], ALU.subtract)
                k.op("dve", "tensor_scalar", [gf], [t2], gf.ap[:],
                     t2.ap[:], 1.0 / GN, None, ALU.mult)
                for c in range(8):
                    k.op("dve", "tensor_tensor_scan", [cumg], [gf],
                         cumg.ap[:, c * C:(c + 1) * C],
                         gf.ap[:, c * C:(c + 1) * C],
                         gf.ap[:, c * C:(c + 1) * C],
                         0.0, ALU.add, ALU.bypass)
                cg = cumg.ap[:, 0:512]
                k.op("dve", "tensor_copy", [dcol], [cumg], dcol.ap[:],
                     cumg.ap[:, C - 1: 512: C])
                k.op("act", "activation", [Dsb], [dcol],
                     Dsb.ap[:, pc * (B * NCH) + b * NCH + tb * 8:
                            pc * (B * NCH) + b * NCH + tb * 8 + 8],
                     dcol.ap[:], AF.Exp)
                k.op("act", "activation", [lam], [cumg], lam.ap[:], cg, AF.Exp)
                k.mm(psA, [(psA.ap[:], wq.ap[:, ic * DK + pc * 128: ic * DK + (pc + 1) * 128],
                            xnf.ap[:, ic * 512:(ic + 1) * 512]) for ic in range(NIC)],
                     [wq, xnf])
                k.op("dve", "scalar_tensor_tensor", [qtf], [psA, lam],
                     qtf.ap[:, pc * 512:(pc + 1) * 512], psA.ap[:], 0.0625,
                     lam.ap[:], ALU.mult, ALU.mult)
                k.mm(psA, [(psA.ap[:], wk_.ap[:, ic * DK + pc * 128: ic * DK + (pc + 1) * 128],
                            xnf.ap[:, ic * 512:(ic + 1) * 512]) for ic in range(NIC)],
                     [wk_, xnf])
                k.op("act", "activation", [enc], [cumg], enc.ap[:], cg, AF.Exp,
                     scale=-1.0)
                k.op("dve", "tensor_tensor", [kif], [psA, enc],
                     kif.ap[:, pc * 512:(pc + 1) * 512], psA.ap[:], enc.ap[:], ALU.mult)
                for c in range(8):
                    k.op("dve", "tensor_scalar", [t1], [cumg],
                         t1.ap[:, c * C:(c + 1) * C], cg[:, c * C:(c + 1) * C],
                         cg[:, c * C + C - 1: c * C + C], None, ALU.subtract)
                k.op("act", "activation", [t1], [t1], t1.ap[:], t1.ap[:], AF.Exp,
                     scale=-1.0)
                k.op("dve", "tensor_tensor", [kef], [psA, t1],
                     kef.ap[:], psA.ap[:], t1.ap[:], ALU.mult)
                for c in range(8):
                    k.transpose(psTb, psTb.ap[0:64, 0:128], kef,
                                kef.ap[:, c * C:(c + 1) * C], ident)
                    k.op("dve", "tensor_copy", [ket], [psTb],
                         ket.ap[0:64, c * DK + pc * 128: c * DK + (pc + 1) * 128],
                         psTb.ap[0:64, 0:128])
            k.dma("sp", [qt_scr], [qtf], qt_scr.ap[gtb], qtf.ap[:])
            k.dma("sp", [ki_scr], [kif], ki_scr.ap[gtb], kif.ap[:])
            k.dma("sp", [khe_scr], [ket],
                  khe_scr.ap[t0:t0 + 512, :].rearrange("(c p) d -> p c d", p=64),
                  ket.ap[:].rearrange("p (c d) -> p c d", d=DK))
    sa.close()
    k.barrier()

    sb_ = ExitStack()
    SBT = 1024
    NSB = T // SBT
    NCB = SBT // C
    NS2 = 2
    qtb = [k.sb(sb_, f"p3b_qt{s}", [128, 2 * SBT], BF16) for s in range(NS2)]
    kib = [k.sb(sb_, f"p3b_ki{s}", [128, 2 * SBT], BF16) for s in range(NS2)]
    keb = [k.sb(sb_, f"p3b_ke{s}", [64, NCB * HK], BF16) for s in range(NS2)]
    vb = [k.sb(sb_, f"p3b_v{s}", [64, NCB * HV], BF16) for s in range(NS2)]
    onb = [k.sb(sb_, f"p3b_on{s}", [64, NCB * HV], BF16) for s in range(NS2)]
    Sm = [k.sb(sb_, f"p3b_Sm{s}", [128, 2 * HV], F32) for s in range(NS2)]
    Sbf = [k.sb(sb_, f"p3b_Sbf{s}", [128, 2 * HV], BF16) for s in range(NS2)]
    stb = [k.sb(sb_, f"p3b_st{s}", [64, 64], BF16) for s in range(NS2)]
    sqs = [k.sb(sb_, f"p3b_sqs{s}", [64, HV], BF16) for s in range(NS2)]
    ssq = [k.sb(sb_, f"p3b_ssq{s}", [64, 1], F32) for s in range(NS2)]
    rso = [k.sb(sb_, f"p3b_rso{s}", [64, 1], F32) for s in range(NS2)]
    psST = [k.psum(sb_, f"p3b_psST{s}", [64, 64], F32) for s in range(NS2)]
    psO = [k.psum(sb_, f"p3b_psO{s}", [64, 512], F32) for s in range(NS2)]
    psU = [k.psum(sb_, f"p3b_psU{s}", [128, 1024], F32) for s in range(NS2)]

    def sb3_load(s, h, b, sb2):
        base_t = b * T + sb2 * SBT
        for i, tbl in enumerate(range(sb2 * 2, sb2 * 2 + 2)):
            gtb = b * NTB + tbl
            k.dma("sp", [qtb[s]], [qt_scr],
                  qtb[s].ap[:].rearrange("p (q t) -> p q t", q=2)[:, :, i * 512:(i + 1) * 512],
                  qt_scr.ap[gtb].rearrange("p (q t) -> p q t", q=NIC)[:, 2 * h:2 * h + 2, :])
            k.dma("sp", [kib[s]], [ki_scr],
                  kib[s].ap[:].rearrange("p (q t) -> p q t", q=2)[:, :, i * 512:(i + 1) * 512],
                  ki_scr.ap[gtb].rearrange("p (q t) -> p q t", q=NIC)[:, 2 * h:2 * h + 2, :])
        k.dma("sp", [keb[s]], [khe_scr],
              keb[s].ap[:].rearrange("p (c d) -> p c d", d=HK),
              khe_scr.ap[base_t:base_t + SBT, h * HK:(h + 1) * HK]
              .rearrange("(c p) d -> p c d", p=64))
        k.dma("sp", [vb[s]], [v_scr],
              vb[s].ap[:].rearrange("p (c d) -> p c d", d=HV),
              v_scr.ap[base_t:base_t + SBT, h * HV:(h + 1) * HV]
              .rearrange("(c p) d -> p c d", p=64))

    def sb3_chunk(s, h, b, sb2, c):
        tof = c * C
        gch = b * NCH + sb2 * NCB + c
        q0 = qtb[s].ap[:, tof:tof + C]
        q1 = qtb[s].ap[:, SBT + tof:SBT + tof + C]
        k0 = kib[s].ap[:, tof:tof + C]
        k1 = kib[s].ap[:, SBT + tof:SBT + tof + C]
        k.mm(psST[s], [(psST[s].ap[:], k0, q0), (psST[s].ap[:], k1, q1)],
             [qtb[s], kib[s]])
        k.op("dve", "select", [stb[s]], [psST[s], triu64, z64],
             stb[s].ap[:], triu64.ap[:], psST[s].ap[:], z64.ap[:],
             add_drain=True)
        k.mm(psO[s], [(psO[s].ap[:], q0, Sbf[s].ap[0:128, 0:HV]),
                      (psO[s].ap[:], q1, Sbf[s].ap[0:128, HV:2 * HV]),
                      (psO[s].ap[:], stb[s].ap[:],
                       vb[s].ap[0:64, c * HV:(c + 1) * HV])],
             [qtb[s], Sbf[s], stb[s], vb[s]])
        k.op("act", "activation", [sqs[s], ssq[s]], [psO[s]],
             sqs[s].ap[:], psO[s].ap[:], AF.Square, accum_out=ssq[s].ap[:])
        k.op("act", "activation", [rso[s]], [ssq[s], k.cpool],
             rso[s].ap[:], ssq[s].ap[:], AF.Ln, bias=eps_col[0:64, :],
             scale=1.0 / HV)
        k.op("act", "activation", [rso[s]], [rso[s]], rso[s].ap[:], rso[s].ap[:],
             AF.Exp, scale=-0.5)
        k.op("dve", "scalar_tensor_tensor", [onb[s]], [psO[s], rso[s], rmsw],
             onb[s].ap[:, c * HV:(c + 1) * HV], psO[s].ap[:], rso[s].ap[:],
             rmsw.ap[0:64, :], ALU.mult, ALU.mult)
        k.mm(psU[s], [(psU[s].ap[:, 0:HV], keb[s].ap[:, c * HK: c * HK + 128],
                       vb[s].ap[0:64, c * HV:(c + 1) * HV])], [keb[s], vb[s]])
        k.mm(psU[s], [(psU[s].ap[:, HV:2 * HV],
                       keb[s].ap[:, c * HK + 128: c * HK + 256],
                       vb[s].ap[0:64, c * HV:(c + 1) * HV])], [keb[s], vb[s]])
        for half in range(2):
            k.op("dve", "scalar_tensor_tensor", [Sm[s]], [Sm[s], Dsb, psU[s]],
                 Sm[s].ap[:, half * HV:(half + 1) * HV],
                 Sm[s].ap[:, half * HV:(half + 1) * HV],
                 Dsb.ap[:, (2 * h + half) * (B * NCH) + gch:
                        (2 * h + half) * (B * NCH) + gch + 1],
                 psU[s].ap[:, half * HV:(half + 1) * HV], ALU.mult, ALU.add)
        k.op("dve", "tensor_copy", [Sbf[s]], [Sm[s]], Sbf[s].ap[:], Sm[s].ap[:])

    def sb3_store(s, h, b, sb2):
        base_t = b * T + sb2 * SBT
        k.dma("sp", [o_scr], [onb[s]],
              o_scr.ap[base_t:base_t + SBT, h * HV:(h + 1) * HV]
              .rearrange("(c p) d -> p c d", p=64),
              onb[s].ap[:].rearrange("p (c d) -> p c d", d=HV))

    for b in range(B):
        for hp in range(HG // 2):
            h0, h1 = 2 * hp, 2 * hp + 1
            for s, h in ((0, h0), (1, h1)):
                k.op("dve", "memset", [Sm[s]], [], Sm[s].ap[:], 0.0)
                k.op("dve", "memset", [Sbf[s]], [], Sbf[s].ap[:], 0.0)
            for sb2 in range(NSB):
                sb3_load(0, h0, b, sb2)
                sb3_load(1, h1, b, sb2)
                for c in range(NCB):
                    sb3_chunk(0, h0, b, sb2, c)
                    sb3_chunk(1, h1, b, sb2, c)
                sb3_store(0, h0, b, sb2)
                sb3_store(1, h1, b, sb2)
    sb_.close()
    k.barrier()

    sc3 = ExitStack()
    wgs = k.sb(sc3, "p3c_wg", [128, NIC * DV], BF16)
    wos = k.sb(sc3, "p3c_wo", [128, (DV // 128) * D], BF16)
    k.dma("sp", [wgs], [g["wg_bf"]], wgs.ap[:], g["wg_bf"].ap[:])
    k.dma("sp", [wos], [g["wo_bf"]], wos.ap[:], g["wo_bf"].ap[:])
    xnfb = k.sb(sc3, "p3c_xnf", [128, NIC * 512], BF16)
    ob = k.sb(sc3, "p3c_ob", [128, 4 * DV], BF16)
    ogt = k.sb(sc3, "p3c_ogt", [128, DV], BF16)
    ogf = k.sb(sc3, "p3c_ogf", [128, (DV // 128) * 512], BF16)
    xres = k.sb(sc3, "p3c_xres", [128, 4 * D], F32)
    out3 = k.sb(sc3, "p3c_out", [128, D], F32)
    psG = k.psum(sc3, "p3c_psG", [128, 1024], F32)
    psY = k.psum(sc3, "p3c_psY", [128, 1024], F32)
    psTb2 = k.psum(sc3, "p3c_psTb", [128, 512], BF16)

    for b in range(B):
        for tb in range(NTB):
            gtb = b * NTB + tb
            t0 = b * T + tb * 512
            k.dma("sp", [xnfb], [xnf_scr], xnfb.ap[:], xnf_scr.ap[gtb])
            k.dma("sp", [ob], [o_scr],
                  ob.ap[:].rearrange("p (f d) -> p f d", d=DV),
                  o_scr.ap[t0:t0 + 512, :].rearrange("(f p) d -> p f d", p=128))
            k.dma("sp", [xres], [g["x2"]],
                  xres.ap[:].rearrange("p (f d) -> p f d", d=D),
                  g["x2"].ap[t0:t0 + 512, :].rearrange("(f p) d -> p f d", p=128))
            for j in range(4):
                for q4 in range(4):
                    half = q4 % 2
                    k.mm(psG, [(psG.ap[:, half * 512:(half + 1) * 512],
                                xnfb.ap[:, ic * 512 + j * 128: ic * 512 + (j + 1) * 128],
                                wgs.ap[:, ic * DV + q4 * 512: ic * DV + (q4 + 1) * 512])
                               for ic in range(NIC)], [wgs, xnfb])
                    if half == 1:
                        for hh in range(2):
                            k.op("act", "activation", [ogt], [psG],
                                 ogt.ap[:, (q4 - 1 + hh) * 512:(q4 + hh) * 512],
                                 psG.ap[:, hh * 512:(hh + 1) * 512], AF.Silu)
                k.op("dve", "tensor_tensor", [ogt], [ogt, ob],
                     ogt.ap[:], ogt.ap[:], ob.ap[:, j * DV:(j + 1) * DV], ALU.mult)
                for ic in range(DV // 128):
                    k.transpose(psTb2, psTb2.ap[0:128, 0:128], ogt,
                                ogt.ap[:, ic * 128:(ic + 1) * 128], ident)
                    k.op("dve", "tensor_copy", [ogf], [psTb2],
                         ogf.ap[:, ic * 512 + j * 128: ic * 512 + (j + 1) * 128],
                         psTb2.ap[0:128, 0:128])
            for j in range(4):
                for half in range(2):
                    k.mm(psY, [(psY.ap[:, half * 512:(half + 1) * 512],
                                ogf.ap[:, ic * 512 + j * 128: ic * 512 + (j + 1) * 128],
                                wos.ap[:, ic * D + half * 512: ic * D + (half + 1) * 512])
                               for ic in range(DV // 128)], [ogf, wos])
                k.op("dve", "tensor_tensor", [out3], [psY, xres],
                     out3.ap[:], psY.ap[:], xres.ap[:, j * D:(j + 1) * D], ALU.add)
                k.dma("sp", [g["x3"]], [out3],
                      g["x3"].ap[t0 + j * 128: t0 + (j + 1) * 128, :], out3.ap[:])
    sc3.close()
    sc.close()
    k.barrier()


def emit_quant(k, e, x_buf, x_ap, w, bufs):
    u, cc, G, sbc = bufs["u"], bufs["cc"], bufs["G"], bufs["sb"]
    sg, cb, gb = bufs["sg"], bufs["cb"], bufs["gb"]
    nb = w // 16
    c3 = cc.ap[:, 0:w].rearrange("p (n c) -> p n c", c=16)
    s3 = sbc.ap[:, 0:w // 2].rearrange("p (n c) -> p n c", c=8)
    k.op("act", "activation", [cc], [x_buf], cc.ap[:, 0:w], x_ap, AF.Abs)
    k.op("dve", "tensor_tensor", [sbc], [cc], s3[:, :, 0:8], c3[:, :, 0:8],
         c3[:, :, 8:16], ALU.max)
    k.op("dve", "tensor_tensor", [sbc], [sbc], s3[:, :, 0:4], s3[:, :, 0:4],
         s3[:, :, 4:8], ALU.max)
    k.op("dve", "tensor_tensor", [sbc], [sbc], s3[:, :, 0:2], s3[:, :, 0:2],
         s3[:, :, 2:4], ALU.max)
    k.op("dve", "tensor_tensor", [sbc], [sbc], s3[:, :, 0:1], s3[:, :, 0:1],
         s3[:, :, 1:2], ALU.max)
    k.op("dve", "tensor_scalar", [sbc], [sbc],
         sbc.ap[:, w // 2: w // 2 + nb], sbc.ap[:, 0:w // 2][:, 0::8],
         1.0 / 6.0, 1e-12, ALU.mult, ALU.max)
    k.op("dve", "reciprocal", [sbc], [sbc],
         sbc.ap[:, w // 2 + nb: w // 2 + 2 * nb],
         sbc.ap[:, w // 2: w // 2 + nb])
    k.op("dve", "tensor_copy", [gb], [sbc], gb.ap[:, 0:nb],
         sbc.ap[:, w // 2: w // 2 + nb])
    s_b = gb.ap[:, 0:nb].rearrange("p (n o) -> p n o", o=1).broadcast_to((128, nb, 16))
    rs_b = sbc.ap[:, w // 2 + nb: w // 2 + 2 * nb].rearrange(
        "p (n o) -> p n o", o=1).broadcast_to((128, nb, 16))
    k.op("dve", "tensor_tensor", [u], [cc, sbc],
         u.ap[:, 0:w].rearrange("p (n c) -> p n c", c=16), c3, rs_b, ALU.mult)
    uap = u.ap[:, 0:w]
    k.op("dve", "tensor_single_scalar", [G], [u], G.ap[:, 0:w], uap, 0.25, ALU.is_gt)
    k.op("dve", "tensor_single_scalar", [cb], [u], cb.ap[:, 0:w], uap, 0.75, ALU.is_gt)
    k.op("dve", "tensor_single_scalar", [sg], [u], sg.ap[:, 0:w], uap, 1.25, ALU.is_gt)
    k.op("dve", "tensor_tensor", [G], [G, cb], G.ap[:, 0:w], G.ap[:, 0:w],
         cb.ap[:, 0:w], ALU.add)
    k.op("dve", "tensor_single_scalar", [cb], [u], cb.ap[:, 0:w], uap, 1.75, ALU.is_gt)
    k.op("dve", "tensor_tensor", [sg], [sg, cb], sg.ap[:, 0:w], sg.ap[:, 0:w],
         cb.ap[:, 0:w], ALU.add)
    k.op("dve", "tensor_tensor", [G], [G, sg], G.ap[:, 0:w], G.ap[:, 0:w],
         sg.ap[:, 0:w], ALU.add)
    k.op("dve", "tensor_scalar", [G], [G], G.ap[:, 0:w], G.ap[:, 0:w], 0.5,
         None, ALU.mult)
    cbA, m25 = k.constcol(-2.5)
    cbB, m35 = k.constcol(-3.5)
    cbC, m50 = k.constcol(-5.0)
    k.op("act", "activation", [sg], [u, cbA], sg.ap[:, 0:w], uap, AF.Sign,
         bias=m25)
    k.op("act", "activation", [cb], [u, cbB], cb.ap[:, 0:w], uap, AF.Sign,
         bias=m35)
    k.op("dve", "tensor_tensor", [sg], [sg, cb], sg.ap[:, 0:w], sg.ap[:, 0:w],
         cb.ap[:, 0:w], ALU.add)
    k.op("act", "activation", [cb], [u, cbC], cb.ap[:, 0:w], uap, AF.Sign,
         bias=m50)
    k.op("dve", "scalar_tensor_tensor", [sg], [sg, cb], sg.ap[:, 0:w],
         sg.ap[:, 0:w], 0.5, cb.ap[:, 0:w], ALU.mult, ALU.add)
    k.op("dve", "tensor_tensor", [G], [G, sg], G.ap[:, 0:w], G.ap[:, 0:w],
         sg.ap[:, 0:w], ALU.add)
    k.op("dve", "tensor_scalar", [G], [G], G.ap[:, 0:w], G.ap[:, 0:w], 2.0,
         None, ALU.add)
    k.op("act", "activation", [cb], [x_buf], cb.ap[:, 0:w], x_ap, AF.Sign)
    k.op("dve", "tensor_tensor", [G], [G, cb], G.ap[:, 0:w], G.ap[:, 0:w],
         cb.ap[:, 0:w], ALU.mult)
    k.op("dve", "tensor_tensor", [bufs["out_buf"]], [G, gb],
         bufs["out_ap"].rearrange("p (n c) -> p n c", c=16),
         G.ap[:, 0:w].rearrange("p (n c) -> p n c", c=16), s_b, ALU.mult)


def build_phase4(k, g, cfg):
    nc = k.nc
    B, T, D = cfg["B"], cfg["T"], cfg["D"]
    I = 2816
    NIC = D // 128
    NHC = I // 128
    NTB = B * T // 512
    HW = I // 2
    sc = ExitStack()

    ident = k.sb(sc, "p4_ident", [128, 128], BF16)
    emit_identity(k, ident)

    wgate = k.sb(sc, "p4_wg", [128, NIC * I], BF16)
    wup = k.sb(sc, "p4_wu", [128, NIC * I], BF16)
    wdown = k.sb(sc, "p4_wd", [128, NHC * D], BF16)

    I4 = 2816
    u = k.sb(sc, "p4_u", [128, I4], F32)
    ccb = k.sb(sc, "p4_cc", [128, I4], F32)
    G = k.sb(sc, "p4_G", [128, I4], BF16)
    sbc = k.sb(sc, "p4_sb", [128, I4 // 2 + 2 * (I4 // 16)], F32)
    sgb = k.sb(sc, "p4_sg", [128, I4], BF16)
    cbb = k.sb(sc, "p4_cb", [128, I4], BF16)
    gbb = k.sb(sc, "p4_gb", [128, I4 // 16], BF16)
    qb = {"u": u, "cc": ccb, "G": G, "sb": sbc, "sg": sgb, "cb": cbb,
          "gb": gbb}

    sa = ExitStack()
    wrow = k.sb(sa, "p4a_wrow", [128, I], F32)
    wq_t = k.sb(sa, "p4a_wqt", [128, I], BF16)
    psTb = k.psum(sa, "p4a_psT", [128, 512], BF16)
    for wsrc, wdst in ((g["w_gate"], wgate), (g["w_up"], wup)):
        for r in range(NHC):
            k.dma("sp", [wrow], [wsrc], wrow.ap[:, 0:D],
                  wsrc.ap[r * 128:(r + 1) * 128, :])
            e = "dve" if r % 2 == 0 else "pool"
            emit_quant(k, e, wrow, wrow.ap[:, 0:D], D,
                       dict(qb, out_buf=wq_t, out_ap=wq_t.ap[:, 0:D]))
            for ic in range(NIC):
                k.transpose(psTb, psTb.ap[0:128, 0:128], wq_t,
                            wq_t.ap[:, ic * 128:(ic + 1) * 128], ident)
                k.op("dve", "tensor_copy", [wdst], [psTb],
                     wdst.ap[:, ic * I + r * 128: ic * I + (r + 1) * 128],
                     psTb.ap[0:128, 0:128])
    for r in range(NIC):
        k.dma("sp", [wrow], [g["w_down"]], wrow.ap[:],
              g["w_down"].ap[r * 128:(r + 1) * 128, :])
        for half in range(2):
            e = "dve" if half == 0 else "pool"
            emit_quant(k, e, wrow, wrow.ap[:, half * HW:(half + 1) * HW], HW,
                       dict(qb, out_buf=wq_t,
                            out_ap=wq_t.ap[:, half * HW:(half + 1) * HW]))
        for hc in range(NHC):
            k.transpose(psTb, psTb.ap[0:128, 0:128], wq_t,
                        wq_t.ap[:, hc * 128:(hc + 1) * 128], ident)
            k.op("dve", "tensor_copy", [wdown], [psTb],
                 wdown.ap[:, hc * D + r * 128: hc * D + (r + 1) * 128],
                 psTb.ap[0:128, 0:128])
    sa.close()
    k.barrier()

    sb_ = ExitStack()
    xin = k.sb(sb_, "p4_xin", [128, D], F32)
    xqf = k.sb(sb_, "p4_xqf", [128, NIC * 128], BF16)
    xqfv = [Buf(xqf.ap[:], "p4_xqf_a"), Buf(xqf.ap[:], "p4_xqf_b")]
    hh = k.sb(sb_, "p4_hh", [128, I], F32)
    hqf = k.sb(sb_, "p4_hqf", [128, 22 * 128], BF16)
    hqfv = [Buf(hqf.ap[:], "p4_hqf_a"), Buf(hqf.ap[:], "p4_hqf_b")]
    stats = k.sb(sb_, "p4_stats", [128, 12], F32)
    mv = k.sb(sb_, "p4_mv", [128, 2], F32)
    rstd = k.sb(sb_, "p4_rstd", [128, 1], F32)
    out4 = k.sb(sb_, "p4_out", [128, D], F32)
    psG = k.psum(sb_, "p4_psG", [128, 1408], F32)
    psY = k.psum(sb_, "p4_psY", [128, 1024], F32)
    psT2l = [k.psum(sb_, f"p4_psT2{i}", [128, 512], BF16) for i in range(2)]

    for tb in range(NTB):
        for j in range(4):
            ti = tb * 4 + j
            t0 = ti * 128
            k.dma("sp", [xin], [g["x3"]], xin.ap[:], g["x3"].ap[t0:t0 + 128, :])
            mu_ap, rstd_ap = emit_ln(k, mv, stats, rstd, xin, xin.ap[:], D)
            k.op("dve", "tensor_scalar", [hh], [xin, mv, rstd],
                 hh.ap[:, 0:D], xin.ap[:], mu_ap, rstd_ap, ALU.subtract, ALU.mult)
            emit_quant(k, "dve", hh, hh.ap[:, 0:D], D,
                       dict(qb, out_buf=sgb, out_ap=sgb.ap[:, 0:D]))
            for ic in range(NIC):
                pT = psT2l[ic % 2]
                k.transpose(pT, pT.ap[0:128, 0:128], sgb,
                            sgb.ap[:, ic * 128:(ic + 1) * 128], ident)
                k.op("dve", "tensor_copy", [xqfv[ic % 2]], [pT],
                     xqf.ap[:, ic * 128:(ic + 1) * 128],
                     pT.ap[0:128, 0:128])
            for half in range(2):
                cols = [(0, 512), (512, 1024), (1024, 1408)]
                for wsb, is_up in ((wgate, False), (wup, True)):
                    for (c0, c1) in cols:
                        k.mm(psG, [(psG.ap[:, c0:c1],
                                    xqf.ap[:, ic * 128:(ic + 1) * 128],
                                    wsb.ap[:, ic * I + half * HW + c0:
                                           ic * I + half * HW + c1])
                                   for ic in range(NIC)],
                             [wsb, xqfv[0], xqfv[1]])
                    if not is_up:
                        for (c0, c1) in cols:
                            k.op("act", "activation", [hh], [psG],
                                 hh.ap[:, half * HW + c0: half * HW + c1],
                                 psG.ap[:, c0:c1], AF.Silu)
                    else:
                        k.op("dve", "tensor_tensor", [hh], [psG, hh],
                             hh.ap[:, half * HW:(half + 1) * HW], psG.ap[:],
                             hh.ap[:, half * HW:(half + 1) * HW], ALU.mult)
            emit_quant(k, "dve", hh, hh.ap[:], I,
                       dict(qb, out_buf=sgb, out_ap=sgb.ap[:, 0:I]))
            for hc in range(22):
                pT = psT2l[hc % 2]
                k.transpose(pT, pT.ap[0:128, 0:128], sgb,
                            sgb.ap[:, hc * 128:(hc + 1) * 128], ident)
                k.op("dve", "tensor_copy", [hqfv[hc % 2]], [pT],
                     hqf.ap[:, hc * 128:(hc + 1) * 128],
                     pT.ap[0:128, 0:128])
            for colh in range(2):
                k.mm(psY, [(psY.ap[:, colh * 512:(colh + 1) * 512],
                            hqf.ap[:, hc2 * 128:(hc2 + 1) * 128],
                            wdown.ap[:, hc2 * D + colh * 512:
                                     hc2 * D + (colh + 1) * 512])
                           for hc2 in range(22)],
                     [hqfv[0], hqfv[1], wdown])
            k.dma("sp", [xin], [g["x3"]], xin.ap[:], g["x3"].ap[t0:t0 + 128, :])
            k.op("dve", "tensor_tensor", [out4], [psY, xin],
                 out4.ap[:], psY.ap[:], xin.ap[:], ALU.add)
            k.dma("sp", [g["y"]], [out4], g["y"].ap[t0:t0 + 128, :], out4.ap[:])
    sb_.close()
    sc.close()
    k.barrier()


def layout_l0_attn(w_kqv, w_proj):
    w = np.asarray(w_kqv).reshape(16, 64, 3, 1024)
    wk = w[:, :, 0, :].reshape(1024, 1024)
    wq = w[:, :, 1, :].reshape(1024, 1024)
    wv = w[:, :, 2, :].reshape(1024, 1024)
    wkqv_fm = np.concatenate([wk.T, wq.T, wv.T], axis=1)
    wproj_fm = np.ascontiguousarray(np.asarray(w_proj).T)
    return np.ascontiguousarray(wkqv_fm), wproj_fm


def sbready(w_fm):
    K, N = w_fm.shape
    return np.ascontiguousarray(
        np.asarray(w_fm).reshape(K // 128, 128, N).transpose(1, 0, 2)
        .reshape(128, (K // 128) * N))


CFG = {"B": 4, "T": 2048, "D": 1024}
FUSED = True
NCORES = 8

_P1_W = [("wkqv_fm", [128, 8 * 3072]), ("wproj_fm", [128, 8 * 1024]),
         ("b_proj", [1, 1024])]
_P2_W = [("Wr", [1024, 8]), ("br", [1, 8]), ("ln_g", [1, 1024]), ("ln_b", [1, 1024]),
         ("W1", [8 * 128, 8 * 4096]), ("b1", [8, 4096]), ("W2", [8 * 128, 32 * 1024]),
         ("b2", [8, 1024])]
_P3_W = [("Wq", [128, 8 * 1024]), ("Wk", [128, 8 * 1024]), ("Wv", [128, 8 * 2048]),
         ("Wg", [128, 8 * 2048]), ("gk_w1", [1024, 16]), ("gk_w2", [16, 1024]),
         ("gk_b", [1, 1024]), ("Wo", [128, 16 * 1024]), ("rms_w", [1, 512])]
_P4_W = [("w_gate", [2816, 1024]), ("w_up", [2816, 1024]), ("w_down", [1024, 2816])]


def _weights_from_inputs(inputs):
    wkqv_fm, wproj_fm = layout_l0_attn(inputs["L0_attn_w_kqv"], inputs["L0_attn_w_proj"])
    r = lambda a: np.ascontiguousarray(np.asarray(a, np.float32).reshape(1, -1))
    c = lambda a: np.ascontiguousarray(np.asarray(a, np.float32))
    W1 = np.asarray(inputs["L0_ffn_W1"], np.float32)
    W2 = np.asarray(inputs["L0_ffn_W2"], np.float32)
    w1r = np.ascontiguousarray(
        W1.reshape(8, 8, 128, 4096).transpose(0, 2, 1, 3).reshape(8 * 128, 8 * 4096))
    w2r = np.ascontiguousarray(
        W2.reshape(8, 32, 128, 1024).transpose(0, 2, 1, 3).reshape(8 * 128, 32 * 1024))
    return {
        "wkqv_fm": sbready(wkqv_fm.astype(np.float32)),
        "wproj_fm": sbready(wproj_fm.astype(np.float32)),
        "b_proj": r(inputs["L0_attn_b_proj"]),
        "Wr": c(inputs["L0_ffn_Wr"]), "br": r(inputs["L0_ffn_br"]),
        "ln_g": r(inputs["L0_ffn_ln_g"]), "ln_b": r(inputs["L0_ffn_ln_b"]),
        "W1": w1r, "b1": c(inputs["L0_ffn_b1"]),
        "W2": w2r, "b2": c(inputs["L0_ffn_b2"]),
        "Wq": sbready(c(inputs["L1_attn_Wq"])), "Wk": sbready(c(inputs["L1_attn_Wk"])),
        "Wv": sbready(c(inputs["L1_attn_Wv"])), "Wg": sbready(c(inputs["L1_attn_Wg"])),
        "gk_w1": c(inputs["L1_attn_gk_w1"]), "gk_w2": c(inputs["L1_attn_gk_w2"]),
        "gk_b": r(inputs["L1_attn_gk_b"]), "Wo": sbready(c(inputs["L1_attn_Wo"])),
        "rms_w": r(inputs["L1_attn_rms_w"]),
        "w_gate": c(inputs["L1_ffn_w_gate"]), "w_up": c(inputs["L1_ffn_w_up"]),
        "w_down": c(inputs["L1_ffn_w_down"]),
    }


def _run_phase(build_fn, wspecs, in_name, out_name, xs, wmap):
    NT = CFG["B"] * CFG["T"]
    nc = bass.Bass("TRN2", target_bir_lowering=False)
    k = K(nc)
    g = {in_name: k.dram(in_name, [NT, CFG["D"]], F32, kind="ExternalInput"),
         out_name: k.dram(out_name, [NT, CFG["D"]], F32, kind="ExternalOutput")}
    for n, s in wspecs:
        g[n] = k.dram(n, s, F32, kind="ExternalInput")
    build_fn(k, g, CFG)
    k.finish_program()
    k.es.close()
    nc.finalize()
    in_maps = []
    for c in range(NCORES):
        m = {in_name: xs[c]}
        for n, _ in wspecs:
            m[n] = wmap[n]
        in_maps.append(m)
    res = run_bass_kernel_spmd(nc, in_maps, core_ids=list(range(NCORES)))
    return [r[out_name] for r in res.results]


def _run_fused(xs, wmap):
    NT = CFG["B"] * CFG["T"]
    D = CFG["D"]
    nc = bass.Bass("TRN2", target_bir_lowering=False)
    k = K(nc)
    g = {"x0": k.dram("x0", [NT, D], F32, kind="ExternalInput"),
         "y": k.dram("y", [NT, D], F32, kind="ExternalOutput"),
         "x1": k.dram("x1", [NT, D], F32, kind="Internal"),
         "x2": k.dram("x2", [NT, D], F32, kind="Internal"),
         "x3": k.dram("x3", [NT, D], F32, kind="Internal")}
    for n, s in _P1_W + _P2_W + _P3_W + _P4_W:
        g[n] = k.dram(n, s, F32, kind="ExternalInput")
    build_phase0(k, g, CFG)
    build_phase1(k, g, CFG)
    build_phase2(k, g, CFG)
    build_phase3(k, g, CFG)
    build_phase4(k, g, CFG)
    k.finish_program()
    k.es.close()
    nc.finalize()
    in_maps = []
    for c in range(NCORES):
        m = {"x0": xs[c]}
        for n, _ in _P1_W + _P2_W + _P3_W + _P4_W:
            m[n] = wmap[n]
        in_maps.append(m)
    res = run_bass_kernel_spmd(nc, in_maps, core_ids=list(range(NCORES)))
    return [r["y"] for r in res.results]


def kernel(**inputs):
    x = np.asarray(inputs["L0_attn_x"], np.float32)
    D = CFG["D"]
    xs = [np.ascontiguousarray(x[4 * c:4 * c + 4].reshape(-1, D)) for c in range(NCORES)]
    wmap = _weights_from_inputs(inputs)
    ys = _run_fused(xs, wmap)
    out = np.concatenate([y.reshape(4, CFG["T"], D) for y in ys], axis=0)
    return out.astype(np.float32)
